# version 07
# baseline (speedup 1.0000x reference)
import contextlib
import math

import numpy as np
import concourse.bass as bass
import concourse.mybir as mybir
from concourse.bass_utils import run_bass_kernel_spmd
from concourse.bass_core_rust import IndirectOffsetOnAxis

F32 = mybir.dt.float32
BF16 = mybir.dt.bfloat16
I32 = mybir.dt.int32
AF = mybir.ActivationFunctionType
ALU = mybir.AluOpType
AX = mybir.AxisListType

P = 128
C = 1024
NCH = 8
H = 16
HID = 4096
NHID = 32
MAGIC = 0x12345678
LN1E4 = math.log(10000.0)

ENGINES = ("sync", "vector", "scalar", "tensor", "gpsimd")


class Prog:

    def __init__(self, nc):
        self.nc = nc
        self.sems = {}
        self.queue = {e: [] for e in ENGINES}
        self.count = {}
        self.last_writer = {}
        self.readers = {}
        self.clock = {e: {} for e in ENGINES}
        self.es = None
        self.arena_of = {}
        self.dup = set()
        self.curpar = [None]
        self.recording = False
        self.rec = []

    def _resolve(self, base):
        if base in self.dup:
            par = self.curpar[0]
            assert par is not None, f"parity tensor {base} used outside tile context"
            return f"{base}~{par}"
        return base

    def init_sems(self, es):
        self.es = es
        for e in ENGINES:
            self.sems[e] = es.enter_context(self.nc.semaphore(f"d_{e}"))
            self.count[e] = 0

    def chan(self, name):
        if name not in self.sems:
            self.sems[name] = self.es.enter_context(self.nc.semaphore(name))
            self.count[name] = 0
        return self.sems[name]

    def _expand(self, keys):
        out = []
        for k in keys:
            if isinstance(k, tuple):
                base = self._resolve(k[0])
                k = (base,) + tuple(k[1:])
            else:
                base = self._resolve(k)
                k = base
            out.append(k)
            ar = self.arena_of.get(base)
            if ar is None:
                continue
            if isinstance(ar, (list, tuple)):
                for a in ar:
                    out.append(("@", a))
            else:
                out.append(("@", ar))
        return out

    def emit(self, engine, fn, reads=(), writes=(), dma=None):
        if self.recording:
            self.rec.append((engine, fn, tuple(reads), tuple(writes), dma,
                             self.curpar[0]))
            return None
        reads = self._expand(reads)
        writes = self._expand(writes)
        need = {}

        def _need(s, v):
            if v > need.get(s, 0):
                need[s] = v

        for k in reads:
            w = self.last_writer.get(k)
            if w is not None:
                _need(*w)
        for k in writes:
            w = self.last_writer.get(k)
            if w is not None:
                _need(*w)
            for s, v in self.readers.get(k, {}).items():
                _need(s, v)

        myclock = self.clock[engine]
        waits = []
        for s, v in sorted(need.items(), key=lambda kv: str(kv[0])):
            if myclock.get(s, 0) >= v:
                continue
            waits.append((s, v))
            myclock[s] = v

        if dma is None:
            sem_name, inc = engine, 1
        else:
            self.chan(dma)
            sem_name, inc = dma, 16
        self.count[sem_name] += inc
        cnt = self.count[sem_name]
        sems = self.sems
        sem = sems[sem_name]

        par = self.curpar[0]

        def run(eng):
            self.curpar[0] = par
            for (s, v) in waits:
                eng.wait_ge(sems[s], v)
            last = fn(eng)
            assert last is not None, f"emit fn on {engine} returned None"
            last.then_inc(sem, inc)

        self.queue[engine].append(run)
        for k in reads:
            self.readers.setdefault(k, {})[sem_name] = cnt
        for k in writes:
            self.last_writer[k] = (sem_name, cnt)
            self.readers[k] = {}
        return (sem_name, cnt)

    def barrier_on(self, engine, deps):
        sems = self.sems
        myclock = self.clock[engine]
        waits = []
        for s, v in deps:
            if myclock.get(s, 0) >= v:
                continue
            waits.append((s, v))
            myclock[s] = v

        def run(eng):
            for (s, v) in waits:
                eng.wait_ge(sems[s], v)

        self.queue[engine].append(run)

    def record(self):
        self.recording = True
        self.rec = []

    def stop_record(self):
        self.recording = False
        out = self.rec
        self.rec = []
        return out

    def commit(self, item):
        engine, fn, reads, writes, dma, par = item
        sav = self.curpar[0]
        self.curpar[0] = par
        self.emit(engine, fn, reads=reads, writes=writes, dma=dma)
        self.curpar[0] = sav

    def seal(self, block):
        for e in ENGINES:
            q = self.queue[e]
            if not q:
                continue
            section = getattr(block, e)

            def make(q=q):
                def _(eng):
                    for run in q:
                        run(eng)
                return _

            section(make())


def splits(s, e_, mx=512):
    o = s
    while o < e_:
        nn = min(mx, e_ - o)
        yield (o, nn)
        o += nn


def build_program(NT=4096, TT=512, dbg=None):
    HL = 4
    W = TT + 2 * HL
    NG = TT // P
    NTILE = NT // TT
    SEQ = 2 * NT
    T2, T4, TH, W2 = TT + 2, TT + 4, TT // 2, TT + 2 * HL - 2
    LNW = ((T4 + 31) & ~31) + 16
    assert TT % P == 0 and NT % TT == 0

    nc = bass.Bass("TRN2", target_bir_lowering=False, detect_race_conditions=False)

    x_my = nc.dram_tensor("x_my", [NT + 2 * HL, C], F32, kind="ExternalInput").ap()
    out_my = nc.dram_tensor("out_my", [NT, C], F32, kind="ExternalOutput").ap()
    wn = {}
    for nm, shp in [
        ("cpe1_w", [C, 3]), ("cpe1_b", [C]), ("norm1_g", [C]), ("norm1_b", [C]),
        ("in_w", [C, C]), ("in_b", [C]), ("actp_w", [C, C]), ("actp_b", [C]),
        ("dwc_w", [C, 3]), ("dwc_b", [C]), ("qk_w", [C, 2 * C]), ("qk_b", [2 * C]),
        ("lepe_w", [C, 3]), ("lepe_b", [C]), ("out_w", [C, C]), ("out_b", [C]),
        ("cpe2_w", [C, 3]), ("cpe2_b", [C]), ("norm2_g", [C]), ("norm2_b", [C]),
        ("fc1_w", [C, HID]), ("fc1_b", [HID]), ("fc2_w", [HID, C]), ("fc2_b", [C]),
        ("fWin", [C, 64]), ("fbin", [64]), ("fW1", [64, 16]), ("fb1", [64, 16]),
        ("fW2", [64, 16]), ("fb2", [64]), ("fWout", [64, C]), ("fbout", [C]),
    ]:
        wn[nm] = nc.dram_tensor(nm, list(shp), F32, kind="ExternalInput").ap()
    idx_in = {}
    for nm, n in (("kv_idx", 65), ("a2a_idx", P), ("a2b_idx", P), ("fkv_idx", 2),
                  ("fa2_idx", 2), ("haloA_idx", P), ("haloB_idx", P)):
        idx_in[nm] = nc.dram_tensor(nm, [n, 1], I32, kind="ExternalInput").ap()
    lrmask_in = nc.dram_tensor("lrmask", [P, 2], F32, kind="ExternalInput").ap()

    a1sp = nc.dram_tensor("a1sp", [NTILE, P, NCH, TT], F32).ap()
    a2sp = nc.dram_tensor("a2sp", [NTILE, P, NCH, TT], F32).ap()
    y0sp = nc.dram_tensor("y0sp", [NT, C], F32).ap()
    qrsp = nc.dram_tensor("qrsp", [NTILE, P, NCH, TT], BF16).ap()
    arsp = nc.dram_tensor("arsp", [NTILE, P, NCH, TT], BF16).ap()
    lpsp = nc.dram_tensor("lpsp", [NTILE, P, NCH, TT], BF16).ap()
    hrow = nc.dram_tensor("hrow", [3 * P, NCH], F32).ap()
    fc1bf = nc.dram_tensor("fc1bf", [NHID, P, NCH, P], BF16).ap()
    fc2bf = nc.dram_tensor("fc2bf", [NHID, P, C], BF16).ap()
    qkbf = nc.dram_tensor("qkbf", [16, P, NCH, P], BF16).ap()
    kmb = nc.dram_tensor("kmb", [C], F32).ap()
    S_kv = nc.dram_tensor("S_kv", [130, C], F32, addr_space="Shared").ap()
    S_a2 = nc.dram_tensor("S_a2", [3 * P, NCH], F32, addr_space="Shared").ap()
    S_fl = nc.dram_tensor("S_fl", [8, 16], I32, addr_space="Shared").ap()

    es = contextlib.ExitStack()
    BUDGET = 207 * 1024
    arena = nc.alloc_sbuf_tensor("ARENA", [P, BUDGET // 4], F32)
    abase = nc.lookup_mloc(arena).addr
    cur = [0]
    tens = {}
    arena_of = {}
    groups = {}

    def _nbytes(shape, dtype):
        esz = {F32: 4, BF16: 2, I32: 4}[dtype]
        n = esz
        for s in shape[1:]:
            n *= s
        return (n + 63) & ~63

    def group(gname, nbytes):
        nbytes = (nbytes + 63) & ~63
        groups[gname] = (abase + cur[0], nbytes)
        cur[0] += nbytes
        assert cur[0] <= BUDGET, f"SBUF over budget at group {gname}: {cur[0]}"

    def alloc(name, shape, dtype, g=None, suboff=0):
        n = _nbytes(shape, dtype)
        if g is None:
            addr = abase + cur[0]
            cur[0] += n
            assert cur[0] <= BUDGET, f"SBUF over budget at {name}: {cur[0]}"
            arena_of[name] = name
        else:
            gaddr, gsize = groups[g]
            assert suboff + n <= gsize + 63, f"{name} exceeds group {g}"
            addr = gaddr + suboff
            arena_of[name] = g
        tens[name] = nc.alloc_sbuf_tensor_at(name, list(shape), dtype, offset=addr)
        return tens[name]

    DUPG = []

    def dgroup(gname, nbytes):
        DUPG.append(gname)
        group(gname + "~0", nbytes)
        group(gname + "~1", nbytes)

    dgroup("GA", max(NCH * W * 4, 16 * TT * 2))
    dgroup("GB", max(NCH * (TT + 2) * 4, 16 * TT * 2))
    dgroup("GC", NG * C * 4)
    dgroup("G1", NCH * W * 2)
    dgroup("G2", NCH * T4 * 2)
    dgroup("G3", NCH * T4 * 2)
    dgroup("G4", max(NCH * T4 * 2, 16 * TH * 2))
    dgroup("GLN", (3 * LNW + max(LNW, 512)) * 4)
    dgroup("GKV", 4 * TT * 4)
    dgroup("GSC", 2 * TT * 2)
    group("GR1", 8 * 1024)

    DUPN = set()

    def dalloc(name, shape, dtype, g=None, suboff=0):
        DUPN.add(name)
        for par in range(2):
            gg = (g + f"~{par}") if g is not None and g in DUPG else g
            alloc(f"{name}~{par}", shape, dtype, gg, suboff)

    dalloc("a1f", [P, NCH, W], F32)
    alloc("w_in", [P, NCH, C], BF16)
    alloc("w_actp", [P, NCH, C], BF16)
    alloc("ident_b", [P, P], BF16)
    alloc("ident_f", [P, P], F32)
    alloc("maskU", [P, P], BF16)
    alloc("maskL", [P, P], BF16)
    alloc("Rm", [P, NCH, P], BF16)
    alloc("ones_row", [1, 512], BF16)
    alloc("zrow", [1, P], BF16)
    alloc("ones_col", [P, 1], BF16)
    alloc("invC_col", [1, P], F32)
    alloc("taps", [P, 4, NCH, 4], F32)
    alloc("cst", [P, 40], F32)
    alloc("crope_col", [P, NCH], F32)
    alloc("halfpi", [P, 1], F32)
    alloc("eps5", [P, 1], F32)
    alloc("magic", [2, 16], I32)
    alloc("fbuf", [1, 64], I32)
    alloc("idx", [P, 8], I32)
    alloc("lrmask", [P, 2], F32)
    alloc("zero8", [P, 3, NCH], F32)
    alloc("g_cols", [P, 2, NCH], F32)
    alloc("b_cols", [P, 2, NCH], F32)
    alloc("b_colsb", [P, 2, NCH], BF16)
    alloc("rows2", [1, 2 * C], BF16)
    alloc("binrow", [1, 64], BF16)
    alloc("ffn_c", [P, 52], F32)
    alloc("w_Win", [P, NCH, 64], BF16)
    alloc("w_Wout", [P, C], BF16)
    dalloc("s2g2", [P, 2, TH], BF16)
    alloc("usq", [P, C], BF16)
    dalloc("stats", [P, 16], F32)
    dalloc("red1", [P, 1], F32)
    alloc("ksum", [P, NCH], F32)
    alloc("kv_raw", [65, C], F32)
    alloc("kv_cmb", [P, C], F32)
    alloc("kv_bd", [P, NCH, P], BF16)
    alloc("kmr_bd", [P, NCH, 16], BF16)
    alloc("km_col", [P, NCH], F32)
    alloc("km_bf", [P, NCH], BF16)
    alloc("a2keep", [P, 2, NCH], F32)
    alloc("a2rd", [P, 2, NCH], F32)
    alloc("a2eg", [P, NTILE, NCH, 2], F32)
    alloc("hpar", [P, 2, NCH], F32)
    dalloc("evs", [P, 2, 512], F32)
    dalloc("u_fm", [P, NCH, W], F32, "GA")
    dalloc("xi0", [P, NCH, T4], BF16, "GA")
    dalloc("qke", [P, 16, TT], BF16, "GA")
    dalloc("a1l", [P, NCH, TT], F32, "GA")
    dalloc("y0l", [P, NG, C], F32, "GA")
    dalloc("u_tm", [P, NG, C], F32, "GB")
    dalloc("y0t", [P, NG, C], F32, "GB")
    dalloc("krtvh", [P, 2, NG, C], BF16, "GB")
    dalloc("a2w", [P, NCH, TT], F32, "GB")
    dalloc("a2m", [P, NCH, TT], F32, "GB")
    dalloc("g1", [P, 16, TT], BF16, "GB")
    dalloc("x0", [P, NG, C], F32, "GC")
    dalloc("qro", [P, NCH, TT], BF16, "GC")
    dalloc("kro", [P, NCH, TT], BF16, "GC", suboff=NCH * TT * 2)
    dalloc("a3", [P, NCH, TT], F32, "GC")
    dalloc("u_bf", [P, NCH, W], BF16, "G1")
    dalloc("xed", [P, C], F32, "G1")
    dalloc("cscr", [P, NCH, W2], BF16, "G1")
    dalloc("lepe_sb", [P, NCH, TT], BF16, "G1")
    dalloc("qr2", [P, NCH, TT], BF16, "G1")
    dalloc("cvs", [P, NCH, TT], BF16, "G1")
    dalloc("a1bf", [P, NCH, T4], BF16, "G2")
    dalloc("xi", [P, NCH, T2], BF16, "G2")
    dalloc("ar2", [P, NCH, TT], BF16, "G2")
    dalloc("a3bf", [P, NCH, TT], BF16, "G2")
    dalloc("sqbf", [P, NCH, T4], BF16, "G3")
    dalloc("acr", [P, NCH, TT], BF16, "G3")
    dalloc("lp2", [P, NCH, TT], BF16, "G3")
    dalloc("hscr", [P, 16, TH], BF16, "G4")
    dalloc("xn1", [P, NCH, T4], BF16, "G4")
    dalloc("aab", [P, NCH, TT], BF16, "G4")
    dalloc("xn2", [P, NCH, TT], BF16, "G4")
    dalloc("mur", [P, LNW], F32, "GLN")
    dalloc("nmu2", [P, LNW], F32, "GLN", suboff=LNW * 4)
    dalloc("rstd", [P, LNW], F32, "GLN", suboff=2 * LNW * 4)
    dalloc("lscr", [P, max(LNW, 512)], F32, "GLN", suboff=3 * LNW * 4)
    dalloc("srow", [1, 2, LNW], F32, "GLN")
    dalloc("zsh", [16, TT], F32, "GKV")
    dalloc("zscr", [P, TT], F32, "GKV", suboff=TT * 4)
    dalloc("zbf", [16, TT], BF16, "GKV", suboff=2 * TT * 4)
    dalloc("zae", [P, TT], BF16, "GKV", suboff=2 * TT * 4 + TT * 2)
    dalloc("lab", [P, TT], BF16, "GKV", suboff=3 * TT * 4)
    dalloc("scab", [P, 2, TT], BF16, "GSC")
    alloc("r_qkb", [1, 2 * C], BF16, "GR1")
    alloc("r_inb", [1, C], BF16, "GR1", suboff=2 * C * 2)
    alloc("r_actb", [1, C], BF16, "GR1", suboff=3 * C * 2)
    alloc("r_fc1b", [1, HID], BF16, "GR1")
    alloc("selz", [16, NCH * P], BF16, "GR1", suboff=3 * C * 2)
    a1f_addr = nc.lookup_mloc(tens["a1f~0"]).addr
    tens["w_out"] = nc.alloc_sbuf_tensor_at("w_out", [P, NCH, C], BF16, offset=a1f_addr)
    arena_of["w_out"] = ["a1f~0", "a1f~1"]
    gc_addr = groups["GC~0"][0]
    tens["w_stage"] = nc.alloc_sbuf_tensor_at("w_stage", [P, 4, C], F32, offset=gc_addr)
    arena_of["w_stage"] = ["GC~0", "GC~1"]
    w_in_addr = nc.lookup_mloc(tens["w_in"]).addr
    tens["w1s"] = nc.alloc_sbuf_tensor_at("w1s", [P, 4, NCH, P], BF16, offset=w_in_addr)
    tens["w2s"] = nc.alloc_sbuf_tensor_at("w2s", [P, 4, C], BF16,
                                          offset=w_in_addr + 4 * NCH * P * 2)
    alloc("w3s", [P, 4, NCH, P], BF16)
    arena_of["w1s"] = "WP1"
    arena_of["w2s"] = "WP1"
    for nm2 in ("w_in", "w_actp"):
        arena_of[nm2] = "WP1"
    used_kb = cur[0] / 1024

    pw = [es.enter_context(nc.psum_tensor(f"pw{i}", [P, 512], F32)) for i in range(4)]
    pt4 = es.enter_context(nc.psum_tensor("pt4", [P, 1024], BF16))
    pt5 = es.enter_context(nc.psum_tensor("pt5", [P, 1024], BF16))
    pkv = es.enter_context(nc.psum_tensor("pkv", [P, 1024], F32))
    for i in range(4):
        arena_of[f"pw{i}"] = f"pw{i}"
    arena_of["pkv"] = "pkv"
    arena_of["pt4"] = "pt4"
    arena_of["pt5"] = "pt5"

    p = Prog(nc)
    p.arena_of = arena_of
    p.dup = DUPN
    p.init_sems(es)
    poll1 = es.enter_context(nc.semaphore("poll1"))
    poll2 = es.enter_context(nc.semaphore("poll2"))

    t = lambda nm: tens[p._resolve(nm)]
    p.curpar[0] = 0
    PB = lambda i: f"pw{i}"

    for nm, b in (("ident_b", 0), ("ident_f", 0), ("maskU", 1), ("maskL", -1)):
        tt = t(nm)
        p.emit("gpsimd", lambda e, tt=tt: e.memset(tt[:, :], 0.0), writes=[nm])
        p.emit("gpsimd", lambda e, tt=tt, b=b: e.affine_select(
            out=tt[:, :], in_=tt[:, :], compare_op=ALU.not_equal, fill=1.0,
            base=b, pattern=[[-1, P]], channel_multiplier=1),
            reads=[nm], writes=[nm])
    for nm, val in (("ones_row", 1.0), ("ones_col", 1.0), ("zrow", 0.0), ("invC_col", 1.0 / C),
                    ("halfpi", float(np.pi / 2)), ("zero8", 0.0), ("ksum", 0.0),
                    ("eps5", 1e-5)):
        tt = t(nm)
        p.emit("vector", lambda e, tt=tt, val=val: e.memset(tt[:], val), writes=[nm])
    p.emit("vector", lambda e: e.memset(t("magic")[:, :], MAGIC), writes=["magic"])
    p.emit("vector", lambda e: e.memset(t("fbuf")[:, :], 0), writes=["fbuf"])
    ix = t("idx")
    for col, (nm, n) in enumerate((("kv_idx", 65), ("a2a_idx", P), ("fkv_idx", 2),
                                   ("fa2_idx", 2), ("haloA_idx", P), ("haloB_idx", P),
                                   ("a2b_idx", P))):
        p.emit("sync", lambda e, col=col, nm=nm, n=n: e.dma_start(
            out=ix[0:n, col:col + 1], in_=idx_in[nm]), writes=[("idx", col)], dma=f"c_ix{col}")
    p.emit("sync", lambda e: e.dma_start(out=t("lrmask")[:, :], in_=lrmask_in),
           writes=["lrmask"], dma="c_lrm")

    cs = t("cst")
    p.emit("gpsimd", lambda e: e.memset(t("mur")[:, 0:64], 0.0), writes=[("mur", "pp")])
    p.emit("gpsimd", lambda e: e.affine_select(
        out=t("mur")[:, 0:64], in_=t("mur")[:, 0:64], compare_op=ALU.not_equal,
        fill=1.0, base=0, pattern=[[-2, 64]], channel_multiplier=1),
        reads=[("mur", "pp")], writes=[("mur", "pp")])
    p.emit("vector", lambda e: e.reduce_sum(out=cs[:, 17:18], in_=t("mur")[:, 0:64], axis=AX.X),
           reads=[("mur", "pp")], writes=[("cst", "even")])
    p.emit("vector", lambda e: e.tensor_scalar(out=cs[:, 16:17], in0=cs[:, 17:18],
                                               scalar1=-2.0, scalar2=1.0, op0=ALU.mult, op1=ALU.add),
           reads=[("cst", "even")], writes=[("cst", "sgn")])
    p.emit("gpsimd", lambda e: e.iota(cs[:, 18:19], pattern=[[0, 1]], base=0,
                                      channel_multiplier=1, allow_small_or_imprecise_dtypes=True),
           writes=[("cst", "p")])
    p.emit("vector", lambda e: e.tensor_scalar(out=cs[:, 19:20], in0=cs[:, 18:19],
                                               scalar1=0.5, scalar2=-0.25, op0=ALU.mult, op1=ALU.add),
           reads=[("cst", "p")], writes=[("cst", "fl0")])
    p.emit("vector", lambda e: e.scalar_tensor_tensor(out=cs[:, 20:21], in0=cs[:, 16:17],
                                                      scalar=-0.25, in1=cs[:, 19:20],
                                                      op0=ALU.mult, op1=ALU.add),
           reads=[("cst", "sgn"), ("cst", "fl0")], writes=[("cst", "fl")])
    p.emit("gpsimd", lambda e: e.iota(cs[:, 21:29], pattern=[[64, NCH]], base=0,
                                      channel_multiplier=0, allow_small_or_imprecise_dtypes=True),
           writes=[("cst", "jb")])
    p.emit("vector", lambda e: e.tensor_scalar(out=cs[:, 21:29], in0=cs[:, 21:29],
                                               scalar1=cs[:, 20:21], scalar2=None, op0=ALU.add),
           reads=[("cst", "jb"), ("cst", "fl")], writes=[("cst", "jm")])
    p.emit("scalar", lambda e: e.activation(out=cs[:, 29:37], in_=cs[:, 21:29],
                                            func=AF.Exp, scale=-LN1E4 / 512.0),
           reads=[("cst", "jm")], writes=[("cst", "th")])
    p.emit("scalar", lambda e: e.activation(out=cs[:, 0:8], in_=cs[:, 29:37],
                                            func=AF.Sin, bias=t("halfpi")[:, 0:1]),
           reads=[("cst", "th"), "halfpi"], writes=[("cst", "cos")])
    p.emit("scalar", lambda e: e.activation(out=cs[:, 8:16], in_=cs[:, 29:37], func=AF.Sin),
           reads=[("cst", "th")], writes=[("cst", "sin")])
    p.emit("vector", lambda e: e.scalar_tensor_tensor(out=t("crope_col")[:, :], in0=cs[:, 8:16],
                                                      scalar=cs[:, 16:17], in1=cs[:, 0:8],
                                                      op0=ALU.mult, op1=ALU.add),
           reads=[("cst", "cos"), ("cst", "sin"), ("cst", "sgn")], writes=["crope_col"])
    mr = t("mur")
    p.emit("vector", lambda e: e.tensor_scalar(out=mr[:, 64:72], in0=cs[:, 8:16],
                                               scalar1=cs[:, 16:17], scalar2=-0.5,
                                               op0=ALU.mult, op1=ALU.mult),
           reads=[("cst", "sin"), ("cst", "sgn")], writes=[("mur", "ss")])
    p.emit("vector", lambda e: e.scalar_tensor_tensor(out=mr[:, 72:80], in0=cs[:, 8:16],
                                                      scalar=0.5, in1=mr[:, 64:72],
                                                      op0=ALU.mult, op1=ALU.add),
           reads=[("cst", "sin"), ("mur", "ss")], writes=[("mur", "su")])
    p.emit("vector", lambda e: e.scalar_tensor_tensor(out=mr[:, 80:88], in0=cs[:, 8:16],
                                                      scalar=-0.5, in1=mr[:, 64:72],
                                                      op0=ALU.mult, op1=ALU.add),
           reads=[("cst", "sin"), ("mur", "ss")], writes=[("mur", "so")])
    Rm = t("Rm")
    for cc in range(NCH):
        p.emit("vector", lambda e, cc=cc: e.tensor_scalar(
            out=Rm[:, cc, :], in0=t("ident_b")[:, :], scalar1=cs[:, cc:cc + 1],
            scalar2=None, op0=ALU.mult),
            reads=["ident_b", ("cst", "cos")], writes=[("Rm", cc)])
        p.emit("vector", lambda e, cc=cc: e.scalar_tensor_tensor(
            out=Rm[:, cc, :], in0=t("maskU")[:, :], scalar=mr[:, 72 + cc:73 + cc],
            in1=Rm[:, cc, :], op0=ALU.mult, op1=ALU.add),
            reads=["maskU", ("mur", "su"), ("Rm", cc)], writes=[("Rm", cc)])
        p.emit("vector", lambda e, cc=cc: e.scalar_tensor_tensor(
            out=Rm[:, cc, :], in0=t("maskL")[:, :], scalar=mr[:, 80 + cc:81 + cc],
            in1=Rm[:, cc, :], op0=ALU.mult, op1=ALU.add),
            reads=["maskL", ("mur", "so"), ("Rm", cc)], writes=[("Rm", cc)])
    taps = t("taps")
    for i, (wnm, bnm, plus1) in enumerate((("cpe1_w", "cpe1_b", True),
                                           ("dwc_w", "dwc_b", False),
                                           ("lepe_w", "lepe_b", False),
                                           ("cpe2_w", "cpe2_b", True))):
        p.emit("sync", lambda e, i=i, wnm=wnm: e.dma_start(
            out=taps[:, i, :, 0:3], in_=wn[wnm].rearrange("(cc p) k -> p cc k", p=P)),
            writes=[("taps", i, "w")], dma="c_tp")
        p.emit("sync", lambda e, i=i, bnm=bnm: e.dma_start(
            out=taps[:, i, :, 3:4], in_=wn[bnm].rearrange("(cc p) -> p cc", p=P)[:, :, None],
            allow_slow_non_contiguous=True),
            writes=[("taps", i, "b")], dma="c_tp2")
        if plus1:
            p.emit("vector", lambda e, i=i: e.tensor_scalar(
                out=taps[:, i, :, 1:2], in0=taps[:, i, :, 1:2], scalar1=1.0,
                scalar2=None, op0=ALU.add),
                reads=[("taps", i, "w")], writes=[("taps", i, "w")])
    for col, nm in ((0, "norm1_g"), (1, "norm2_g")):
        p.emit("sync", lambda e, col=col, nm=nm: e.dma_start(
            out=t("g_cols")[:, col, :], in_=wn[nm].rearrange("(cc p) -> p cc", p=P),
            allow_slow_non_contiguous=True),
            writes=[("g_cols", col)], dma="c_gc")
    for col, nm in ((0, "norm1_b"), (1, "norm2_b")):
        p.emit("sync", lambda e, col=col, nm=nm: e.dma_start(
            out=t("b_cols")[:, col, :], in_=wn[nm].rearrange("(cc p) -> p cc", p=P),
            allow_slow_non_contiguous=True),
            writes=[("b_cols", col)], dma="c_bc")
    p.emit("vector", lambda e: e.tensor_copy(out=t("b_colsb")[:, :, :], in_=t("b_cols")[:, :, :]),
           reads=[("b_cols", 0), ("b_cols", 1)], writes=["b_colsb"])

    fcn = t("ffn_c")
    for sl0, nm in ((0, "fW1"), (32, "fW2")):
        for hh in range(2):
            p.emit("sync", lambda e, sl0=sl0, nm=nm, hh=hh: e.dma_start(
                out=fcn[hh * 64:(hh + 1) * 64, sl0:sl0 + 16], in_=wn[nm]),
                writes=[("ffn_c", nm, hh)], dma="c_fc")
    for hh in range(2):
        p.emit("sync", lambda e, hh=hh: e.dma_start(
            out=fcn[hh * 64:(hh + 1) * 64, 16:32], in_=wn["fb1"]),
            writes=[("ffn_c", "fb1", hh)], dma="c_fc2")
    stg = t("w_stage").rearrange("p a b -> p (a b)")
    p.emit("sync", lambda e: e.dma_start(
        out=stg[:, 0:512].rearrange("p (cc m) -> p cc m", m=64),
        in_=wn["fWin"].rearrange("(cc p) m -> p cc m", p=P)),
        writes=[("w_stage", "win")], dma="c_ws")
    p.emit("vector", lambda e: e.tensor_copy(
        out=t("w_Win").rearrange("p cc m -> p (cc m)"), in_=stg[:, 0:512]),
        reads=[("w_stage", "win")], writes=["w_Win"])
    p.emit("sync", lambda e: e.dma_start(out=stg[0:64, C:2 * C], in_=wn["fWout"]),
           writes=[("w_stage", "wout")], dma="c_ws")
    p.emit("vector", lambda e: e.tensor_copy(out=t("w_Wout")[0:64, :], in_=stg[0:64, C:2 * C]),
           reads=[("w_stage", "wout")], writes=["w_Wout"])
    p.emit("gpsimd", lambda e: e.dma_start(out=t("w_Wout")[64:128, :], in_=t("w_Wout")[0:64, :]),
           reads=["w_Wout"], writes=[("w_Wout", "hi")], dma="c_wodup")
    p.emit("sync", lambda e: e.dma_start(out=stg[0:64, 2 * C:2 * C + 1], in_=wn["fb2"][:, None]),
           writes=[("w_stage", "fb2")], dma="c_ws")
    p.emit("vector", lambda e: e.tensor_copy(out=t("scab")[0:64, 0, 0:1], in_=stg[0:64, 2 * C:2 * C + 1]),
           reads=[("w_stage", "fb2")], writes=[("scab", "fb2")])
    def mm_bout(e):
        e.matmul(pw[0][0:1, 0:512], t("scab")[0:64, 0, 0:1], t("w_Wout")[0:64, 0:512],
                 start=True, stop=True)
        return e.matmul(pw[1][0:1, 0:512], t("scab")[0:64, 0, 0:1], t("w_Wout")[0:64, 512:1024],
                        start=True, stop=True)
    p.emit("tensor", mm_bout, reads=[("scab", "fb2"), "w_Wout"], writes=[(PB(0),), (PB(1),)])
    p.emit("sync", lambda e: e.dma_start(out=stg[0:1, 3 * C:4 * C], in_=wn["fbout"][None, :]),
           writes=[("w_stage", "bout0")], dma="c_ws")
    for half in range(2):
        p.emit("vector", lambda e, half=half: e.tensor_tensor(
            out=t("lscr")[0:1, 0:512], in0=pw[half][0:1, 0:512],
            in1=stg[0:1, 3 * C + half * 512:3 * C + (half + 1) * 512], op=ALU.add),
            reads=[(PB(half),), ("w_stage", "bout0")], writes=[("lscr", "bo")])
        p.emit("vector", lambda e, half=half: e.tensor_copy(
            out=t("rows2")[0:1, half * 512:(half + 1) * 512],
            in_=t("lscr")[0:1, 0:512]),
            reads=[("lscr", "bo")], writes=[("rows2", "bout")])
    p.emit("sync", lambda e: e.dma_start(out=stg[0:1, 2 * C:2 * C + 64], in_=wn["fbin"][None, :]),
           writes=[("w_stage", "bin")], dma="c_ws")
    p.emit("vector", lambda e: e.tensor_copy(out=t("binrow")[0:1, :],
                                             in_=stg[0:1, 2 * C:2 * C + 64]),
           reads=[("w_stage", "bin")], writes=["binrow"])

    def load_w1(wname, dst, ncols, gfold, bcol, b0name, rowdst, rowkey):
        qkeflat = t("qke").rearrange("p a b -> p (a b)")
        for cc in range(NCH):
            p.emit("sync", lambda e, cc=cc: e.dma_start(
                out=stg[:, 0:ncols], in_=wn[wname][cc * P:(cc + 1) * P, :]),
                writes=[("w_stage", wname, cc)], dma=f"c_wl{cc % 4}")
            if gfold is not None:
                p.emit("vector", lambda e, cc=cc: e.tensor_scalar(
                    out=dst[:, cc, :], in0=stg[:, 0:ncols],
                    scalar1=t("g_cols")[:, gfold, cc:cc + 1], scalar2=None, op0=ALU.mult),
                    reads=[("w_stage", wname, cc), ("g_cols", gfold)], writes=[(wname, cc)])
            else:
                p.emit("vector", lambda e, cc=cc: e.tensor_copy(out=dst[:, cc, :], in_=stg[:, 0:ncols]),
                       reads=[("w_stage", wname, cc)], writes=[(wname, cc)])
            if bcol is not None:
                p.emit("vector", lambda e, cc=cc: e.tensor_copy(
                    out=qkeflat[:, 0:ncols], in_=stg[:, 0:ncols]),
                    reads=[("w_stage", wname, cc)], writes=[("qke", "stagebf")])
                def mmb(e, cc=cc):
                    last = None
                    for oi, (o, nn) in enumerate(splits(0, ncols)):
                        bank = pw[2] if oi == 0 else pw[3]
                        last = e.matmul(bank[0:1, 0:nn], t("b_colsb")[:, bcol, cc:cc + 1],
                                        qkeflat[:, o:o + nn],
                                        start=(cc == 0), stop=(cc == NCH - 1))
                    return last
                p.emit("tensor", mmb, reads=[("qke", "stagebf"), "b_colsb"],
                       writes=[(PB(2),), (PB(3),)])
        if bcol is not None:
            p.emit("sync", lambda e: e.dma_start(out=stg[0:1, 3 * C:3 * C + ncols],
                                                 in_=wn[b0name][None, :]),
                   writes=[("w_stage", "b0w")], dma="c_wl2")
            for oi, (o, nn) in enumerate(splits(0, ncols)):
                bank = pw[2] if oi == 0 else pw[3]
                p.emit("vector", lambda e, o=o, nn=nn, bank=bank: e.tensor_tensor(
                    out=t("lscr")[0:1, 0:nn], in0=bank[0:1, 0:nn],
                    in1=stg[0:1, 3 * C + o:3 * C + o + nn], op=ALU.add),
                    reads=[(PB(2),), (PB(3),), ("w_stage", "b0w")], writes=[("lscr", "bw")])
                p.emit("vector", lambda e, o=o, nn=nn: e.tensor_copy(
                    out=rowdst[0:1, o:o + nn], in_=t("lscr")[0:1, 0:nn]),
                    reads=[("lscr", "bw")], writes=[rowkey])

    load_w1("in_w", t("w_in"), C, 0, 0, "in_b", t("r_inb"), ("r_inb",))
    load_w1("actp_w", t("w_actp"), C, 0, 0, "actp_b", t("r_actb"), ("r_actb",))
    qkef = t("qke").rearrange("p a b -> p (a b)")
    for cc in range(NCH):
        p.emit("sync", lambda e, cc=cc: e.dma_start(
            out=stg[:, 0:2 * C], in_=wn["qk_w"][cc * P:(cc + 1) * P, :]),
            writes=[("w_stage", "qk", cc)], dma=f"c_wl{cc % 4}")
        p.emit("vector", lambda e, cc=cc: e.tensor_copy(out=qkef[:, 0:2 * C], in_=stg[:, 0:2 * C]),
               reads=[("w_stage", "qk", cc)], writes=[("qke", "stagebf")])
        p.emit("sync", lambda e, cc=cc: e.dma_start(
            out=qkbf[:, :, cc, :].rearrange("j p m -> p j m"),
            in_=qkef[:, 0:2 * C].rearrange("p (j m) -> p j m", m=P)),
            reads=[("qke", "stagebf")], writes=[("qkbf", cc)], dma="c_wlo")
    p.emit("sync", lambda e: e.dma_start(out=stg[0:1, 2 * C:4 * C], in_=wn["qk_b"][None, :]),
           writes=[("w_stage", "qkb")], dma="c_wl2")
    p.emit("vector", lambda e: e.tensor_copy(out=t("r_qkb")[0:1, :], in_=stg[0:1, 2 * C:4 * C]),
           reads=[("w_stage", "qkb")], writes=[("r_qkb",)])

    def ln_fm(a_bf, abf_keys, sqk, nco, a_f, acol, af_keys, xnt, xn_name, b0):
        assert nco <= 512
        b1 = b0 + 1
        def mmsum(e):
            last = None
            for cc in range(NCH):
                last = e.matmul(pw[b1][0:1, 0:nco], t("ones_col")[:, :], a_bf[:, cc, 0:nco],
                                start=(cc == 0), stop=(cc == NCH - 1))
            return last
        p.emit("tensor", mmsum, reads=abf_keys + ["ones_col"], writes=[(PB(b1),)])
        def mmsq(e):
            last = None
            for cc in range(NCH):
                last = e.matmul(pw[b1][32:33, 0:nco], t("ones_col")[:, :],
                                t("sqbf")[:, cc, 0:nco],
                                start=(cc == 0), stop=(cc == NCH - 1))
            return last
        p.emit("tensor", mmsq, reads=[sqk, "ones_col"], writes=[(PB(b1), "q")])
        p.emit("scalar", lambda e: e.activation(
            out=t("srow")[0:1, 0, 0:nco], in_=pw[b1][0:1, 0:nco], func=AF.Identity),
            reads=[(PB(b1),), (PB(b1), "q")], writes=[("srow", 0)])
        p.emit("scalar", lambda e: e.activation(
            out=t("srow")[0:1, 1, 0:nco], in_=pw[b1][32:33, 0:nco], func=AF.Identity),
            reads=[(PB(b1), "q")], writes=[("srow", 1)])
        def mmrep(e):
            e.matmul(pw[b0][:, 0:nco], t("invC_col")[0:1, :],
                     t("srow")[0:1, 0, 0:nco], start=True, stop=True)
            return e.matmul(pw[b1][:, 0:nco], t("invC_col")[0:1, :],
                            t("srow")[0:1, 1, 0:nco], start=True, stop=True)
        p.emit("tensor", mmrep, reads=[("srow", 0), ("srow", 1), "invC_col"],
               writes=[(PB(b0),), (PB(b1), "E")])
        p.emit("scalar", lambda e: e.activation(
            out=t("mur")[:, 0:nco], in_=pw[b0][:, 0:nco], func=AF.Identity),
            reads=[(PB(b0),)], writes=[("mur", "v")])
        p.emit("gpsimd", lambda e: e.tensor_tensor(
            out=t("nmu2")[:, 0:nco], in0=t("mur")[:, 0:nco],
            in1=t("mur")[:, 0:nco], op=ALU.mult),
            reads=[("mur", "v")], writes=["nmu2"])
        p.emit("scalar", lambda e: e.activation(
            out=t("rstd")[:, 0:nco], in_=pw[b1][:, 0:nco], func=AF.Identity,
            bias=t("eps5")[:, 0:1]),
            reads=[(PB(b1), "E"), "eps5"], writes=[("rstd", "E")])
        p.emit("gpsimd", lambda e: e.tensor_tensor(
            out=t("rstd")[:, 0:nco], in0=t("rstd")[:, 0:nco],
            in1=t("nmu2")[:, 0:nco], op=ALU.subtract),
            reads=[("rstd", "E"), "nmu2"], writes=[("rstd", "v")])
        p.emit("vector", lambda e: e.reciprocal(out=t("rstd")[:, 0:nco], in_=t("rstd")[:, 0:nco]),
               reads=[("rstd", "v")], writes=[("rstd", "v")])
        p.emit("scalar", lambda e: e.activation(out=t("rstd")[:, 0:nco], in_=t("rstd")[:, 0:nco],
                                                func=AF.Sqrt),
               reads=[("rstd", "v")], writes=[("rstd", "v")])
        for cc in range(NCH):
            p.emit("vector", lambda e, cc=cc: e.tensor_tensor(
                out=t("lscr")[:, 0:nco], in0=a_f[:, cc, acol:acol + nco],
                in1=t("mur")[:, 0:nco], op=ALU.subtract),
                reads=af_keys + [("mur", "v")], writes=[("lscr", "x")])
            p.emit("vector", lambda e, cc=cc: e.tensor_tensor(
                out=xnt[:, cc, 0:nco], in0=t("lscr")[:, 0:nco],
                in1=t("rstd")[:, 0:nco], op=ALU.mult),
                reads=[("lscr", "x"), ("rstd", "v")], writes=[(xn_name, cc)])

    def pkv_open(e):
        e.matmul(pkv[:, 0:512], t("zrow")[0:1, :], t("ones_row")[0:1, 0:512],
                 start=True, stop=False)
        return e.matmul(pkv[:, 512:1024], t("zrow")[0:1, :], t("ones_row")[0:1, 0:512],
                        start=True, stop=False)
    p.emit("tensor", pkv_open, reads=["zrow", "ones_row"], writes=[("pkv",)])

    rot2 = {0: 0, 1: 0}

    def nbank(par):
        rot2[par] = 1 - rot2[par]
        return 2 * par + rot2[par]

    def pass1_head(ti):
        p.curpar[0] = ti % 2
        par = ti % 2
        PB0, PB1 = 2 * par, 2 * par + 1
        t0 = ti * TT
        x0, u_tm, u_fm = t("x0"), t("u_tm"), t("u_fm")
        xed = t("xed")
        st = t("stats")
        p.emit("sync", lambda e: e.dma_start(
            out=x0[:, :, :], in_=x_my[t0 + HL:t0 + HL + TT, :].rearrange("(g p) c -> p g c", p=P)),
            writes=["x0"], dma=f"c_x0{ti % 2}")
        p.emit("sync", lambda e: e.dma_start(
            out=xed[0:4, :], in_=x_my[t0:t0 + HL, :]),
            writes=[("xed", "el")], dma=f"c_xe{ti % 2}")
        p.emit("sync", lambda e: e.dma_start(
            out=xed[4:8, :], in_=x_my[t0 + HL + TT:t0 + 2 * HL + TT, :]),
            writes=[("xed", "er")], dma=f"c_xe{ti % 2}b")
        edk = [("xed", "el"), ("xed", "er")]

        for g in range(NG):
            p.emit("vector", lambda e, g=g: e.reduce_sum(out=st[:, g:g + 1], in_=x0[:, g, :], axis=AX.X),
                   reads=["x0"], writes=[("stats", "s", g)])
            p.emit("scalar", lambda e, g=g: e.activation(out=t("usq")[:, :], in_=x0[:, g, :],
                                                         func=AF.Square, accum_out=st[:, 4 + g:5 + g]),
                   reads=["x0"], writes=[("stats", "q", g), "usq"])
        p.emit("vector", lambda e: e.reduce_sum(out=t("red1")[0:8, 0:1], in_=xed[0:8, :], axis=AX.X),
               reads=edk, writes=[("red1", "e")])
        p.emit("scalar", lambda e: e.activation(out=t("usq")[0:8, :], in_=xed[0:8, :],
                                                func=AF.Square, accum_out=st[0:8, 8:9]),
               reads=edk, writes=[("stats", "qe"), "usq"])

        def fin(sl_s, sl_q, sl_mu, sl_nb, keys, tag):
            p.emit("vector", lambda e: e.tensor_scalar(out=sl_mu, in0=sl_s, scalar1=1.0 / C,
                                                       scalar2=None, op0=ALU.mult),
                   reads=keys, writes=[("stats", tag, "mu")])
            p.emit("vector", lambda e: e.tensor_tensor(out=sl_nb, in0=sl_mu, in1=sl_mu, op=ALU.mult),
                   reads=[("stats", tag, "mu")], writes=[("stats", tag, "m2")])
            p.emit("vector", lambda e: e.scalar_tensor_tensor(out=sl_q, in0=sl_q, scalar=1.0 / C,
                                                              in1=sl_nb, op0=ALU.mult, op1=ALU.subtract),
                   reads=keys + [("stats", tag, "m2")], writes=[("stats", tag, "v")])
            p.emit("vector", lambda e: e.tensor_scalar(out=sl_q, in0=sl_q, scalar1=1e-5,
                                                       scalar2=None, op0=ALU.add),
                   reads=[("stats", tag, "v")], writes=[("stats", tag, "v")])
            p.emit("vector", lambda e: e.reciprocal(out=sl_q, in_=sl_q),
                   reads=[("stats", tag, "v")], writes=[("stats", tag, "v")])
            p.emit("scalar", lambda e: e.activation(out=sl_q, in_=sl_q, func=AF.Sqrt),
                   reads=[("stats", tag, "v")], writes=[("stats", tag, "r")])
            p.emit("vector", lambda e: e.scalar_tensor_tensor(out=sl_nb, in0=sl_mu, scalar=-1.0,
                                                              in1=sl_q, op0=ALU.mult, op1=ALU.mult),
                   reads=[("stats", tag, "r"), ("stats", tag, "mu")], writes=[("stats", tag, "nb")])

        fin(st[:, 0:NG], st[:, 4:4 + NG], st[:, 10:10 + NG], t("lscr")[:, 0:NG],
            [("stats", "s", g) for g in range(NG)] + [("stats", "q", g) for g in range(NG)], "g")
        fin(t("red1")[0:8, 0:1], st[0:8, 8:9], st[0:8, 9:10], t("lscr")[0:8, 4:5],
            [("red1", "e"), ("stats", "qe")], "e")

        for g in range(NG):
            p.emit("scalar", lambda e, g=g: e.activation(
                out=u_tm[:, g, :], in_=x0[:, g, :], func=AF.Identity,
                scale=st[:, 4 + g:5 + g], bias=t("lscr")[:, g:g + 1]),
                reads=["x0", ("stats", "g", "r"), ("stats", "g", "nb")], writes=[("u_tm", g)])
        p.emit("scalar", lambda e: e.activation(
            out=xed[32:40, :], in_=xed[0:8, :], func=AF.Identity,
            scale=st[0:8, 8:9], bias=t("lscr")[0:8, 4:5]),
            reads=edk + [("stats", "e", "r"), ("stats", "e", "nb")], writes=[("xed", "ue")])

        for g in range(NG):
            def trg(e, g=g):
                last = None
                for cc in range(NCH):
                    bank = pw[PB0 + cc // 4]
                    last = e.transpose(bank[:, (cc % 4) * P:(cc % 4 + 1) * P],
                                       u_tm[:, g, cc * P:(cc + 1) * P], t("ident_f")[:, :])
                return last
            p.emit("tensor", trg, reads=[("u_tm", g), "ident_f"],
                   writes=[(PB(PB0),), (PB(PB1),)])
            for half in range(2):
                p.emit("vector", lambda e, g=g, half=half: e.tensor_copy(
                    out=u_fm[:, half * 4:(half + 1) * 4, HL + g * P:HL + (g + 1) * P],
                    in_=pw[PB0 + half][:, :].rearrange("p (cc n) -> p cc n", n=P)),
                    reads=[(PB(PB0 + half),)], writes=[("u_fm", g, half)])
        def tre(e):
            last = None
            for cc in range(NCH):
                last = e.transpose(pw[PB0][:, cc * 8:(cc + 1) * 8], xed[32:40, cc * P:(cc + 1) * P],
                                   t("ident_f")[32:40, 32:40])
            return last
        p.emit("tensor", tre, reads=[("xed", "ue"), "ident_f"], writes=[(PB(PB0),)])
        def eedge(e):
            last = None
            for cc in range(NCH):
                last = e.tensor_copy(out=u_fm[:, cc, 0:4], in_=pw[PB0][:, cc * 8:cc * 8 + 4])
                last = e.tensor_copy(out=u_fm[:, cc, W - 4:W], in_=pw[PB0][:, cc * 8 + 4:cc * 8 + 8])
            return last
        p.emit("vector", eedge, reads=[(PB(PB0),)], writes=[("u_fm", "edge")])
        ufmk = [("u_fm", g, h) for g in range(NG) for h in range(2)] + [("u_fm", "edge")]

        p.emit("gpsimd", lambda e: e.tensor_copy(
            out=t("u_bf")[:, :, HL:HL + TT],
            in_=u_fm[:, :, HL:HL + TT]),
            reads=ufmk, writes=["u_bf"])

        def mm_win(e):
            last = None
            for hh in range(2):
                for cc in range(NCH):
                    last = e.matmul(pw[PB1][hh * 64:(hh + 1) * 64, 0:TH], t("w_Win")[:, cc, :],
                                    t("u_bf")[:, cc, HL + hh * TH:HL + (hh + 1) * TH],
                                    start=(cc == 0), stop=False)
                last = e.matmul(pw[PB1][hh * 64:(hh + 1) * 64, 0:TH],
                                t("binrow")[0:1, :],
                                t("ones_row")[0:1, 0:TH], start=False, stop=True)
            return last
        p.emit("tensor", mm_win, reads=["u_bf", "w_Win", "binrow", "ones_row"],
               writes=[(PB(PB1),)])
        p.emit("vector", lambda e: e.tensor_copy(out=t("s2g2")[:, 0, :], in_=pw[PB1][:, 0:TH]),
               reads=[(PB(PB1),)], writes=[("s2g2", 0)])
        def inner1(e):
            last = None
            for i in range(16):
                last = e.activation(out=t("hscr")[:, i, :], in_=t("s2g2")[:, 0, :],
                                    func=AF.Identity, scale=fcn[:, i:i + 1],
                                    bias=fcn[:, 16 + i:17 + i])
            return last
        p.emit("scalar", inner1,
               reads=[("s2g2", 0), ("ffn_c", "fW1", 0), ("ffn_c", "fW1", 1),
                      ("ffn_c", "fb1", 0), ("ffn_c", "fb1", 1)],
               writes=[("hscr", 0)])
        def inner2(e):
            last = None
            for i in range(16):
                last = e.tensor_scalar(out=t("hscr")[:, i, :], in0=t("hscr")[:, i, :],
                                       scalar1=0.0, scalar2=fcn[:, 32 + i:33 + i],
                                       op0=ALU.max, op1=ALU.mult)
            return last
        p.emit("vector", inner2, reads=[("hscr", 0), ("ffn_c", "fW2", 0), ("ffn_c", "fW2", 1)],
               writes=[("hscr", 1)])
        hs = t("hscr").rearrange("p i n -> p (i n)")
        for stepi, width in enumerate((8 * TH, 4 * TH, 2 * TH, TH)):
            p.emit("vector", lambda e, width=width: e.tensor_tensor(
                out=hs[:, 0:width], in0=hs[:, 0:width], in1=hs[:, width:2 * width], op=ALU.add),
                reads=[("hscr", stepi + 1)], writes=[("hscr", stepi + 2)])
        p.emit("vector", lambda e: e.tensor_copy(out=t("s2g2")[:, 1, :], in_=hs[:, 0:TH]),
               reads=[("hscr", 5)], writes=[("s2g2", 1)])
        for g in range(NG):
            def mmwo(e, g=g):
                hh, off = divmod(g, max(NG // 2, 1))
                last = None
                for half in range(2):
                    last = e.matmul(pw[PB0 + half][:, 0:512],
                                    t("s2g2")[hh * 64:(hh + 1) * 64, 1, off * P:(off + 1) * P],
                                    t("w_Wout")[hh * 64:(hh + 1) * 64, half * 512:(half + 1) * 512],
                                    start=True, stop=False)
                    last = e.matmul(pw[PB0 + half][:, 0:512],
                                    t("ones_row")[0:1, 0:P],
                                    t("rows2")[0:1, half * 512:(half + 1) * 512],
                                    start=False, stop=True)
                return last
            p.emit("tensor", mmwo, reads=[("s2g2", 1), "w_Wout", ("w_Wout", "hi"), ("rows2", "bout"), "ones_row"],
                   writes=[(PB(PB0),), (PB(PB1),)])
            for half in range(2):
                p.emit("scalar", lambda e, half=half: e.activation(
                    out=t("evs")[:, half, :], in_=pw[PB0 + half][:, 0:512],
                    func=AF.Identity),
                    reads=[(PB(PB0 + half),)], writes=[("evs", half)])
                p.emit("gpsimd", lambda e, g=g, half=half: e.tensor_tensor(
                    out=t("y0t")[:, g, half * 512:(half + 1) * 512],
                    in0=x0[:, g, half * 512:(half + 1) * 512],
                    in1=t("evs")[:, half, :], op=ALU.add),
                    reads=["x0", ("evs", half)], writes=[("y0t", g, half)])
        p.emit("sync", lambda e: e.dma_start(
            out=y0sp[t0:t0 + TT, :].rearrange("(g p) c -> p g c", p=P), in_=t("y0t")[:, :, :]),
            reads=[("y0t", g, h) for g in range(NG) for h in range(2)], dma=f"c_y0{ti % 2}")

        a1f = t("a1f")
        scr = t("cscr").rearrange("p cc n -> p (cc n)")
        tp = taps
        for cc in range(NCH):
            p.emit("vector", lambda e, cc=cc: e.tensor_scalar(
                out=scr[:, cc * W2:cc * W2 + W2], in0=u_fm[:, cc, 0:W2],
                scalar1=tp[:, 0, cc, 0:1], scalar2=tp[:, 0, cc, 3:4], op0=ALU.mult, op1=ALU.add),
                reads=ufmk + [("taps", 0, "w"), ("taps", 0, "b")], writes=[("cscr", cc)])
            p.emit("vector", lambda e, cc=cc: e.scalar_tensor_tensor(
                out=scr[:, cc * W2:cc * W2 + W2], in0=u_fm[:, cc, 2:W],
                scalar=tp[:, 0, cc, 2:3], in1=scr[:, cc * W2:cc * W2 + W2],
                op0=ALU.mult, op1=ALU.add),
                reads=ufmk + [("taps", 0, "w"), ("cscr", cc)], writes=[("cscr", cc)])
            p.emit("vector", lambda e, cc=cc: e.scalar_tensor_tensor(
                out=a1f[:, cc, 1:W - 1], in0=u_fm[:, cc, 1:W - 1],
                scalar=tp[:, 0, cc, 1:2], in1=scr[:, cc * W2:cc * W2 + W2],
                op0=ALU.mult, op1=ALU.add),
                reads=ufmk + [("taps", 0, "w"), ("cscr", cc)], writes=[("a1f", cc)])
        a1k = [("a1f", cc) for cc in range(NCH)]
        p.emit("sync", lambda e: e.dma_start(
            out=a1sp[ti],
            in_=a1f[:, :, HL:HL + TT]),
            reads=a1k, dma=f"c_a1{ti % 2}")

        p.emit("gpsimd", lambda e: e.tensor_copy(
            out=t("a1bf")[:, :, 0:T4],
            in_=a1f[:, :, 2:W2]),
            reads=a1k, writes=["a1bf"])
        p.emit("scalar", lambda e: e.activation(
            out=t("sqbf")[:, :, :],
            in_=a1f[:, :, 2:W2], func=AF.Square),
            reads=a1k, writes=["sqbf"])
        ln_fm(t("a1bf"), ["a1bf"], "sqbf", T4, a1f, 2, a1k, t("xn1"), "xn1", PB0)
        xn1k = [("xn1", cc) for cc in range(NCH)]

        for m in range(NCH):
            bk = nbank(par)
            def mma(e, m=m, bk=bk):
                last = None
                for cc in range(NCH):
                    last = e.matmul(pw[bk][:, 0:TT], t("w_actp")[:, cc, m * P:(m + 1) * P],
                                    t("xn1")[:, cc, 2:T2], start=(cc == 0), stop=False)
                return e.matmul(pw[bk][:, 0:TT], t("r_actb")[0:1, m * P:(m + 1) * P],
                                t("ones_row")[0:1, 0:TT], start=False, stop=True)
            p.emit("tensor", mma, reads=xn1k + ["w_actp", ("r_actb",), "ones_row"],
                   writes=[(PB(bk),)])
            p.emit("scalar", lambda e, m=m, bk=bk: e.activation(
                out=t("acr")[:, m, :], in_=pw[bk][:, 0:TT], func=AF.Silu),
                reads=[(PB(bk),)], writes=[("acr", m)])
        p.emit("sync", lambda e: e.dma_start(
            out=arsp[ti], in_=t("acr")[:, :, :]),
            reads=[("acr", m) for m in range(NCH)], dma=f"c_ar{ti % 2}")

        for m in range(NCH):
            chunks = [(o, nn, nbank(par)) for (o, nn) in splits(0, T4)]
            def mmi(e, m=m, chunks=chunks):
                last = None
                for (o, nn, bk) in chunks:
                    for cc in range(NCH):
                        last = e.matmul(pw[bk][:, 0:nn], t("w_in")[:, cc, m * P:(m + 1) * P],
                                        t("xn1")[:, cc, o:o + nn], start=(cc == 0), stop=False)
                    last = e.matmul(pw[bk][:, 0:nn], t("r_inb")[0:1, m * P:(m + 1) * P],
                                    t("ones_row")[0:1, 0:nn], start=False, stop=True)
                return last
            p.emit("tensor", mmi, reads=xn1k + ["w_in", ("r_inb",), "ones_row"],
                   writes=[(PB(bk),) for (_o, _n, bk) in chunks])
            for (o, nn, bk) in chunks:
                p.emit("vector", lambda e, m=m, o=o, nn=nn, bk=bk: e.tensor_copy(
                    out=t("xi0")[:, m, o:o + nn], in_=pw[bk][:, 0:nn]),
                    reads=[(PB(bk),)], writes=[("xi0", m)])
        xi0k = [("xi0", m) for m in range(NCH)]
        if ti == 0:
            p.emit("vector", lambda e: e.tensor_scalar(
                out=t("xi0")[:, :, 0:2], in0=t("xi0")[:, :, 0:2],
                scalar1=t("lrmask")[:, 0:1], scalar2=None, op0=ALU.mult),
                reads=xi0k + ["lrmask"], writes=xi0k)
        if ti == NTILE - 1:
            p.emit("vector", lambda e: e.tensor_scalar(
                out=t("xi0")[:, :, T2:T4], in0=t("xi0")[:, :, T2:T4],
                scalar1=t("lrmask")[:, 1:2], scalar2=None, op0=ALU.mult),
                reads=xi0k + ["lrmask"], writes=xi0k)

        for cc in range(NCH):
            p.emit("vector", lambda e, cc=cc: e.tensor_scalar(
                out=scr[:, cc * W2:cc * W2 + T2], in0=t("xi0")[:, cc, 0:T2],
                scalar1=tp[:, 1, cc, 0:1], scalar2=tp[:, 1, cc, 3:4], op0=ALU.mult, op1=ALU.add),
                reads=xi0k + [("taps", 1, "w"), ("taps", 1, "b")], writes=[("cscr", cc)])
            p.emit("vector", lambda e, cc=cc: e.scalar_tensor_tensor(
                out=scr[:, cc * W2:cc * W2 + T2], in0=t("xi0")[:, cc, 2:T4],
                scalar=tp[:, 1, cc, 2:3], in1=scr[:, cc * W2:cc * W2 + T2],
                op0=ALU.mult, op1=ALU.add),
                reads=xi0k + [("taps", 1, "w"), ("cscr", cc)], writes=[("cscr", cc)])
            p.emit("vector", lambda e, cc=cc: e.scalar_tensor_tensor(
                out=t("xi")[:, cc, :], in0=t("xi0")[:, cc, 1:T2 + 1],
                scalar=tp[:, 1, cc, 1:2], in1=scr[:, cc * W2:cc * W2 + T2],
                op0=ALU.mult, op1=ALU.add),
                reads=xi0k + [("taps", 1, "w"), ("cscr", cc)], writes=[("xi", cc, "c")])
            p.emit("scalar", lambda e, cc=cc: e.activation(
                out=t("xi")[:, cc, :], in_=t("xi")[:, cc, :], func=AF.Silu),
                reads=[("xi", cc, "c")], writes=[("xi", cc)])
        xik = [("xi", cc) for cc in range(NCH)]
        if ti == 0:
            p.emit("vector", lambda e: e.tensor_scalar(
                out=t("xi")[:, :, 0:1], in0=t("xi")[:, :, 0:1],
                scalar1=t("lrmask")[:, 0:1], scalar2=None, op0=ALU.mult),
                reads=xik + ["lrmask"], writes=xik)
        if ti == NTILE - 1:
            p.emit("vector", lambda e: e.tensor_scalar(
                out=t("xi")[:, :, T2 - 1:T2], in0=t("xi")[:, :, T2 - 1:T2],
                scalar1=t("lrmask")[:, 1:2], scalar2=None, op0=ALU.mult),
                reads=xik + ["lrmask"], writes=xik)

        for cc in range(NCH):
            p.emit("vector", lambda e, cc=cc: e.tensor_scalar(
                out=scr[:, cc * W2:cc * W2 + TT], in0=t("xi")[:, cc, 0:TT],
                scalar1=tp[:, 2, cc, 0:1], scalar2=tp[:, 2, cc, 3:4], op0=ALU.mult, op1=ALU.add),
                reads=xik + [("taps", 2, "w"), ("taps", 2, "b")], writes=[("cscr", cc)])
            p.emit("vector", lambda e, cc=cc: e.scalar_tensor_tensor(
                out=scr[:, cc * W2:cc * W2 + TT], in0=t("xi")[:, cc, 2:T2],
                scalar=tp[:, 2, cc, 2:3], in1=scr[:, cc * W2:cc * W2 + TT],
                op0=ALU.mult, op1=ALU.add),
                reads=xik + [("taps", 2, "w"), ("cscr", cc)], writes=[("cscr", cc)])
            p.emit("vector", lambda e, cc=cc: e.scalar_tensor_tensor(
                out=t("lepe_sb")[:, cc, :], in0=t("xi")[:, cc, 1:TT + 1],
                scalar=tp[:, 2, cc, 1:2], in1=scr[:, cc * W2:cc * W2 + TT],
                op0=ALU.mult, op1=ALU.add),
                reads=xik + [("taps", 2, "w"), ("cscr", cc)], writes=[("lepe_sb", cc)])
        p.emit("sync", lambda e: e.dma_start(
            out=lpsp[ti], in_=t("lepe_sb")[:, :, :]),
            reads=[("lepe_sb", cc) for cc in range(NCH)], dma=f"c_lp{ti % 2}")

        return

    def qk_joint(ta, tb):
        for j in range(16):
            sl3 = j % 4
            p.curpar[0] = 0
            p.emit("sync", lambda e, j=j, sl3=sl3: e.dma_start(
                out=tens["w3s"][:, sl3, :, :], in_=qkbf[j]),
                reads=[("qkbf", cc2) for cc2 in range(NCH)],
                writes=[("w3s", sl3)], dma=f"c_qs{sl3}")
            for par in (0, 1):
                p.curpar[0] = par
                bk = nbank(par)
                def mmq(e, j=j, bk=bk, sl3=sl3):
                    last = None
                    for cc in range(NCH):
                        last = e.matmul(pw[bk][:, 0:TT], tens["w3s"][:, sl3, cc, :],
                                        t("xi")[:, cc, 1:TT + 1], start=(cc == 0), stop=False)
                    return e.matmul(pw[bk][:, 0:TT], t("r_qkb")[0:1, j * P:(j + 1) * P],
                                    t("ones_row")[0:1, 0:TT], start=False, stop=True)
                p.emit("tensor", mmq,
                       reads=[("xi", cc) for cc in range(NCH)] +
                             [("w3s", sl3), ("r_qkb",), "ones_row"],
                       writes=[(PB(bk),)])
                p.emit("scalar", lambda e, bk=bk: e.activation(
                    out=t("scab")[:, 0, :], in_=pw[bk][:, 0:TT], func=AF.Relu,
                    scale=-1.0),
                    reads=[(PB(bk),)], writes=[("scab", 0)])
                p.emit("scalar", lambda e: e.activation(
                    out=t("scab")[:, 1, :], in_=t("scab")[:, 0, :], func=AF.Exp,
                    scale=-1.0),
                    reads=[("scab", 0)], writes=[("scab", 1)])
                p.emit("vector", lambda e, j=j, bk=bk: e.scalar_tensor_tensor(
                    out=t("qke")[:, j, :], in0=pw[bk][:, 0:TT], scalar=0.0,
                    in1=t("scab")[:, 1, :], op0=ALU.max, op1=ALU.add),
                    reads=[(PB(bk),), ("scab", 1)], writes=[("qke", j)])
        p.curpar[0] = 0

    def pass1_tail(ti):
        p.curpar[0] = ti % 2
        par = ti % 2
        t0 = ti * TT
        xik = [("xi", cc) for cc in range(NCH)]

        for cc in range(NCH):
            p.emit("vector", lambda e, cc=cc: e.reduce_sum(
                out=t("red1")[:, 0:1], in_=t("qke")[:, 8 + cc, :], axis=AX.X),
                reads=[("qke", 8 + cc)], writes=[("red1", "k")])
            p.emit("vector", lambda e, cc=cc: e.tensor_tensor(
                out=t("ksum")[:, cc:cc + 1], in0=t("ksum")[:, cc:cc + 1],
                in1=t("red1")[:, 0:1], op=ALU.add),
                reads=[("red1", "k"), "ksum"], writes=["ksum"])

        for j in range(16):
            bk = nbank(par)
            p.emit("tensor", lambda e, j=j, bk=bk: e.matmul(
                pw[bk][:, 0:TT], t("Rm")[:, j % 8, :], t("qke")[:, j, :], start=True, stop=True),
                reads=[("qke", j), ("Rm", j % 8)], writes=[(PB(bk),)])
            dst = t("qro") if j < 8 else t("kro")
            key = ("qro", j) if j < 8 else ("kro", j - 8)
            p.emit("scalar", lambda e, j=j, bk=bk, dst=dst: e.activation(
                out=dst[:, j % 8, :], in_=pw[bk][:, 0:TT], func=AF.Identity,
                bias=t("crope_col")[:, j % 8:j % 8 + 1]),
                reads=[(PB(bk),), "crope_col"], writes=[key])
        p.emit("sync", lambda e: e.dma_start(
            out=qrsp[ti], in_=t("qro")[:, :, :]),
            reads=[("qro", j) for j in range(8)], dma=f"c_qr{ti % 2}")

        pt = pt4 if par == 0 else pt5
        ptk = "pt4" if par == 0 else "pt5"
        for g in range(NG):
            def trk(e, g=g):
                last = None
                for cc in range(NCH):
                    last = e.transpose(pt[:, cc * P:(cc + 1) * P],
                                       t("kro")[:, cc, g * P:(g + 1) * P], t("ident_b")[:, :])
                return last
            p.emit("tensor", trk, reads=[("kro", cc) for cc in range(NCH)] + ["ident_b"],
                   writes=[(ptk,)])
            p.emit("scalar", lambda e, g=g: e.activation(
                out=t("krtvh")[:, 0, g, :], in_=pt[:, :], func=AF.Identity),
                reads=[(ptk,)], writes=[("krtvh", "k", g)])
            def trv(e, g=g):
                last = None
                for cc in range(NCH):
                    last = e.transpose(pt[:, cc * P:(cc + 1) * P],
                                       t("xi")[:, cc, 1 + g * P:1 + (g + 1) * P], t("ident_b")[:, :])
                return last
            p.emit("tensor", trv, reads=xik + ["ident_b", ("krtvh", "k", g)], writes=[(ptk, "v")])
            p.emit("scalar", lambda e, g=g: e.activation(
                out=t("krtvh")[:, 1, g, :], in_=pt[:, :], func=AF.Identity),
                reads=[(ptk, "v")], writes=[("krtvh", "v", g)])
            def mmkv(e, g=g, ti=ti):
                last = None
                for h in range(H):
                    last = e.matmul(pkv[0:64, h * 64:(h + 1) * 64],
                                    t("krtvh")[:, 0, g, h * 64:(h + 1) * 64],
                                    t("krtvh")[:, 1, g, h * 64:(h + 1) * 64],
                                    start=False, stop=False)
                return last
            p.emit("tensor", mmkv, reads=[("krtvh", "k", g), ("krtvh", "v", g)], writes=[("pkv",)])

    assert NTILE % 2 == 0

    def pair_run(tile_fn, ta, tb):
        p.record()
        tile_fn(ta)
        A = p.stop_record()
        p.record()
        tile_fn(tb)
        B = p.stop_record()
        for i in range(max(len(A), len(B))):
            if i < len(A):
                p.commit(A[i])
            if i < len(B):
                p.commit(B[i])

    for tj in range(NTILE // 2):
        pair_run(pass1_head, 2 * tj, 2 * tj + 1)
        qk_joint(2 * tj, 2 * tj + 1)
        pair_run(pass1_tail, 2 * tj, 2 * tj + 1)
    p.curpar[0] = 0

    def pkv_close(e):
        e.matmul(pkv[:, 0:512], t("zrow")[0:1, :], t("ones_row")[0:1, 0:512],
                 start=False, stop=True)
        return e.matmul(pkv[:, 512:1024], t("zrow")[0:1, :], t("ones_row")[0:1, 0:512],
                        start=False, stop=True)
    p.emit("tensor", pkv_close, reads=["zrow", "ones_row", ("pkv",)], writes=[("pkv",)])
    p.emit("vector", lambda e: e.tensor_copy(out=t("kv_raw")[0:64, :], in_=pkv[0:64, :]),
           reads=[("pkv",)], writes=[("kv_raw", "kv")])
    p.emit("tensor", lambda e: e.transpose(pw[0][0:NCH, 0:P], t("ksum")[:, :], t("ident_f")[:, :]),
           reads=["ksum", "ident_f"], writes=[(PB(0),)])
    p.emit("vector", lambda e: e.tensor_copy(out=t("lscr")[0:NCH, 0:P], in_=pw[0][0:NCH, 0:P]),
           reads=[(PB(0),)], writes=[("lscr", "kst")])
    p.emit("sync", lambda e: e.dma_start(out=kmb.rearrange("(a b) -> a b", a=NCH), in_=t("lscr")[0:NCH, 0:P]),
           reads=[("lscr", "kst")], writes=["kmb"], dma="c_km")
    p.emit("sync", lambda e: e.dma_start(out=t("kv_raw")[64:65, :], in_=kmb[0:C][None, :]),
           reads=["kmb"], writes=[("kv_raw", "ks")], dma="c_km2")
    p.emit("gpsimd", lambda e: e.indirect_dma_start(
        out=S_kv[:, :], out_offset=IndirectOffsetOnAxis(ap=t("idx")[0:65, 0:1], axis=0),
        in_=t("kv_raw")[:, :], in_offset=None),
        reads=[("kv_raw", "kv"), ("kv_raw", "ks"), ("idx", 0)], writes=[("S_kv",)], dma="c_hs1")
    p.emit("gpsimd", lambda e: e.indirect_dma_start(
        out=S_fl[:, :], out_offset=IndirectOffsetOnAxis(ap=t("idx")[0:2, 2:3], axis=0),
        in_=t("magic")[:, :], in_offset=None),
        reads=["magic", ("idx", 2), ("S_kv",)], writes=[("S_fl", "kv")], dma="c_hs1f")

    def make_poll(sem, rows, tag, prev_keys):
        def fn(e):
            with e.register(f"tg_{tag}") as tg:
                e.reg_mov(tg, 0)
                for i, r in enumerate(rows):
                    e.dma_start(out=t("fbuf")[0:1, i * 16:(i + 1) * 16],
                                in_=S_fl[r:r + 1, :]).then_inc(sem, 16)
                    e.reg_alu(tg, tg, 16, ALU.add)
                    e.wait_ge(sem, tg)
                    with e.While(lambda i=i: e.value_load(t("fbuf")[0:1, i * 16:i * 16 + 1]) != MAGIC):
                        e.dma_start(out=t("fbuf")[0:1, i * 16:(i + 1) * 16],
                                    in_=S_fl[r:r + 1, :]).then_inc(sem, 16)
                        e.reg_alu(tg, tg, 16, ALU.add)
                        e.wait_ge(sem, tg)
            return e.nop()
        p.emit("gpsimd", fn, reads=prev_keys, writes=[("fbuf", tag)])

    make_poll(poll1, (0, 1), "kv", [("S_fl", "kv"), "fbuf"])
    p.emit("sync", lambda e: e.dma_start(
        out=t("kv_cmb")[0:65, :], in_=S_kv[0:65, :]),
        reads=[("fbuf", "kv"), ("S_kv",)], writes=[("kv_cmb", "rd")], dma="c_hs1r")
    p.emit("vector", lambda e: e.tensor_copy(out=t("kv_raw")[0:65, :], in_=t("kv_cmb")[0:65, :]),
           reads=[("kv_cmb", "rd"), ("kv_raw", "kv"), ("kv_raw", "ks")], writes=[("kv_raw", "s0")])
    p.emit("sync", lambda e: e.dma_start(
        out=t("kv_cmb")[0:65, :], in_=S_kv[65:130, :]),
        reads=[("kv_raw", "s0"), ("S_kv",)], writes=[("kv_cmb", "rd2")], dma="c_hs1r2")
    p.emit("vector", lambda e: e.tensor_tensor(
        out=t("kv_raw")[0:65, :], in0=t("kv_raw")[0:65, :], in1=t("kv_cmb")[0:65, :],
        op=ALU.add),
        reads=[("kv_cmb", "rd2"), ("kv_raw", "s0")], writes=[("kv_raw", "tot")])
    p.emit("vector", lambda e: e.tensor_scalar(
        out=t("kv_raw")[64:65, :], in0=t("kv_raw")[64:65, :], scalar1=1.0 / SEQ,
        scalar2=1.0, op0=ALU.mult, op1=ALU.add),
        reads=[("kv_raw", "tot")], writes=[("kv_raw", "km")])
    p.emit("sync", lambda e: e.dma_start(out=kmb[None, 0:C], in_=t("kv_raw")[64:65, :]),
           reads=[("kv_raw", "km")], writes=["kmb"], dma="c_km3")
    p.emit("sync", lambda e: e.dma_start(out=t("km_col")[:, :], in_=kmb.rearrange("(cc p) -> p cc", p=P),
           allow_slow_non_contiguous=True),
           reads=["kmb"], writes=["km_col"], dma="c_km4")
    p.emit("vector", lambda e: e.tensor_copy(out=t("km_bf")[:, :], in_=t("km_col")[:, :]),
           reads=["km_col"], writes=["km_bf"])
    def mmkm(e):
        last = None
        for cc in range(NCH):
            last = e.matmul(pw[0][:, cc:cc + 1], t("Rm")[:, cc, :], t("km_bf")[:, cc:cc + 1],
                            start=True, stop=True)
        return last
    p.emit("tensor", mmkm, reads=["km_bf"] + [("Rm", cc) for cc in range(NCH)], writes=[(PB(0),)])
    p.emit("vector", lambda e: e.tensor_copy(out=t("km_bf")[:, :], in_=pw[0][:, 0:NCH]),
           reads=[(PB(0),)], writes=[("km_bf", "r")])
    p.emit("gpsimd", lambda e: e.memset(t("kmr_bd")[:, :, :], 0.0), writes=[("kmr_bd", "z")])

    kmrflat = t("kmr_bd").rearrange("p cc m -> p (cc m)")
    p.emit("gpsimd", lambda e: e.dma_start(
        out=kmrflat[0:64, 0:127:18], in_=t("km_bf")[0:64, :],
        allow_slow_non_contiguous=True),
        reads=[("km_bf", "r"), ("kmr_bd", "z")], writes=[("kmr_bd", "a")], dma="c_kbd")
    p.emit("gpsimd", lambda e: e.dma_start(
        out=kmrflat[64:128, 1:128:18], in_=t("km_bf")[64:128, :],
        allow_slow_non_contiguous=True),
        reads=[("km_bf", "r"), ("kmr_bd", "z"), ("kmr_bd", "a")],
        writes=[("kmr_bd", "b")], dma="c_kbd2")
    p.emit("vector", lambda e: e.tensor_scalar(
        out=t("kv_raw")[0:64, :], in0=t("kv_raw")[0:64, :], scalar1=1.0 / SEQ,
        scalar2=None, op0=ALU.mult),
        reads=[("kv_raw", "tot")], writes=[("kv_raw", "sc")])
    p.emit("vector", lambda e: e.tensor_copy(
        out=t("qr2").rearrange("p cc n -> p (cc n)")[0:64, 0:C], in_=t("kv_raw")[0:64, :]),
        reads=[("kv_raw", "sc")], writes=[("qr2", "kvbf")])
    p.emit("gpsimd", lambda e: e.memset(t("kv_bd")[:, :, :], 0.0), writes=[("kv_bd", "z")])
    qr2flat = t("qr2").rearrange("p cc n -> p (cc n)")
    p.emit("gpsimd", lambda e: e.dma_start(
        out=t("kv_bd")[0:64, :, 0:64],
        in_=qr2flat[0:64, 0:C].rearrange("p (cc hf ee) -> p cc hf ee", cc=NCH, hf=2)[:, :, 0, :]),
        reads=[("qr2", "kvbf"), ("kv_bd", "z")], writes=[("kv_bd", "a")], dma="c_kvb")
    p.emit("gpsimd", lambda e: e.dma_start(
        out=t("kv_bd")[64:128, :, 64:128],
        in_=qr2flat[0:64, 0:C].rearrange("p (cc hf ee) -> p cc hf ee", cc=NCH, hf=2)[:, :, 1, :]),
        reads=[("qr2", "kvbf"), ("kv_bd", "z"), ("kv_bd", "a")], writes=[("kv_bd", "b")],
        dma="c_kvb2")
    selz = t("selz")
    p.emit("gpsimd", lambda e: e.memset(selz[:, :], 0.0), writes=["selz"])
    p.emit("gpsimd", lambda e: e.affine_select(
        out=selz.rearrange("h (cc hf m) -> h cc hf m", cc=NCH, hf=2),
        in_=selz.rearrange("h (cc hf m) -> h cc hf m", cc=NCH, hf=2),
        compare_op=ALU.not_equal, fill=1.0, base=0,
        pattern=[[-2, NCH], [-1, 2], [0, 64]], channel_multiplier=1),
        reads=["selz"], writes=["selz"])

    kvbdk = [("kv_bd", "a"), ("kv_bd", "b")]
    kmrk = [("kmr_bd", "a"), ("kmr_bd", "b")]

    def load_w2(wname, dst, nsrc, ncols, wkey, gfold=None):
        for cc in range(nsrc):
            p.emit("sync", lambda e, cc=cc: e.dma_start(
                out=stg[:, 0:ncols], in_=wn[wname][cc * P:(cc + 1) * P, :]),
                writes=[("w_stage", wname, cc)], dma=f"c_w2{cc % 4}")
            if gfold is None:
                p.emit("vector", lambda e, cc=cc: e.tensor_copy(
                    out=dst[:, cc, :], in_=stg[:, 0:ncols]),
                    reads=[("w_stage", wname, cc)], writes=[(wkey, cc)])
            else:
                p.emit("vector", lambda e, cc=cc: e.tensor_scalar(
                    out=dst[:, cc, :], in0=stg[:, 0:ncols],
                    scalar1=t("g_cols")[:, gfold, cc:cc + 1], scalar2=None, op0=ALU.mult),
                    reads=[("w_stage", wname, cc), ("g_cols", gfold)], writes=[(wkey, cc)])

    load_w2("out_w", t("w_out"), NCH, C, "w_out")
    woutk = [("w_out", cc) for cc in range(NCH)]
    p.emit("sync", lambda e: e.dma_start(out=stg[0:1, 3 * C:4 * C], in_=wn["out_b"][None, :]),
           writes=[("w_stage", "r_out_b")], dma="c_wl2")
    p.emit("vector", lambda e: e.tensor_copy(out=t("rows2")[0:1, 0:C], in_=stg[0:1, 3 * C:4 * C]),
           reads=[("w_stage", "r_out_b")], writes=[("rows2", "out_b")])

    def pass2a_tile(ti):
        p.curpar[0] = ti % 2
        par = ti % 2
        PB0, PB1 = 2 * par, 2 * par + 1
        t0 = ti * TT
        p.emit("sync", lambda e: e.dma_start(
            out=t("qr2")[:, :, :], in_=qrsp[ti]),
            writes=["qr2"], dma=f"c_q2{ti % 2}")
        p.emit("sync", lambda e: e.dma_start(
            out=t("ar2")[:, :, :], in_=arsp[ti]),
            writes=["ar2"], dma=f"c_r2{ti % 2}")
        p.emit("sync", lambda e: e.dma_start(
            out=t("lp2")[:, :, :], in_=lpsp[ti]),
            writes=["lp2"], dma=f"c_l2{ti % 2}")
        p.emit("sync", lambda e: e.dma_start(
            out=t("a1l")[:, :, :], in_=a1sp[ti]),
            writes=["a1l"], dma=f"c_al{ti % 2}")
        def mms(e):
            last = None
            for cc in range(NCH):
                last = e.matmul(pw[PB0][0:16, 0:TT], t("kmr_bd")[:, cc, :], t("qr2")[:, cc, :],
                                start=(cc == 0), stop=(cc == NCH - 1))
            return last
        p.emit("tensor", mms, reads=["qr2"] + kmrk, writes=[(PB(PB0),)])
        p.emit("vector", lambda e: e.tensor_scalar(out=t("zsh")[:, :], in0=pw[PB0][0:16, 0:TT],
                                                   scalar1=1.0, scalar2=1e-6, op0=ALU.mult, op1=ALU.add),
               reads=[(PB(PB0),)], writes=["zsh"])
        p.emit("vector", lambda e: e.reciprocal(out=t("zsh")[:, :], in_=t("zsh")[:, :]),
               reads=["zsh"], writes=["zsh"])
        p.emit("vector", lambda e: e.tensor_copy(out=t("zbf")[:, :], in_=t("zsh")[:, :]),
               reads=["zsh"], writes=["zbf"])
        for cc in range(NCH):
            p.emit("tensor", lambda e, cc=cc: e.matmul(
                pw[PB0][:, 0:TT], t("selz")[:, cc * P:(cc + 1) * P], t("zbf")[:, :],
                start=True, stop=True),
                reads=["zbf", "selz"], writes=[(PB(PB0),)])
            p.emit("vector", lambda e, cc=cc: e.tensor_tensor(
                out=t("zae")[:, :], in0=pw[PB0][:, 0:TT], in1=t("ar2")[:, cc, :], op=ALU.mult),
                reads=[(PB(PB0),), "ar2"], writes=["zae"])
            p.emit("vector", lambda e, cc=cc: e.tensor_tensor(
                out=t("lab")[:, :], in0=t("lp2")[:, cc, :], in1=t("ar2")[:, cc, :], op=ALU.mult),
                reads=["lp2", "ar2"], writes=[("lab",)])
            bk = PB1
            p.emit("tensor", lambda e, cc=cc, bk=bk: e.matmul(
                pw[bk][:, 0:TT], t("kv_bd")[:, cc, :], t("qr2")[:, cc, :], start=True, stop=True),
                reads=["qr2"] + kvbdk, writes=[(PB(bk),)])
            p.emit("vector", lambda e, bk=bk: e.tensor_tensor(
                out=t("zscr")[:, :], in0=pw[bk][:, 0:TT], in1=t("zae")[:, :], op=ALU.mult),
                reads=[(PB(bk),), "zae"], writes=["zscr"])
            p.emit("vector", lambda e, cc=cc: e.tensor_tensor(
                out=t("aab")[:, cc, :], in0=t("zscr")[:, :], in1=t("lab")[:, :], op=ALU.add),
                reads=["zscr", ("lab",)], writes=[("aab", cc)])
        aak = [("aab", cc) for cc in range(NCH)]
        for m in range(NCH):
            bk = nbank(par)
            def mmo(e, m=m, bk=bk):
                last = None
                for cc in range(NCH):
                    last = e.matmul(pw[bk][:, 0:TT], t("w_out")[:, cc, m * P:(m + 1) * P],
                                    t("aab")[:, cc, :], start=(cc == 0), stop=False)
                return e.matmul(pw[bk][:, 0:TT], t("rows2")[0:1, m * P:(m + 1) * P],
                                t("ones_row")[0:1, 0:TT], start=False, stop=True)
            p.emit("tensor", mmo, reads=aak + woutk + [("rows2", "out_b"), "ones_row"],
                   writes=[(PB(bk),)])
            p.emit("scalar", lambda e, m=m, bk=bk: e.activation(
                out=t("evs")[:, m % 2, 0:TT], in_=pw[bk][:, 0:TT], func=AF.Identity),
                reads=[(PB(bk),)], writes=[("evs", m % 2)])
            p.emit("gpsimd", lambda e, m=m: e.tensor_tensor(
                out=t("a2w")[:, m, :], in0=t("a1l")[:, m, :], in1=t("evs")[:, m % 2, 0:TT],
                op=ALU.add),
                reads=["a1l", ("evs", m % 2)], writes=[("a2w", m)])
        a2k = [("a2w", m) for m in range(NCH)]
        p.emit("sync", lambda e: e.dma_start(
            out=a2sp[ti], in_=t("a2w")[:, :, :]),
            reads=a2k, dma=f"c_s2{ti % 2}")
        p.emit("vector", lambda e: e.tensor_copy(
            out=t("a2eg")[:, ti, :, 0:1], in_=t("a2w")[:, :, 0:1]),
            reads=a2k, writes=[("a2eg", ti, 0)])
        p.emit("vector", lambda e: e.tensor_copy(
            out=t("a2eg")[:, ti, :, 1:2], in_=t("a2w")[:, :, TT - 1:TT]),
            reads=a2k, writes=[("a2eg", ti, 1)])
        if ti == 0:
            p.emit("vector", lambda e: e.tensor_copy(
                out=t("a2keep")[:, 0, :], in_=t("a2w")[:, :, 0:1].rearrange("p cc o -> p (cc o)")),
                reads=a2k, writes=[("a2keep", 0)])
        if ti == NTILE - 1:
            p.emit("vector", lambda e: e.tensor_copy(
                out=t("a2keep")[:, 1, :], in_=t("a2w")[:, :, TT - 1:TT].rearrange("p cc o -> p (cc o)")),
                reads=a2k, writes=[("a2keep", 1)])

    for tj in range(NTILE // 2):
        pair_run(pass2a_tile, 2 * tj, 2 * tj + 1)
    p.curpar[0] = 0

    p.emit("sync", lambda e: e.dma_start(
        out=hrow.rearrange("(a p) b -> p a b", p=P), in_=t("zero8")[:, :, :]),
        reads=["zero8"], writes=[("hrow", "z")], dma="c_hz")
    p.emit("gpsimd", lambda e: e.indirect_dma_start(
        out=S_a2[:, :], out_offset=IndirectOffsetOnAxis(ap=t("idx")[:, 1:2], axis=0),
        in_=t("a2keep")[:, 0, :], in_offset=None),
        reads=[("a2keep", 0), ("idx", 1)], writes=[("S_a2", "a")], dma="c_h2a")
    p.emit("gpsimd", lambda e: e.indirect_dma_start(
        out=S_a2[:, :], out_offset=IndirectOffsetOnAxis(ap=t("idx")[:, 6:7], axis=0),
        in_=t("a2keep")[:, 1, :], in_offset=None),
        reads=[("a2keep", 1), ("idx", 6), ("S_a2", "a")], writes=[("S_a2", "b")], dma="c_h2b")
    p.emit("gpsimd", lambda e: e.indirect_dma_start(
        out=S_fl[:, :], out_offset=IndirectOffsetOnAxis(ap=t("idx")[0:2, 3:4], axis=0),
        in_=t("magic")[:, :], in_offset=None),
        reads=["magic", ("idx", 3), ("S_a2", "a"), ("S_a2", "b")], writes=[("S_fl", "a2")],
        dma="c_h2f")
    make_poll(poll2, (2, 3), "a2", [("S_fl", "a2"), "fbuf"])
    p.emit("sync", lambda e: e.dma_start(
        out=t("a2rd")[:, :, :], in_=S_a2[0:2 * P, :].rearrange("(s p) n -> p s n", s=2)),
        reads=[("fbuf", "a2"), ("S_a2", "a"), ("S_a2", "b")], writes=["a2rd"], dma="c_h2r")
    p.emit("gpsimd", lambda e: e.indirect_dma_start(
        out=hrow[:, :], out_offset=IndirectOffsetOnAxis(ap=t("idx")[:, 4:5], axis=0),
        in_=t("a2rd")[:, 0, :], in_offset=None),
        reads=["a2rd", ("idx", 4), ("hrow", "z")], writes=[("hrow", "a")], dma="c_h3a")
    p.emit("gpsimd", lambda e: e.indirect_dma_start(
        out=hrow[:, :], out_offset=IndirectOffsetOnAxis(ap=t("idx")[:, 5:6], axis=0),
        in_=t("a2rd")[:, 1, :], in_offset=None),
        reads=["a2rd", ("idx", 5), ("hrow", "z"), ("hrow", "a")], writes=[("hrow", "b")],
        dma="c_h3b")
    hrk = [("hrow", "a"), ("hrow", "b")]
    p.emit("sync", lambda e: e.dma_start(
        out=t("hpar")[:, :, :], in_=hrow[0:2 * P, :].rearrange("(s p) b -> p s b", s=2)),
        reads=hrk, writes=["hpar"], dma="c_hp")

    cvf = tens["qke~1"].rearrange("p a b -> p (a b)")
    qkf = tens["qke~0"].rearrange("p a b -> p (a b)")
    for cc in range(NCH):
        p.emit("sync", lambda e, cc=cc: e.dma_start(
            out=stg[:, 0:HID], in_=wn["fc1_w"][cc * P:(cc + 1) * P, :]),
            writes=[("w_stage", "f1", cc)], dma=f"c_w2{cc % 4}")
        p.emit("vector", lambda e, cc=cc: e.tensor_scalar(
            out=cvf[:, 0:HID], in0=stg[:, 0:HID],
            scalar1=t("g_cols")[:, 1, cc:cc + 1], scalar2=None, op0=ALU.mult),
            reads=[("w_stage", "f1", cc), ("g_cols", 1)], writes=[("qke~1", "fold")])
        p.emit("sync", lambda e, cc=cc: e.dma_start(
            out=fc1bf[:, :, cc, :].rearrange("hcc p m -> p hcc m"),
            in_=cvf[:, 0:HID].rearrange("p (hcc m) -> p hcc m", m=P)),
            reads=[("qke~1", "fold")], writes=[("fc1bf", cc)], dma=f"c_w2o{cc % 4}")
        p.emit("vector", lambda e, cc=cc: e.tensor_copy(out=qkf[:, 0:HID], in_=stg[:, 0:HID]),
               reads=[("w_stage", "f1", cc)], writes=[("qke~0", "raw")])
        if cc == 0:
            def mmb2open(e):
                last = None
                for i in range(4):
                    last = e.matmul(pw[i][:, 0:512], t("zrow")[0:1, :],
                                    t("ones_row")[0:1, 0:512], start=True, stop=False)
                return last
            p.emit("tensor", mmb2open, reads=["zrow", "ones_row"],
                   writes=[(PB(i),) for i in range(4)])
        def mmb2(e, cc=cc):
            last = None
            for oi, (o, nn) in enumerate(splits(0, HID)):
                bank = pw[oi % 4]
                row = 0 if oi < 4 else 32
                last = e.matmul(bank[row:row + 1, 0:nn], t("b_colsb")[:, 1, cc:cc + 1],
                                qkf[:, o:o + nn], start=False, stop=False)
            return last
        p.emit("tensor", mmb2, reads=[("qke~0", "raw"), "b_colsb"],
               writes=[(PB(i),) for i in range(4)])
    def mmb2close(e):
        last = None
        for i in range(4):
            last = e.matmul(pw[i][:, 0:512], t("zrow")[0:1, :],
                            t("ones_row")[0:1, 0:512], start=False, stop=True)
        return last
    p.emit("tensor", mmb2close, reads=["zrow", "ones_row"] + [(PB(i),) for i in range(4)],
           writes=[(PB(i),) for i in range(4)])
    p.emit("sync", lambda e: e.dma_start(
        out=stg[0:1, 0:HID], in_=wn["fc1_b"][None, :]),
        writes=[("w_stage", "f1b0")], dma="c_w2b")
    for oi, (o, nn) in enumerate(splits(0, HID)):
        bank = pw[oi % 4]
        row = 0 if oi < 4 else 32
        p.emit("vector", lambda e, o=o, nn=nn, bank=bank, row=row: e.tensor_tensor(
            out=t("lscr")[0:1, 0:nn], in0=bank[row:row + 1, 0:nn],
            in1=stg[0:1, o:o + nn], op=ALU.add),
            reads=[(PB(oi % 4),), ("w_stage", "f1b0")], writes=[("lscr", "f")])
        p.emit("vector", lambda e, o=o, nn=nn: e.tensor_copy(
            out=t("r_fc1b")[0:1, o:o + nn], in_=t("lscr")[0:1, 0:nn]),
            reads=[("lscr", "f")], writes=[("r_fc1b",)])
    for hcc in range(NHID):
        p.emit("sync", lambda e, hcc=hcc: e.dma_start(
            out=stg[:, 0:C], in_=wn["fc2_w"][hcc * P:(hcc + 1) * P, :]),
            writes=[("w_stage", "f2", hcc)], dma=f"c_w2{hcc % 4}")
        p.emit("vector", lambda e, hcc=hcc: e.tensor_copy(out=cvf[:, 0:C], in_=stg[:, 0:C]),
               reads=[("w_stage", "f2", hcc)], writes=[("qke~1", "f2")])
        p.emit("sync", lambda e, hcc=hcc: e.dma_start(
            out=fc2bf[hcc, :, :], in_=cvf[:, 0:C]),
            reads=[("qke~1", "f2")], writes=[("fc2bf", hcc)], dma=f"c_w2o{hcc % 4}")
    p.emit("sync", lambda e: e.dma_start(out=stg[0:1, 3 * C:4 * C], in_=wn["fc2_b"][None, :]),
           writes=[("w_stage", "r_fc2_b")], dma="c_wl2")
    p.emit("vector", lambda e: e.tensor_copy(out=t("rows2")[0:1, C:2 * C], in_=stg[0:1, 3 * C:4 * C]),
           reads=[("w_stage", "r_fc2_b")], writes=[("rows2", "fc2_b")])

    def fc2bank(m):
        return pw[m % 4][:, 0:TT], (PB(m % 4),)

    def pass2b_head(ti):
        p.curpar[0] = ti % 2
        par = ti % 2
        PB0, PB1 = 2 * par, 2 * par + 1
        t0 = ti * TT
        a2m = t("a2m")
        p.emit("sync", lambda e: e.dma_start(
            out=a2m[:, :, :], in_=a2sp[ti]),
            writes=["a2m"], dma=f"c_t2{ti % 2}")
        a2tk = ["a2m"]
        if ti == 0:
            Lk, Lap = ["hpar"], lambda cc: t("hpar")[:, 0, cc:cc + 1]
        else:
            Lk = [("a2eg", ti - 1, 1)]
            Lap = lambda cc: t("a2eg")[:, ti - 1, cc, 1:2]
        if ti == NTILE - 1:
            Rk, Rap = ["hpar"], lambda cc: t("hpar")[:, 1, cc:cc + 1]
        else:
            Rk = [("a2eg", ti + 1, 0)]
            Rap = lambda cc: t("a2eg")[:, ti + 1, cc, 0:1]
        p.emit("sync", lambda e: e.dma_start(
            out=t("y0l")[:, :, :], in_=y0sp[t0:t0 + TT, :].rearrange("(g p) c -> p g c", p=P)),
            writes=["y0l"], dma=f"c_y2{ti % 2}")
        cv = t("cvs")
        tp = taps
        for cc in range(NCH):
            p.emit("vector", lambda e, cc=cc: e.tensor_scalar(
                out=cv[:, cc, 1:TT], in0=a2m[:, cc, 0:TT - 1], scalar1=tp[:, 3, cc, 0:1],
                scalar2=tp[:, 3, cc, 3:4], op0=ALU.mult, op1=ALU.add),
                reads=a2tk + [("taps", 3, "w"), ("taps", 3, "b")], writes=[("cvs", cc)])
            p.emit("vector", lambda e, cc=cc: e.tensor_scalar(
                out=cv[:, cc, 0:1], in0=Lap(cc), scalar1=tp[:, 3, cc, 0:1],
                scalar2=tp[:, 3, cc, 3:4], op0=ALU.mult, op1=ALU.add),
                reads=Lk + [("taps", 3, "w"), ("taps", 3, "b")], writes=[("cvs", cc)])
            p.emit("vector", lambda e, cc=cc: e.scalar_tensor_tensor(
                out=cv[:, cc, 0:TT - 1], in0=a2m[:, cc, 1:TT], scalar=tp[:, 3, cc, 2:3],
                in1=cv[:, cc, 0:TT - 1], op0=ALU.mult, op1=ALU.add),
                reads=a2tk + [("taps", 3, "w"), ("cvs", cc)], writes=[("cvs", cc)])
            p.emit("vector", lambda e, cc=cc: e.scalar_tensor_tensor(
                out=cv[:, cc, TT - 1:TT], in0=Rap(cc), scalar=tp[:, 3, cc, 2:3],
                in1=cv[:, cc, TT - 1:TT], op0=ALU.mult, op1=ALU.add),
                reads=Rk + [("taps", 3, "w"), ("cvs", cc)], writes=[("cvs", cc)])
            p.emit("vector", lambda e, cc=cc: e.scalar_tensor_tensor(
                out=t("a3")[:, cc, :], in0=a2m[:, cc, :], scalar=tp[:, 3, cc, 1:2],
                in1=cv[:, cc, :], op0=ALU.mult, op1=ALU.add),
                reads=a2tk + [("taps", 3, "w"), ("cvs", cc)], writes=[("a3", cc)])
        a3k = [("a3", cc) for cc in range(NCH)]
        p.emit("gpsimd", lambda e: e.tensor_copy(
            out=t("a3bf")[:, :, :],
            in_=t("a3")[:, :, :]),
            reads=a3k, writes=["a3bf"])
        p.emit("scalar", lambda e: e.activation(
            out=t("sqbf")[:, :, 0:TT],
            in_=t("a3")[:, :, :], func=AF.Square),
            reads=a3k, writes=["sqbf"])
        ln_fm(t("a3bf"), ["a3bf"], "sqbf", TT, t("a3"), 0, a3k, t("xn2"), "xn2", PB0)
        xn2k = [("xn2", cc) for cc in range(NCH)]
        return

    def mlp_joint(ta, tb):
        assert 2 * TT <= 512
        for r in range(2):
            for hh in range(16):
                hcc = 16 * r + hh
                sl = hh % 4
                p.curpar[0] = 0
                p.emit("sync", lambda e, hcc=hcc, sl=sl: e.dma_start(
                    out=t("w1s")[:, sl, :, :], in_=fc1bf[hcc]),
                    reads=[("fc1bf", cc2) for cc2 in range(NCH)],
                    writes=[("w1s", sl)], dma=f"c_f1{sl}")
                for par in (0, 1):
                    p.curpar[0] = par
                    PB0 = 2 * par
                    slp = PB0 + (hh % 2)
                    def mmf1(e, hcc=hcc, sl=sl, slp=slp):
                        last = None
                        for cc in range(NCH):
                            last = e.matmul(pw[slp][:, 0:TT], t("w1s")[:, sl, cc, :],
                                            t("xn2")[:, cc, :], start=(cc == 0), stop=False)
                        return e.matmul(pw[slp][:, 0:TT],
                                        t("r_fc1b")[0:1, hcc * P:(hcc + 1) * P],
                                        t("ones_row")[0:1, 0:TT], start=False, stop=True)
                    p.emit("tensor", mmf1,
                           reads=[("xn2", cc) for cc in range(NCH)] +
                                 [("w1s", sl), ("r_fc1b",), "ones_row"],
                           writes=[(PB(slp),)])
                    p.emit("scalar", lambda e, hh=hh, slp=slp: e.activation(
                        out=t("g1")[:, hh, :], in_=pw[slp][:, 0:TT], func=AF.Gelu),
                        reads=[(PB(slp),)], writes=[("g1", hh)])
            for mh in range(2):
                for par in (0, 1):
                    p.curpar[0] = par
                    PB0, PB1 = 2 * par, 2 * par + 1
                    def f2open(e, PB0=PB0, PB1=PB1):
                        e.matmul(pw[PB0][:, 0:512], t("zrow")[0:1, :],
                                 t("ones_row")[0:1, 0:512], start=True, stop=False)
                        return e.matmul(pw[PB1][:, 0:512], t("zrow")[0:1, :],
                                        t("ones_row")[0:1, 0:512], start=True, stop=False)
                    p.emit("tensor", f2open, reads=["zrow", "ones_row"],
                           writes=[(PB(PB0),), (PB(PB1),)])
                for hh in range(16):
                    hcc = 16 * r + hh
                    sl = hh % 4
                    p.curpar[0] = 0
                    p.emit("sync", lambda e, hcc=hcc, sl=sl, mh=mh: e.dma_start(
                        out=t("w2s")[:, sl, 0:512],
                        in_=fc2bf[hcc, :, mh * 512:(mh + 1) * 512]),
                        reads=[("fc2bf", hcc)], writes=[("w2s", sl)], dma=f"c_f2{sl}")
                    for par in (0, 1):
                        p.curpar[0] = par
                        PB0, PB1 = 2 * par, 2 * par + 1
                        def mmf2(e, hh=hh, sl=sl, PB0=PB0):
                            last = None
                            for mm in range(4):
                                bnk = pw[PB0 + mm // 2]
                                last = e.matmul(bnk[:, (mm % 2) * TT:(mm % 2 + 1) * TT],
                                                t("w2s")[:, sl, mm * P:(mm + 1) * P],
                                                t("g1")[:, hh, :], start=False, stop=False)
                            return last
                        p.emit("tensor", mmf2, reads=[("g1", hh), ("w2s", sl)],
                               writes=[(PB(PB0),), (PB(PB1),)])
                for par in (0, 1):
                    p.curpar[0] = par
                    PB0, PB1 = 2 * par, 2 * par + 1
                    def f2close(e, r=r, mh=mh, PB0=PB0, PB1=PB1):
                        if r == 0:
                            for mm in range(4):
                                m = mh * 4 + mm
                                bnk = pw[PB0 + mm // 2]
                                e.matmul(bnk[:, (mm % 2) * TT:(mm % 2 + 1) * TT],
                                         t("rows2")[0:1, C + m * P:C + (m + 1) * P],
                                         t("ones_row")[0:1, 0:TT], start=False, stop=False)
                        e.matmul(pw[PB0][:, 0:512], t("zrow")[0:1, :],
                                 t("ones_row")[0:1, 0:512], start=False, stop=True)
                        return e.matmul(pw[PB1][:, 0:512], t("zrow")[0:1, :],
                                        t("ones_row")[0:1, 0:512], start=False, stop=True)
                    p.emit("tensor", f2close,
                           reads=[("rows2", "fc2_b"), "ones_row", "zrow",
                                  (PB(PB0),), (PB(PB1),)],
                           writes=[(PB(PB0),), (PB(PB1),)])
                    for mm in range(4):
                        m = mh * 4 + mm
                        p.emit("scalar", lambda e, m=m, mm=mm, PB0=PB0: e.activation(
                            out=t("evs")[:, mm % 2, 0:TT],
                            in_=pw[PB0 + mm // 2][:, (mm % 2) * TT:(mm % 2 + 1) * TT],
                            func=AF.Identity),
                            reads=[(PB(PB0 + mm // 2),)], writes=[("evs", mm % 2)])
                        p.emit("gpsimd", lambda e, m=m, mm=mm: e.tensor_tensor(
                            out=t("a3")[:, m, :], in0=t("a3")[:, m, :],
                            in1=t("evs")[:, mm % 2, 0:TT], op=ALU.add),
                            reads=[("evs", mm % 2), ("a3", m)], writes=[("a3", m)])
        p.curpar[0] = 0

    def pass2b_exit(ti):
        p.curpar[0] = ti % 2
        par = ti % 2
        PB0, PB1 = 2 * par, 2 * par + 1
        t0 = ti * TT
        a3k = [("a3", cc) for cc in range(NCH)]
        for g in range(NG):
            def trx(e, g=g):
                last = None
                for cc in range(NCH):
                    bank = pw[PB0 + cc // 4]
                    last = e.transpose(bank[:, (cc % 4) * P:(cc % 4 + 1) * P],
                                       t("a3")[:, cc, g * P:(g + 1) * P], t("ident_f")[:, :])
                return last
            p.emit("tensor", trx, reads=a3k + ["ident_f"],
                   writes=[(PB(PB0),), (PB(PB1),)])
            for half in range(2):
                p.emit("scalar", lambda e, half=half: e.activation(
                    out=t("evs")[:, half, :], in_=pw[PB0 + half][:, 0:512],
                    func=AF.Identity),
                    reads=[(PB(PB0 + half),)], writes=[("evs", half)])
                p.emit("gpsimd", lambda e, g=g, half=half: e.tensor_tensor(
                    out=t("y0l")[:, g, half * 512:(half + 1) * 512],
                    in0=t("y0l")[:, g, half * 512:(half + 1) * 512],
                    in1=t("evs")[:, half, :], op=ALU.add),
                    reads=["y0l", ("evs", half)], writes=[("y0l", g, half)])
        p.emit("sync", lambda e: e.dma_start(
            out=out_my[t0:t0 + TT, :].rearrange("(g p) c -> p g c", p=P), in_=t("y0l")[:, :, :]),
            reads=[("y0l", g, h) for g in range(NG) for h in range(2)], dma=f"c_o{ti % 2}")

    for tj in range(NTILE // 2):
        pair_run(pass2b_head, 2 * tj, 2 * tj + 1)
        mlp_joint(2 * tj, 2 * tj + 1)
        pair_run(pass2b_exit, 2 * tj, 2 * tj + 1)

    p.barrier_on("gpsimd", [(ch, p.count[ch]) for ch in ("c_o0", "c_o1") if ch in p.count])

    with nc.Block() as block:
        p.seal(block)
    es.close()
    return nc


def kernel(**inputs):
    B, L, Cc = 4, 8192, 1024
    NT = L // 2
    HL = 4
    x = np.asarray(inputs["L0_attn_x"], dtype=np.float32)
    nc = build_program(NT=NT, TT=256)

    wmap = {
        "cpe1_w": inputs["L0_attn_cpe1_w"].reshape(Cc, 3), "cpe1_b": inputs["L0_attn_cpe1_b"],
        "norm1_g": inputs["L0_attn_norm1_g"], "norm1_b": inputs["L0_attn_norm1_b"],
        "in_w": inputs["L0_attn_in_w"], "in_b": inputs["L0_attn_in_b"],
        "actp_w": inputs["L0_attn_actp_w"], "actp_b": inputs["L0_attn_actp_b"],
        "dwc_w": inputs["L0_attn_dwc_w"].reshape(Cc, 3), "dwc_b": inputs["L0_attn_dwc_b"],
        "qk_w": inputs["L0_attn_qk_w"], "qk_b": inputs["L0_attn_qk_b"],
        "lepe_w": inputs["L0_attn_lepe_w"].reshape(Cc, 3), "lepe_b": inputs["L0_attn_lepe_b"],
        "out_w": inputs["L0_attn_out_w"], "out_b": inputs["L0_attn_out_b"],
        "cpe2_w": inputs["L0_attn_cpe2_w"].reshape(Cc, 3), "cpe2_b": inputs["L0_attn_cpe2_b"],
        "norm2_g": inputs["L0_attn_norm2_g"], "norm2_b": inputs["L0_attn_norm2_b"],
        "fc1_w": inputs["L0_attn_fc1_w"], "fc1_b": inputs["L0_attn_fc1_b"],
        "fc2_w": inputs["L0_attn_fc2_w"], "fc2_b": inputs["L0_attn_fc2_b"],
        "fWin": inputs["L0_ffn_Win"], "fbin": inputs["L0_ffn_bin_"],
        "fW1": inputs["L0_ffn_W1"], "fb1": inputs["L0_ffn_b1"],
        "fW2": inputs["L0_ffn_W2"], "fb2": inputs["L0_ffn_b2"],
        "fWout": inputs["L0_ffn_Wout"], "fbout": inputs["L0_ffn_bout"],
    }
    wmap = {k: np.ascontiguousarray(np.asarray(v, dtype=np.float32)) for k, v in wmap.items()}

    ar = np.arange(128, dtype=np.int32)
    in_maps = []
    for core in range(8):
        b, half = core // 2, core % 2
        xm = np.zeros((NT + 2 * HL, Cc), dtype=np.float32)
        s = half * NT
        lo, hi = s - HL, s + NT + HL
        clo, chi = max(lo, 0), min(hi, L)
        xm[clo - lo:clo - lo + (chi - clo)] = x[b, clo:chi]
        lr = np.zeros((128, 2), dtype=np.float32)
        lr[:, 0] = 1.0 if half == 1 else 0.0
        lr[:, 1] = 1.0 if half == 0 else 0.0
        m = {"x_my": xm, "lrmask": lr,
             "kv_idx": (np.arange(65, dtype=np.int32) + 65 * half).reshape(65, 1),
             "fkv_idx": np.array([[half], [half]], dtype=np.int32),
             "fa2_idx": np.array([[2 + half], [2 + half]], dtype=np.int32)}
        if half == 0:
            m["a2a_idx"] = (ar + 2 * 128).reshape(128, 1)
            m["a2b_idx"] = (ar + 0).reshape(128, 1)
            m["haloA_idx"] = (ar + 2 * 128).reshape(128, 1)
            m["haloB_idx"] = (ar + 128).reshape(128, 1)
        else:
            m["a2a_idx"] = (ar + 128).reshape(128, 1)
            m["a2b_idx"] = (ar + 2 * 128).reshape(128, 1)
            m["haloA_idx"] = (ar + 0).reshape(128, 1)
            m["haloB_idx"] = (ar + 2 * 128).reshape(128, 1)
        m.update(wmap)
        in_maps.append(m)

    res = run_bass_kernel_spmd(nc, in_maps, core_ids=list(range(8)))
    out = np.empty((B, L, Cc), dtype=np.float32)
    for core in range(8):
        b, half = core // 2, core % 2
        out[b, half * NT:(half + 1) * NT] = np.asarray(res.results[core]["out_my"])
    return out


# version 08
# speedup vs baseline: 1.1769x; 1.1769x over previous
import contextlib
import math

import numpy as np
import concourse.bass as bass
import concourse.mybir as mybir
from concourse.bass_utils import run_bass_kernel_spmd
from concourse.bass_core_rust import IndirectOffsetOnAxis

F32 = mybir.dt.float32
BF16 = mybir.dt.bfloat16
I32 = mybir.dt.int32
AF = mybir.ActivationFunctionType
ALU = mybir.AluOpType
AX = mybir.AxisListType

P = 128
C = 1024
NCH = 8
H = 16
HID = 4096
NHID = 32
MAGIC = 0x12345678
LN1E4 = math.log(10000.0)

ENGINES = ("sync", "vector", "scalar", "tensor", "gpsimd")


class Prog:

    def __init__(self, nc):
        self.nc = nc
        self.sems = {}
        self.queue = {e: [] for e in ENGINES}
        self.count = {}
        self.last_writer = {}
        self.readers = {}
        self.clock = {e: {} for e in ENGINES}
        self.es = None
        self.arena_of = {}
        self.dup = set()
        self.curpar = [None]
        self.recording = False
        self.rec = []

    def _resolve(self, base):
        if base in self.dup:
            par = self.curpar[0]
            assert par is not None, f"parity tensor {base} used outside tile context"
            return f"{base}~{par}"
        return base

    def init_sems(self, es):
        self.es = es
        for e in ENGINES:
            self.sems[e] = es.enter_context(self.nc.semaphore(f"d_{e}"))
            self.count[e] = 0

    def chan(self, name):
        if name not in self.sems:
            self.sems[name] = self.es.enter_context(self.nc.semaphore(name))
            self.count[name] = 0
        return self.sems[name]

    def _expand(self, keys):
        out = []
        for k in keys:
            if isinstance(k, tuple):
                base = self._resolve(k[0])
                k = (base,) + tuple(k[1:])
            else:
                base = self._resolve(k)
                k = base
            out.append(k)
            ar = self.arena_of.get(base)
            if ar is None:
                continue
            if isinstance(ar, (list, tuple)):
                for a in ar:
                    out.append(("@", a))
            else:
                out.append(("@", ar))
        return out

    def emit(self, engine, fn, reads=(), writes=(), dma=None):
        if self.recording:
            self.rec.append((engine, fn, tuple(reads), tuple(writes), dma,
                             self.curpar[0]))
            return None
        reads = self._expand(reads)
        writes = self._expand(writes)
        need = {}

        def _need(s, v):
            if v > need.get(s, 0):
                need[s] = v

        for k in reads:
            w = self.last_writer.get(k)
            if w is not None:
                _need(*w)
        for k in writes:
            w = self.last_writer.get(k)
            if w is not None:
                _need(*w)
            for s, v in self.readers.get(k, {}).items():
                _need(s, v)

        myclock = self.clock[engine]
        waits = []
        for s, v in sorted(need.items(), key=lambda kv: str(kv[0])):
            if myclock.get(s, 0) >= v:
                continue
            waits.append((s, v))
            myclock[s] = v

        if dma is None:
            sem_name, inc = engine, 1
        else:
            self.chan(dma)
            sem_name, inc = dma, 16
        self.count[sem_name] += inc
        cnt = self.count[sem_name]
        sems = self.sems
        sem = sems[sem_name]

        par = self.curpar[0]

        def run(eng):
            self.curpar[0] = par
            for (s, v) in waits:
                eng.wait_ge(sems[s], v)
            last = fn(eng)
            assert last is not None, f"emit fn on {engine} returned None"
            last.then_inc(sem, inc)

        self.queue[engine].append(run)
        for k in reads:
            self.readers.setdefault(k, {})[sem_name] = cnt
        for k in writes:
            self.last_writer[k] = (sem_name, cnt)
            self.readers[k] = {}
        return (sem_name, cnt)

    def barrier_on(self, engine, deps):
        sems = self.sems
        myclock = self.clock[engine]
        waits = []
        for s, v in deps:
            if myclock.get(s, 0) >= v:
                continue
            waits.append((s, v))
            myclock[s] = v

        def run(eng):
            for (s, v) in waits:
                eng.wait_ge(sems[s], v)

        self.queue[engine].append(run)

    def record(self):
        self.recording = True
        self.rec = []

    def stop_record(self):
        self.recording = False
        out = self.rec
        self.rec = []
        return out

    def commit(self, item):
        engine, fn, reads, writes, dma, par = item
        sav = self.curpar[0]
        self.curpar[0] = par
        self.emit(engine, fn, reads=reads, writes=writes, dma=dma)
        self.curpar[0] = sav

    def seal(self, block):
        for e in ENGINES:
            q = self.queue[e]
            if not q:
                continue
            section = getattr(block, e)

            def make(q=q):
                def _(eng):
                    for run in q:
                        run(eng)
                return _

            section(make())


def splits(s, e_, mx=512):
    o = s
    while o < e_:
        nn = min(mx, e_ - o)
        yield (o, nn)
        o += nn


def build_program(NT=4096, TT=512, dbg=None):
    HL = 4
    W = TT + 2 * HL
    NG = TT // P
    NTILE = NT // TT
    SEQ = 2 * NT
    T2, T4, TH, W2 = TT + 2, TT + 4, TT // 2, TT + 2 * HL - 2
    LNW = ((T4 + 31) & ~31) + 16
    assert TT % P == 0 and NT % TT == 0

    nc = bass.Bass("TRN2", target_bir_lowering=False, detect_race_conditions=False)

    x_my = nc.dram_tensor("x_my", [NT + 2 * HL, C], F32, kind="ExternalInput").ap()
    out_my = nc.dram_tensor("out_my", [NT, C], F32, kind="ExternalOutput").ap()
    wn = {}
    for nm, shp in [
        ("cpe1_w", [C, 3]), ("cpe1_b", [C]), ("norm1_g", [C]), ("norm1_b", [C]),
        ("in_w", [C, C]), ("in_b", [C]), ("actp_w", [C, C]), ("actp_b", [C]),
        ("dwc_w", [C, 3]), ("dwc_b", [C]), ("qk_w", [C, 2 * C]), ("qk_b", [2 * C]),
        ("lepe_w", [C, 3]), ("lepe_b", [C]), ("out_w", [C, C]), ("out_b", [C]),
        ("cpe2_w", [C, 3]), ("cpe2_b", [C]), ("norm2_g", [C]), ("norm2_b", [C]),
        ("fc1_w", [C, HID]), ("fc1_b", [HID]), ("fc2_w", [HID, C]), ("fc2_b", [C]),
        ("fWin", [C, 64]), ("fbin", [64]), ("fW1", [64, 16]), ("fb1", [64, 16]),
        ("fW2", [64, 16]), ("fb2", [64]), ("fWout", [64, C]), ("fbout", [C]),
    ]:
        wn[nm] = nc.dram_tensor(nm, list(shp), F32, kind="ExternalInput").ap()
    idx_in = {}
    for nm, n in (("kv_idx", 65), ("a2a_idx", P), ("a2b_idx", P), ("fkv_idx", 2),
                  ("fa2_idx", 2), ("haloA_idx", P), ("haloB_idx", P)):
        idx_in[nm] = nc.dram_tensor(nm, [n, 1], I32, kind="ExternalInput").ap()
    lrmask_in = nc.dram_tensor("lrmask", [P, 2], F32, kind="ExternalInput").ap()

    a1sp = nc.dram_tensor("a1sp", [NTILE, P, NCH, TT], F32).ap()
    a2sp = nc.dram_tensor("a2sp", [NTILE, P, NCH, TT], F32).ap()
    y0sp = nc.dram_tensor("y0sp", [NT, C], F32).ap()
    qrsp = nc.dram_tensor("qrsp", [NTILE, P, NCH, TT], BF16).ap()
    arsp = nc.dram_tensor("arsp", [NTILE, P, NCH, TT], BF16).ap()
    lpsp = nc.dram_tensor("lpsp", [NTILE, P, NCH, TT], BF16).ap()
    hrow = nc.dram_tensor("hrow", [3 * P, NCH], F32).ap()
    fc1bf = nc.dram_tensor("fc1bf", [NHID, P, NCH, P], BF16).ap()
    fc2bf = nc.dram_tensor("fc2bf", [NHID, P, C], BF16).ap()
    qkbf = nc.dram_tensor("qkbf", [16, P, NCH, P], BF16).ap()
    kmb = nc.dram_tensor("kmb", [C], F32).ap()
    S_kv = nc.dram_tensor("S_kv", [130, C], F32, addr_space="Shared").ap()
    S_a2 = nc.dram_tensor("S_a2", [3 * P, NCH], F32, addr_space="Shared").ap()
    S_fl = nc.dram_tensor("S_fl", [8, 16], I32, addr_space="Shared").ap()

    es = contextlib.ExitStack()
    BUDGET = 207 * 1024
    arena = nc.alloc_sbuf_tensor("ARENA", [P, BUDGET // 4], F32)
    abase = nc.lookup_mloc(arena).addr
    cur = [0]
    tens = {}
    arena_of = {}
    groups = {}

    def _nbytes(shape, dtype):
        esz = {F32: 4, BF16: 2, I32: 4}[dtype]
        n = esz
        for s in shape[1:]:
            n *= s
        return (n + 63) & ~63

    def group(gname, nbytes):
        nbytes = (nbytes + 63) & ~63
        groups[gname] = (abase + cur[0], nbytes)
        cur[0] += nbytes
        assert cur[0] <= BUDGET, f"SBUF over budget at group {gname}: {cur[0]}"

    def alloc(name, shape, dtype, g=None, suboff=0):
        n = _nbytes(shape, dtype)
        if g is None:
            addr = abase + cur[0]
            cur[0] += n
            assert cur[0] <= BUDGET, f"SBUF over budget at {name}: {cur[0]}"
            arena_of[name] = name
        else:
            gaddr, gsize = groups[g]
            assert suboff + n <= gsize + 63, f"{name} exceeds group {g}"
            addr = gaddr + suboff
            arena_of[name] = g
        tens[name] = nc.alloc_sbuf_tensor_at(name, list(shape), dtype, offset=addr)
        return tens[name]

    DUPG = []

    def dgroup(gname, nbytes):
        DUPG.append(gname)
        group(gname + "~0", nbytes)
        group(gname + "~1", nbytes)

    dgroup("GA", max(NCH * W * 4, 16 * TT * 2))
    dgroup("GB", max(NCH * (TT + 2) * 4, 16 * TT * 2))
    dgroup("GC", NG * C * 4)
    dgroup("G1", NCH * W * 2)
    dgroup("G2", NCH * T4 * 2)
    dgroup("G3", NCH * T4 * 2)
    dgroup("G4", max(NCH * T4 * 2, 16 * TH * 2))
    dgroup("GLN", (3 * LNW + max(LNW, 512)) * 4)
    dgroup("GKV", 4 * TT * 4)
    dgroup("GSC", 2 * TT * 2)
    group("GR1", 8 * 1024)

    DUPN = set()

    def dalloc(name, shape, dtype, g=None, suboff=0):
        DUPN.add(name)
        for par in range(2):
            gg = (g + f"~{par}") if g is not None and g in DUPG else g
            alloc(f"{name}~{par}", shape, dtype, gg, suboff)

    dalloc("a1f", [P, NCH, W], F32)
    alloc("w_in", [P, NCH, C], BF16)
    alloc("w_actp", [P, NCH, C], BF16)
    alloc("ident_b", [P, P], BF16)
    alloc("ident_f", [P, P], F32)
    alloc("maskU", [P, P], BF16)
    alloc("maskL", [P, P], BF16)
    alloc("Rm", [P, NCH, P], BF16)
    alloc("ones_row", [1, 512], BF16)
    alloc("zrow", [1, P], BF16)
    alloc("ones_col", [P, 1], BF16)
    alloc("invC_col", [1, P], F32)
    alloc("taps", [P, 4, NCH, 4], F32)
    alloc("cst", [P, 40], F32)
    alloc("crope_col", [P, NCH], F32)
    alloc("halfpi", [P, 1], F32)
    alloc("eps5", [P, 1], F32)
    alloc("magic", [2, 16], I32)
    alloc("fbuf", [1, 64], I32)
    alloc("idx", [P, 8], I32)
    alloc("lrmask", [P, 2], F32)
    alloc("zero8", [P, 3, NCH], F32)
    alloc("g_cols", [P, 2, NCH], F32)
    alloc("b_cols", [P, 2, NCH], F32)
    alloc("b_colsb", [P, 2, NCH], BF16)
    alloc("rows2", [1, 2 * C], BF16)
    alloc("binrow", [1, 64], BF16)
    alloc("ffn_c", [P, 52], F32)
    alloc("w_Win", [P, NCH, 64], BF16)
    alloc("w_Wout", [P, C], BF16)
    dalloc("s2g2", [P, 2, TH], BF16)
    alloc("usq", [P, C], BF16)
    dalloc("stats", [P, 16], F32)
    dalloc("red1", [P, 1], F32)
    alloc("ksum", [P, NCH], F32)
    alloc("kv_raw", [65, C], F32)
    alloc("kv_cmb", [P, C], F32)
    alloc("kv_bd", [P, NCH, P], BF16)
    alloc("kmr_bd", [P, NCH, 16], BF16)
    alloc("km_col", [P, NCH], F32)
    alloc("km_bf", [P, NCH], BF16)
    alloc("a2keep", [P, 2, NCH], F32)
    alloc("a2rd", [P, 2, NCH], F32)
    alloc("a2eg", [P, NTILE, NCH, 2], F32)
    alloc("hpar", [P, 2, NCH], F32)
    dalloc("evs", [P, 2, 512], F32)
    dalloc("u_fm", [P, NCH, W], F32, "GA")
    dalloc("xi0", [P, NCH, T4], BF16, "GA")
    dalloc("qke", [P, 16, TT], BF16, "GA")
    dalloc("a1l", [P, NCH, TT], F32, "GA")
    dalloc("y0l", [P, NG, C], F32, "GA")
    dalloc("u_tm", [P, NG, C], F32, "GB")
    dalloc("y0t", [P, NG, C], F32, "GB")
    dalloc("krtvh", [P, 2, NG, C], BF16, "GB")
    dalloc("a2w", [P, NCH, TT], F32, "GB")
    dalloc("a2m", [P, NCH, TT], F32, "GB")
    dalloc("g1", [P, 16, TT], BF16, "GB")
    dalloc("x0", [P, NG, C], F32, "GC")
    dalloc("qro", [P, NCH, TT], BF16, "GC")
    dalloc("kro", [P, NCH, TT], BF16, "GC", suboff=NCH * TT * 2)
    dalloc("a3", [P, NCH, TT], F32, "GC")
    dalloc("u_bf", [P, NCH, W], BF16, "G1")
    dalloc("xed", [P, C], F32, "G1")
    dalloc("cscr", [P, NCH, W2], BF16, "G1")
    dalloc("lepe_sb", [P, NCH, TT], BF16, "G1")
    dalloc("qr2", [P, NCH, TT], BF16, "G1")
    dalloc("cvs", [P, NCH, TT], BF16, "G1")
    dalloc("a1bf", [P, NCH, T4], BF16, "G2")
    dalloc("xi", [P, NCH, T2], BF16, "G2")
    dalloc("ar2", [P, NCH, TT], BF16, "G2")
    dalloc("a3bf", [P, NCH, TT], BF16, "G2")
    dalloc("sqbf", [P, NCH, T4], BF16, "G3")
    dalloc("acr", [P, NCH, TT], BF16, "G3")
    dalloc("lp2", [P, NCH, TT], BF16, "G3")
    dalloc("hscr", [P, 16, TH], BF16, "G4")
    dalloc("xn1", [P, NCH, T4], BF16, "G4")
    dalloc("aab", [P, NCH, TT], BF16, "G4")
    dalloc("xn2", [P, NCH, TT], BF16, "G4")
    dalloc("mur", [P, LNW], F32, "GLN")
    dalloc("nmu2", [P, LNW], F32, "GLN", suboff=LNW * 4)
    dalloc("rstd", [P, LNW], F32, "GLN", suboff=2 * LNW * 4)
    dalloc("lscr", [P, max(LNW, 512)], F32, "GLN", suboff=3 * LNW * 4)
    dalloc("srow", [1, 2, LNW], F32, "GLN")
    dalloc("zsh", [16, TT], F32, "GKV")
    dalloc("zscr", [P, TT], F32, "GKV", suboff=TT * 4)
    dalloc("zbf", [16, TT], BF16, "GKV", suboff=2 * TT * 4)
    dalloc("zae", [P, TT], BF16, "GKV", suboff=2 * TT * 4 + TT * 2)
    dalloc("lab", [P, TT], BF16, "GKV", suboff=3 * TT * 4)
    dalloc("scab", [P, 2, TT], BF16, "GSC")
    alloc("r_qkb", [1, 2 * C], BF16, "GR1")
    alloc("r_inb", [1, C], BF16, "GR1", suboff=2 * C * 2)
    alloc("r_actb", [1, C], BF16, "GR1", suboff=3 * C * 2)
    alloc("r_fc1b", [1, HID], BF16, "GR1")
    alloc("selz", [16, NCH * P], BF16, "GR1", suboff=3 * C * 2)
    a1f_addr = nc.lookup_mloc(tens["a1f~0"]).addr
    tens["w_out"] = nc.alloc_sbuf_tensor_at("w_out", [P, NCH, C], BF16, offset=a1f_addr)
    arena_of["w_out"] = ["a1f~0", "a1f~1"]
    gc_addr = groups["GC~0"][0]
    tens["w_stage"] = nc.alloc_sbuf_tensor_at("w_stage", [P, 4, C], F32, offset=gc_addr)
    arena_of["w_stage"] = ["GC~0", "GC~1"]
    w_in_addr = nc.lookup_mloc(tens["w_in"]).addr
    tens["w1s"] = nc.alloc_sbuf_tensor_at("w1s", [P, 4, NCH, P], BF16, offset=w_in_addr)
    tens["w2s"] = nc.alloc_sbuf_tensor_at("w2s", [P, 4, C], BF16,
                                          offset=w_in_addr + 4 * NCH * P * 2)
    alloc("w3s", [P, 4, NCH, P], BF16)
    arena_of["w1s"] = "WP1"
    arena_of["w2s"] = "WP1"
    for nm2 in ("w_in", "w_actp"):
        arena_of[nm2] = "WP1"
    used_kb = cur[0] / 1024

    pw = [es.enter_context(nc.psum_tensor(f"pw{i}", [P, 512], F32)) for i in range(4)]
    pt4 = es.enter_context(nc.psum_tensor("pt4", [P, 1024], BF16))
    pt5 = es.enter_context(nc.psum_tensor("pt5", [P, 1024], BF16))
    pkv = es.enter_context(nc.psum_tensor("pkv", [P, 1024], F32))
    for i in range(4):
        arena_of[f"pw{i}"] = f"pw{i}"
    arena_of["pkv"] = "pkv"
    arena_of["pt4"] = "pt4"
    arena_of["pt5"] = "pt5"

    p = Prog(nc)
    p.arena_of = arena_of
    p.dup = DUPN
    p.init_sems(es)
    poll1 = es.enter_context(nc.semaphore("poll1"))
    poll2 = es.enter_context(nc.semaphore("poll2"))

    t = lambda nm: tens[p._resolve(nm)]
    p.curpar[0] = 0
    PB = lambda i: f"pw{i}"

    for nm, b in (("ident_b", 0), ("ident_f", 0), ("maskU", 1), ("maskL", -1)):
        tt = t(nm)
        p.emit("gpsimd", lambda e, tt=tt: e.memset(tt[:, :], 0.0), writes=[nm])
        p.emit("gpsimd", lambda e, tt=tt, b=b: e.affine_select(
            out=tt[:, :], in_=tt[:, :], compare_op=ALU.not_equal, fill=1.0,
            base=b, pattern=[[-1, P]], channel_multiplier=1),
            reads=[nm], writes=[nm])
    for nm, val in (("ones_row", 1.0), ("ones_col", 1.0), ("zrow", 0.0), ("invC_col", 1.0 / C),
                    ("halfpi", float(np.pi / 2)), ("zero8", 0.0), ("ksum", 0.0),
                    ("eps5", 1e-5)):
        tt = t(nm)
        p.emit("vector", lambda e, tt=tt, val=val: e.memset(tt[:], val), writes=[nm])
    p.emit("vector", lambda e: e.memset(t("magic")[:, :], MAGIC), writes=["magic"])
    p.emit("vector", lambda e: e.memset(t("fbuf")[:, :], 0), writes=["fbuf"])
    ix = t("idx")
    for col, (nm, n) in enumerate((("kv_idx", 65), ("a2a_idx", P), ("fkv_idx", 2),
                                   ("fa2_idx", 2), ("haloA_idx", P), ("haloB_idx", P),
                                   ("a2b_idx", P))):
        p.emit("sync", lambda e, col=col, nm=nm, n=n: e.dma_start(
            out=ix[0:n, col:col + 1], in_=idx_in[nm]), writes=[("idx", col)], dma="c_pr")
    p.emit("sync", lambda e: e.dma_start(out=t("lrmask")[:, :], in_=lrmask_in),
           writes=["lrmask"], dma="c_pr")

    cs = t("cst")
    p.emit("gpsimd", lambda e: e.memset(t("mur")[:, 0:64], 0.0), writes=[("mur", "pp")])
    p.emit("gpsimd", lambda e: e.affine_select(
        out=t("mur")[:, 0:64], in_=t("mur")[:, 0:64], compare_op=ALU.not_equal,
        fill=1.0, base=0, pattern=[[-2, 64]], channel_multiplier=1),
        reads=[("mur", "pp")], writes=[("mur", "pp")])
    p.emit("vector", lambda e: e.reduce_sum(out=cs[:, 17:18], in_=t("mur")[:, 0:64], axis=AX.X),
           reads=[("mur", "pp")], writes=[("cst", "even")])
    p.emit("vector", lambda e: e.tensor_scalar(out=cs[:, 16:17], in0=cs[:, 17:18],
                                               scalar1=-2.0, scalar2=1.0, op0=ALU.mult, op1=ALU.add),
           reads=[("cst", "even")], writes=[("cst", "sgn")])
    p.emit("gpsimd", lambda e: e.iota(cs[:, 18:19], pattern=[[0, 1]], base=0,
                                      channel_multiplier=1, allow_small_or_imprecise_dtypes=True),
           writes=[("cst", "p")])
    p.emit("vector", lambda e: e.tensor_scalar(out=cs[:, 19:20], in0=cs[:, 18:19],
                                               scalar1=0.5, scalar2=-0.25, op0=ALU.mult, op1=ALU.add),
           reads=[("cst", "p")], writes=[("cst", "fl0")])
    p.emit("vector", lambda e: e.scalar_tensor_tensor(out=cs[:, 20:21], in0=cs[:, 16:17],
                                                      scalar=-0.25, in1=cs[:, 19:20],
                                                      op0=ALU.mult, op1=ALU.add),
           reads=[("cst", "sgn"), ("cst", "fl0")], writes=[("cst", "fl")])
    p.emit("gpsimd", lambda e: e.iota(cs[:, 21:29], pattern=[[64, NCH]], base=0,
                                      channel_multiplier=0, allow_small_or_imprecise_dtypes=True),
           writes=[("cst", "jb")])
    p.emit("vector", lambda e: e.tensor_scalar(out=cs[:, 21:29], in0=cs[:, 21:29],
                                               scalar1=cs[:, 20:21], scalar2=None, op0=ALU.add),
           reads=[("cst", "jb"), ("cst", "fl")], writes=[("cst", "jm")])
    p.emit("scalar", lambda e: e.activation(out=cs[:, 29:37], in_=cs[:, 21:29],
                                            func=AF.Exp, scale=-LN1E4 / 512.0),
           reads=[("cst", "jm")], writes=[("cst", "th")])
    p.emit("scalar", lambda e: e.activation(out=cs[:, 0:8], in_=cs[:, 29:37],
                                            func=AF.Sin, bias=t("halfpi")[:, 0:1]),
           reads=[("cst", "th"), "halfpi"], writes=[("cst", "cos")])
    p.emit("scalar", lambda e: e.activation(out=cs[:, 8:16], in_=cs[:, 29:37], func=AF.Sin),
           reads=[("cst", "th")], writes=[("cst", "sin")])
    p.emit("vector", lambda e: e.scalar_tensor_tensor(out=t("crope_col")[:, :], in0=cs[:, 8:16],
                                                      scalar=cs[:, 16:17], in1=cs[:, 0:8],
                                                      op0=ALU.mult, op1=ALU.add),
           reads=[("cst", "cos"), ("cst", "sin"), ("cst", "sgn")], writes=["crope_col"])
    mr = t("mur")
    p.emit("vector", lambda e: e.tensor_scalar(out=mr[:, 64:72], in0=cs[:, 8:16],
                                               scalar1=cs[:, 16:17], scalar2=-0.5,
                                               op0=ALU.mult, op1=ALU.mult),
           reads=[("cst", "sin"), ("cst", "sgn")], writes=[("mur", "ss")])
    p.emit("vector", lambda e: e.scalar_tensor_tensor(out=mr[:, 72:80], in0=cs[:, 8:16],
                                                      scalar=0.5, in1=mr[:, 64:72],
                                                      op0=ALU.mult, op1=ALU.add),
           reads=[("cst", "sin"), ("mur", "ss")], writes=[("mur", "su")])
    p.emit("vector", lambda e: e.scalar_tensor_tensor(out=mr[:, 80:88], in0=cs[:, 8:16],
                                                      scalar=-0.5, in1=mr[:, 64:72],
                                                      op0=ALU.mult, op1=ALU.add),
           reads=[("cst", "sin"), ("mur", "ss")], writes=[("mur", "so")])
    Rm = t("Rm")
    for cc in range(NCH):
        p.emit("vector", lambda e, cc=cc: e.tensor_scalar(
            out=Rm[:, cc, :], in0=t("ident_b")[:, :], scalar1=cs[:, cc:cc + 1],
            scalar2=None, op0=ALU.mult),
            reads=["ident_b", ("cst", "cos")], writes=[("Rm", cc)])
        p.emit("vector", lambda e, cc=cc: e.scalar_tensor_tensor(
            out=Rm[:, cc, :], in0=t("maskU")[:, :], scalar=mr[:, 72 + cc:73 + cc],
            in1=Rm[:, cc, :], op0=ALU.mult, op1=ALU.add),
            reads=["maskU", ("mur", "su"), ("Rm", cc)], writes=[("Rm", cc)])
        p.emit("vector", lambda e, cc=cc: e.scalar_tensor_tensor(
            out=Rm[:, cc, :], in0=t("maskL")[:, :], scalar=mr[:, 80 + cc:81 + cc],
            in1=Rm[:, cc, :], op0=ALU.mult, op1=ALU.add),
            reads=["maskL", ("mur", "so"), ("Rm", cc)], writes=[("Rm", cc)])
    taps = t("taps")
    for i, (wnm, bnm, plus1) in enumerate((("cpe1_w", "cpe1_b", True),
                                           ("dwc_w", "dwc_b", False),
                                           ("lepe_w", "lepe_b", False),
                                           ("cpe2_w", "cpe2_b", True))):
        p.emit("sync", lambda e, i=i, wnm=wnm: e.dma_start(
            out=taps[:, i, :, 0:3], in_=wn[wnm].rearrange("(cc p) k -> p cc k", p=P)),
            writes=[("taps", i, "w")], dma="c_pr")
        p.emit("sync", lambda e, i=i, bnm=bnm: e.dma_start(
            out=taps[:, i, :, 3:4], in_=wn[bnm].rearrange("(cc p) -> p cc", p=P)[:, :, None],
            allow_slow_non_contiguous=True),
            writes=[("taps", i, "b")], dma="c_pr")
        if plus1:
            p.emit("vector", lambda e, i=i: e.tensor_scalar(
                out=taps[:, i, :, 1:2], in0=taps[:, i, :, 1:2], scalar1=1.0,
                scalar2=None, op0=ALU.add),
                reads=[("taps", i, "w")], writes=[("taps", i, "w")])
    for col, nm in ((0, "norm1_g"), (1, "norm2_g")):
        p.emit("sync", lambda e, col=col, nm=nm: e.dma_start(
            out=t("g_cols")[:, col, :], in_=wn[nm].rearrange("(cc p) -> p cc", p=P),
            allow_slow_non_contiguous=True),
            writes=[("g_cols", col)], dma="c_pr")
    for col, nm in ((0, "norm1_b"), (1, "norm2_b")):
        p.emit("sync", lambda e, col=col, nm=nm: e.dma_start(
            out=t("b_cols")[:, col, :], in_=wn[nm].rearrange("(cc p) -> p cc", p=P),
            allow_slow_non_contiguous=True),
            writes=[("b_cols", col)], dma="c_pr")
    p.emit("vector", lambda e: e.tensor_copy(out=t("b_colsb")[:, :, :], in_=t("b_cols")[:, :, :]),
           reads=[("b_cols", 0), ("b_cols", 1)], writes=["b_colsb"])

    fcn = t("ffn_c")
    for sl0, nm in ((0, "fW1"), (32, "fW2")):
        for hh in range(2):
            p.emit("sync", lambda e, sl0=sl0, nm=nm, hh=hh: e.dma_start(
                out=fcn[hh * 64:(hh + 1) * 64, sl0:sl0 + 16], in_=wn[nm]),
                writes=[("ffn_c", nm, hh)], dma="c_pr")
    for hh in range(2):
        p.emit("sync", lambda e, hh=hh: e.dma_start(
            out=fcn[hh * 64:(hh + 1) * 64, 16:32], in_=wn["fb1"]),
            writes=[("ffn_c", "fb1", hh)], dma="c_pr")
    stg = t("w_stage").rearrange("p a b -> p (a b)")
    p.emit("sync", lambda e: e.dma_start(
        out=stg[:, 0:512].rearrange("p (cc m) -> p cc m", m=64),
        in_=wn["fWin"].rearrange("(cc p) m -> p cc m", p=P)),
        writes=[("w_stage", "win")], dma="c_pr")
    p.emit("vector", lambda e: e.tensor_copy(
        out=t("w_Win").rearrange("p cc m -> p (cc m)"), in_=stg[:, 0:512]),
        reads=[("w_stage", "win")], writes=["w_Win"])
    p.emit("sync", lambda e: e.dma_start(out=stg[0:64, C:2 * C], in_=wn["fWout"]),
           writes=[("w_stage", "wout")], dma="c_pr")
    p.emit("vector", lambda e: e.tensor_copy(out=t("w_Wout")[0:64, :], in_=stg[0:64, C:2 * C]),
           reads=[("w_stage", "wout")], writes=["w_Wout"])
    p.emit("gpsimd", lambda e: e.dma_start(out=t("w_Wout")[64:128, :], in_=t("w_Wout")[0:64, :]),
           reads=["w_Wout"], writes=[("w_Wout", "hi")], dma="c_wodup")
    p.emit("sync", lambda e: e.dma_start(out=stg[0:64, 2 * C:2 * C + 1], in_=wn["fb2"][:, None]),
           writes=[("w_stage", "fb2")], dma="c_pr")
    p.emit("vector", lambda e: e.tensor_copy(out=t("scab")[0:64, 0, 0:1], in_=stg[0:64, 2 * C:2 * C + 1]),
           reads=[("w_stage", "fb2")], writes=[("scab", "fb2")])
    def mm_bout(e):
        e.matmul(pw[0][0:1, 0:512], t("scab")[0:64, 0, 0:1], t("w_Wout")[0:64, 0:512],
                 start=True, stop=True)
        return e.matmul(pw[1][0:1, 0:512], t("scab")[0:64, 0, 0:1], t("w_Wout")[0:64, 512:1024],
                        start=True, stop=True)
    p.emit("tensor", mm_bout, reads=[("scab", "fb2"), "w_Wout"], writes=[(PB(0),), (PB(1),)])
    p.emit("sync", lambda e: e.dma_start(out=stg[0:1, 3 * C:4 * C], in_=wn["fbout"][None, :]),
           writes=[("w_stage", "bout0")], dma="c_pr")
    for half in range(2):
        p.emit("vector", lambda e, half=half: e.tensor_tensor(
            out=t("lscr")[0:1, 0:512], in0=pw[half][0:1, 0:512],
            in1=stg[0:1, 3 * C + half * 512:3 * C + (half + 1) * 512], op=ALU.add),
            reads=[(PB(half),), ("w_stage", "bout0")], writes=[("lscr", "bo")])
        p.emit("vector", lambda e, half=half: e.tensor_copy(
            out=t("rows2")[0:1, half * 512:(half + 1) * 512],
            in_=t("lscr")[0:1, 0:512]),
            reads=[("lscr", "bo")], writes=[("rows2", "bout")])
    p.emit("sync", lambda e: e.dma_start(out=stg[0:1, 2 * C:2 * C + 64], in_=wn["fbin"][None, :]),
           writes=[("w_stage", "bin")], dma="c_pr")
    p.emit("vector", lambda e: e.tensor_copy(out=t("binrow")[0:1, :],
                                             in_=stg[0:1, 2 * C:2 * C + 64]),
           reads=[("w_stage", "bin")], writes=["binrow"])

    def load_w1(wname, dst, ncols, gfold, bcol, b0name, rowdst, rowkey):
        qkeflat = t("qke").rearrange("p a b -> p (a b)")
        for cc in range(NCH):
            p.emit("sync", lambda e, cc=cc: e.dma_start(
                out=stg[:, 0:ncols], in_=wn[wname][cc * P:(cc + 1) * P, :]),
                writes=[("w_stage", wname, cc)], dma="c_wl")
            if gfold is not None:
                p.emit("vector", lambda e, cc=cc: e.tensor_scalar(
                    out=dst[:, cc, :], in0=stg[:, 0:ncols],
                    scalar1=t("g_cols")[:, gfold, cc:cc + 1], scalar2=None, op0=ALU.mult),
                    reads=[("w_stage", wname, cc), ("g_cols", gfold)], writes=[(wname, cc)])
            else:
                p.emit("vector", lambda e, cc=cc: e.tensor_copy(out=dst[:, cc, :], in_=stg[:, 0:ncols]),
                       reads=[("w_stage", wname, cc)], writes=[(wname, cc)])
            if bcol is not None:
                p.emit("vector", lambda e, cc=cc: e.tensor_copy(
                    out=qkeflat[:, 0:ncols], in_=stg[:, 0:ncols]),
                    reads=[("w_stage", wname, cc)], writes=[("qke", "stagebf")])
                def mmb(e, cc=cc):
                    last = None
                    for oi, (o, nn) in enumerate(splits(0, ncols)):
                        bank = pw[2] if oi == 0 else pw[3]
                        last = e.matmul(bank[0:1, 0:nn], t("b_colsb")[:, bcol, cc:cc + 1],
                                        qkeflat[:, o:o + nn],
                                        start=(cc == 0), stop=(cc == NCH - 1))
                    return last
                p.emit("tensor", mmb, reads=[("qke", "stagebf"), "b_colsb"],
                       writes=[(PB(2),), (PB(3),)])
        if bcol is not None:
            p.emit("sync", lambda e: e.dma_start(out=stg[0:1, 3 * C:3 * C + ncols],
                                                 in_=wn[b0name][None, :]),
                   writes=[("w_stage", "b0w")], dma="c_pr")
            for oi, (o, nn) in enumerate(splits(0, ncols)):
                bank = pw[2] if oi == 0 else pw[3]
                p.emit("vector", lambda e, o=o, nn=nn, bank=bank: e.tensor_tensor(
                    out=t("lscr")[0:1, 0:nn], in0=bank[0:1, 0:nn],
                    in1=stg[0:1, 3 * C + o:3 * C + o + nn], op=ALU.add),
                    reads=[(PB(2),), (PB(3),), ("w_stage", "b0w")], writes=[("lscr", "bw")])
                p.emit("vector", lambda e, o=o, nn=nn: e.tensor_copy(
                    out=rowdst[0:1, o:o + nn], in_=t("lscr")[0:1, 0:nn]),
                    reads=[("lscr", "bw")], writes=[rowkey])

    load_w1("in_w", t("w_in"), C, 0, 0, "in_b", t("r_inb"), ("r_inb",))
    load_w1("actp_w", t("w_actp"), C, 0, 0, "actp_b", t("r_actb"), ("r_actb",))
    qkef = t("qke").rearrange("p a b -> p (a b)")
    for cc in range(NCH):
        p.emit("sync", lambda e, cc=cc: e.dma_start(
            out=stg[:, 0:2 * C], in_=wn["qk_w"][cc * P:(cc + 1) * P, :]),
            writes=[("w_stage", "qk", cc)], dma="c_wl")
        p.emit("vector", lambda e, cc=cc: e.tensor_copy(out=qkef[:, 0:2 * C], in_=stg[:, 0:2 * C]),
               reads=[("w_stage", "qk", cc)], writes=[("qke", "stagebf")])
        p.emit("sync", lambda e, cc=cc: e.dma_start(
            out=qkbf[:, :, cc, :].rearrange("j p m -> p j m"),
            in_=qkef[:, 0:2 * C].rearrange("p (j m) -> p j m", m=P)),
            reads=[("qke", "stagebf")], writes=[("qkbf", cc)], dma="c_wlo")
    p.emit("sync", lambda e: e.dma_start(out=stg[0:1, 2 * C:4 * C], in_=wn["qk_b"][None, :]),
           writes=[("w_stage", "qkb")], dma="c_pr")
    p.emit("vector", lambda e: e.tensor_copy(out=t("r_qkb")[0:1, :], in_=stg[0:1, 2 * C:4 * C]),
           reads=[("w_stage", "qkb")], writes=[("r_qkb",)])

    def ln_fm(a_bf, abf_keys, sqk, nco, a_f, acol, af_keys, xnt, xn_name, b0):
        assert nco <= 512
        b1 = b0 + 1
        def mmsum(e):
            last = None
            for cc in range(NCH):
                last = e.matmul(pw[b1][0:1, 0:nco], t("ones_col")[:, :], a_bf[:, cc, 0:nco],
                                start=(cc == 0), stop=(cc == NCH - 1))
            return last
        p.emit("tensor", mmsum, reads=abf_keys + ["ones_col"], writes=[(PB(b1),)])
        def mmsq(e):
            last = None
            for cc in range(NCH):
                last = e.matmul(pw[b1][32:33, 0:nco], t("ones_col")[:, :],
                                t("sqbf")[:, cc, 0:nco],
                                start=(cc == 0), stop=(cc == NCH - 1))
            return last
        p.emit("tensor", mmsq, reads=[sqk, "ones_col"], writes=[(PB(b1), "q")])
        p.emit("scalar", lambda e: e.activation(
            out=t("srow")[0:1, 0, 0:nco], in_=pw[b1][0:1, 0:nco], func=AF.Identity),
            reads=[(PB(b1),), (PB(b1), "q")], writes=[("srow", 0)])
        p.emit("scalar", lambda e: e.activation(
            out=t("srow")[0:1, 1, 0:nco], in_=pw[b1][32:33, 0:nco], func=AF.Identity),
            reads=[(PB(b1), "q")], writes=[("srow", 1)])
        def mmrep(e):
            e.matmul(pw[b0][:, 0:nco], t("invC_col")[0:1, :],
                     t("srow")[0:1, 0, 0:nco], start=True, stop=True)
            return e.matmul(pw[b1][:, 0:nco], t("invC_col")[0:1, :],
                            t("srow")[0:1, 1, 0:nco], start=True, stop=True)
        p.emit("tensor", mmrep, reads=[("srow", 0), ("srow", 1), "invC_col"],
               writes=[(PB(b0),), (PB(b1), "E")])
        p.emit("scalar", lambda e: e.activation(
            out=t("mur")[:, 0:nco], in_=pw[b0][:, 0:nco], func=AF.Identity),
            reads=[(PB(b0),)], writes=[("mur", "v")])
        p.emit("gpsimd", lambda e: e.tensor_tensor(
            out=t("nmu2")[:, 0:nco], in0=t("mur")[:, 0:nco],
            in1=t("mur")[:, 0:nco], op=ALU.mult),
            reads=[("mur", "v")], writes=["nmu2"])
        p.emit("scalar", lambda e: e.activation(
            out=t("rstd")[:, 0:nco], in_=pw[b1][:, 0:nco], func=AF.Identity,
            bias=t("eps5")[:, 0:1]),
            reads=[(PB(b1), "E"), "eps5"], writes=[("rstd", "E")])
        p.emit("gpsimd", lambda e: e.tensor_tensor(
            out=t("rstd")[:, 0:nco], in0=t("rstd")[:, 0:nco],
            in1=t("nmu2")[:, 0:nco], op=ALU.subtract),
            reads=[("rstd", "E"), "nmu2"], writes=[("rstd", "v")])
        p.emit("vector", lambda e: e.reciprocal(out=t("rstd")[:, 0:nco], in_=t("rstd")[:, 0:nco]),
               reads=[("rstd", "v")], writes=[("rstd", "v")])
        p.emit("scalar", lambda e: e.activation(out=t("rstd")[:, 0:nco], in_=t("rstd")[:, 0:nco],
                                                func=AF.Sqrt),
               reads=[("rstd", "v")], writes=[("rstd", "v")])
        for cc in range(NCH):
            p.emit("vector", lambda e, cc=cc: e.tensor_tensor(
                out=t("lscr")[:, 0:nco], in0=a_f[:, cc, acol:acol + nco],
                in1=t("mur")[:, 0:nco], op=ALU.subtract),
                reads=af_keys + [("mur", "v")], writes=[("lscr", "x")])
            p.emit("vector", lambda e, cc=cc: e.tensor_tensor(
                out=xnt[:, cc, 0:nco], in0=t("lscr")[:, 0:nco],
                in1=t("rstd")[:, 0:nco], op=ALU.mult),
                reads=[("lscr", "x"), ("rstd", "v")], writes=[(xn_name, cc)])

    def pkv_open(e):
        e.matmul(pkv[:, 0:512], t("zrow")[0:1, :], t("ones_row")[0:1, 0:512],
                 start=True, stop=False)
        return e.matmul(pkv[:, 512:1024], t("zrow")[0:1, :], t("ones_row")[0:1, 0:512],
                        start=True, stop=False)
    p.emit("tensor", pkv_open, reads=["zrow", "ones_row"], writes=[("pkv",)])

    rot2 = {0: 0, 1: 0}

    def nbank(par):
        rot2[par] = 1 - rot2[par]
        return 2 * par + rot2[par]

    def pass1_head(ti):
        p.curpar[0] = ti % 2
        par = ti % 2
        PB0, PB1 = 2 * par, 2 * par + 1
        t0 = ti * TT
        x0, u_tm, u_fm = t("x0"), t("u_tm"), t("u_fm")
        xed = t("xed")
        st = t("stats")
        p.emit("sync", lambda e: e.dma_start(
            out=x0[:, :, :], in_=x_my[t0 + HL:t0 + HL + TT, :].rearrange("(g p) c -> p g c", p=P)),
            writes=["x0"], dma=f"c_x0{ti % 2}")
        p.emit("sync", lambda e: e.dma_start(
            out=xed[0:4, :], in_=x_my[t0:t0 + HL, :]),
            writes=[("xed", "el")], dma=f"c_xe{ti % 2}")
        p.emit("sync", lambda e: e.dma_start(
            out=xed[4:8, :], in_=x_my[t0 + HL + TT:t0 + 2 * HL + TT, :]),
            writes=[("xed", "er")], dma=f"c_xe{ti % 2}")
        edk = [("xed", "el"), ("xed", "er")]

        for g in range(NG):
            p.emit("vector", lambda e, g=g: e.reduce_sum(out=st[:, g:g + 1], in_=x0[:, g, :], axis=AX.X),
                   reads=["x0"], writes=[("stats", "s", g)])
            p.emit("scalar", lambda e, g=g: e.activation(out=t("usq")[:, :], in_=x0[:, g, :],
                                                         func=AF.Square, accum_out=st[:, 4 + g:5 + g]),
                   reads=["x0"], writes=[("stats", "q", g), "usq"])
        p.emit("vector", lambda e: e.reduce_sum(out=t("red1")[0:8, 0:1], in_=xed[0:8, :], axis=AX.X),
               reads=edk, writes=[("red1", "e")])
        p.emit("scalar", lambda e: e.activation(out=t("usq")[0:8, :], in_=xed[0:8, :],
                                                func=AF.Square, accum_out=st[0:8, 8:9]),
               reads=edk, writes=[("stats", "qe"), "usq"])

        def fin(sl_s, sl_q, sl_mu, sl_nb, keys, tag):
            p.emit("vector", lambda e: e.tensor_scalar(out=sl_mu, in0=sl_s, scalar1=1.0 / C,
                                                       scalar2=None, op0=ALU.mult),
                   reads=keys, writes=[("stats", tag, "mu")])
            p.emit("vector", lambda e: e.tensor_tensor(out=sl_nb, in0=sl_mu, in1=sl_mu, op=ALU.mult),
                   reads=[("stats", tag, "mu")], writes=[("stats", tag, "m2")])
            p.emit("vector", lambda e: e.scalar_tensor_tensor(out=sl_q, in0=sl_q, scalar=1.0 / C,
                                                              in1=sl_nb, op0=ALU.mult, op1=ALU.subtract),
                   reads=keys + [("stats", tag, "m2")], writes=[("stats", tag, "v")])
            p.emit("vector", lambda e: e.tensor_scalar(out=sl_q, in0=sl_q, scalar1=1e-5,
                                                       scalar2=None, op0=ALU.add),
                   reads=[("stats", tag, "v")], writes=[("stats", tag, "v")])
            p.emit("vector", lambda e: e.reciprocal(out=sl_q, in_=sl_q),
                   reads=[("stats", tag, "v")], writes=[("stats", tag, "v")])
            p.emit("scalar", lambda e: e.activation(out=sl_q, in_=sl_q, func=AF.Sqrt),
                   reads=[("stats", tag, "v")], writes=[("stats", tag, "r")])
            p.emit("vector", lambda e: e.scalar_tensor_tensor(out=sl_nb, in0=sl_mu, scalar=-1.0,
                                                              in1=sl_q, op0=ALU.mult, op1=ALU.mult),
                   reads=[("stats", tag, "r"), ("stats", tag, "mu")], writes=[("stats", tag, "nb")])

        fin(st[:, 0:NG], st[:, 4:4 + NG], st[:, 10:10 + NG], t("lscr")[:, 0:NG],
            [("stats", "s", g) for g in range(NG)] + [("stats", "q", g) for g in range(NG)], "g")
        fin(t("red1")[0:8, 0:1], st[0:8, 8:9], st[0:8, 9:10], t("lscr")[0:8, 4:5],
            [("red1", "e"), ("stats", "qe")], "e")

        for g in range(NG):
            p.emit("scalar", lambda e, g=g: e.activation(
                out=u_tm[:, g, :], in_=x0[:, g, :], func=AF.Identity,
                scale=st[:, 4 + g:5 + g], bias=t("lscr")[:, g:g + 1]),
                reads=["x0", ("stats", "g", "r"), ("stats", "g", "nb")], writes=[("u_tm", g)])
        p.emit("scalar", lambda e: e.activation(
            out=xed[32:40, :], in_=xed[0:8, :], func=AF.Identity,
            scale=st[0:8, 8:9], bias=t("lscr")[0:8, 4:5]),
            reads=edk + [("stats", "e", "r"), ("stats", "e", "nb")], writes=[("xed", "ue")])

        for g in range(NG):
            def trg(e, g=g):
                last = None
                for cc in range(NCH):
                    bank = pw[PB0 + cc // 4]
                    last = e.transpose(bank[:, (cc % 4) * P:(cc % 4 + 1) * P],
                                       u_tm[:, g, cc * P:(cc + 1) * P], t("ident_f")[:, :])
                return last
            p.emit("tensor", trg, reads=[("u_tm", g), "ident_f"],
                   writes=[(PB(PB0),), (PB(PB1),)])
            for half in range(2):
                p.emit("vector", lambda e, g=g, half=half: e.tensor_copy(
                    out=u_fm[:, half * 4:(half + 1) * 4, HL + g * P:HL + (g + 1) * P],
                    in_=pw[PB0 + half][:, :].rearrange("p (cc n) -> p cc n", n=P)),
                    reads=[(PB(PB0 + half),)], writes=[("u_fm", g, half)])
        def tre(e):
            last = None
            for cc in range(NCH):
                last = e.transpose(pw[PB0][:, cc * 8:(cc + 1) * 8], xed[32:40, cc * P:(cc + 1) * P],
                                   t("ident_f")[32:40, 32:40])
            return last
        p.emit("tensor", tre, reads=[("xed", "ue"), "ident_f"], writes=[(PB(PB0),)])
        def eedge(e):
            last = None
            for cc in range(NCH):
                last = e.tensor_copy(out=u_fm[:, cc, 0:4], in_=pw[PB0][:, cc * 8:cc * 8 + 4])
                last = e.tensor_copy(out=u_fm[:, cc, W - 4:W], in_=pw[PB0][:, cc * 8 + 4:cc * 8 + 8])
            return last
        p.emit("vector", eedge, reads=[(PB(PB0),)], writes=[("u_fm", "edge")])
        ufmk = [("u_fm", g, h) for g in range(NG) for h in range(2)] + [("u_fm", "edge")]

        p.emit("gpsimd", lambda e: e.tensor_copy(
            out=t("u_bf")[:, :, HL:HL + TT],
            in_=u_fm[:, :, HL:HL + TT]),
            reads=ufmk, writes=["u_bf"])

        def mm_win(e):
            last = None
            for hh in range(2):
                for cc in range(NCH):
                    last = e.matmul(pw[PB1][hh * 64:(hh + 1) * 64, 0:TH], t("w_Win")[:, cc, :],
                                    t("u_bf")[:, cc, HL + hh * TH:HL + (hh + 1) * TH],
                                    start=(cc == 0), stop=False)
                last = e.matmul(pw[PB1][hh * 64:(hh + 1) * 64, 0:TH],
                                t("binrow")[0:1, :],
                                t("ones_row")[0:1, 0:TH], start=False, stop=True)
            return last
        p.emit("tensor", mm_win, reads=["u_bf", "w_Win", "binrow", "ones_row"],
               writes=[(PB(PB1),)])
        p.emit("vector", lambda e: e.tensor_copy(out=t("s2g2")[:, 0, :], in_=pw[PB1][:, 0:TH]),
               reads=[(PB(PB1),)], writes=[("s2g2", 0)])
        def inner1(e):
            last = None
            for i in range(16):
                last = e.activation(out=t("hscr")[:, i, :], in_=t("s2g2")[:, 0, :],
                                    func=AF.Identity, scale=fcn[:, i:i + 1],
                                    bias=fcn[:, 16 + i:17 + i])
            return last
        p.emit("scalar", inner1,
               reads=[("s2g2", 0), ("ffn_c", "fW1", 0), ("ffn_c", "fW1", 1),
                      ("ffn_c", "fb1", 0), ("ffn_c", "fb1", 1)],
               writes=[("hscr", 0)])
        def inner2(e):
            last = None
            for i in range(16):
                last = e.tensor_scalar(out=t("hscr")[:, i, :], in0=t("hscr")[:, i, :],
                                       scalar1=0.0, scalar2=fcn[:, 32 + i:33 + i],
                                       op0=ALU.max, op1=ALU.mult)
            return last
        p.emit("vector", inner2, reads=[("hscr", 0), ("ffn_c", "fW2", 0), ("ffn_c", "fW2", 1)],
               writes=[("hscr", 1)])
        hs = t("hscr").rearrange("p i n -> p (i n)")
        for stepi, width in enumerate((8 * TH, 4 * TH, 2 * TH, TH)):
            p.emit("vector", lambda e, width=width: e.tensor_tensor(
                out=hs[:, 0:width], in0=hs[:, 0:width], in1=hs[:, width:2 * width], op=ALU.add),
                reads=[("hscr", stepi + 1)], writes=[("hscr", stepi + 2)])
        p.emit("vector", lambda e: e.tensor_copy(out=t("s2g2")[:, 1, :], in_=hs[:, 0:TH]),
               reads=[("hscr", 5)], writes=[("s2g2", 1)])
        for g in range(NG):
            def mmwo(e, g=g):
                hh, off = divmod(g, max(NG // 2, 1))
                last = None
                for half in range(2):
                    last = e.matmul(pw[PB0 + half][:, 0:512],
                                    t("s2g2")[hh * 64:(hh + 1) * 64, 1, off * P:(off + 1) * P],
                                    t("w_Wout")[hh * 64:(hh + 1) * 64, half * 512:(half + 1) * 512],
                                    start=True, stop=False)
                    last = e.matmul(pw[PB0 + half][:, 0:512],
                                    t("ones_row")[0:1, 0:P],
                                    t("rows2")[0:1, half * 512:(half + 1) * 512],
                                    start=False, stop=True)
                return last
            p.emit("tensor", mmwo, reads=[("s2g2", 1), "w_Wout", ("w_Wout", "hi"), ("rows2", "bout"), "ones_row"],
                   writes=[(PB(PB0),), (PB(PB1),)])
            for half in range(2):
                p.emit("scalar", lambda e, half=half: e.activation(
                    out=t("evs")[:, half, :], in_=pw[PB0 + half][:, 0:512],
                    func=AF.Identity),
                    reads=[(PB(PB0 + half),)], writes=[("evs", half)])
                p.emit("gpsimd", lambda e, g=g, half=half: e.tensor_tensor(
                    out=t("y0t")[:, g, half * 512:(half + 1) * 512],
                    in0=x0[:, g, half * 512:(half + 1) * 512],
                    in1=t("evs")[:, half, :], op=ALU.add),
                    reads=["x0", ("evs", half)], writes=[("y0t", g, half)])
        p.emit("sync", lambda e: e.dma_start(
            out=y0sp[t0:t0 + TT, :].rearrange("(g p) c -> p g c", p=P), in_=t("y0t")[:, :, :]),
            reads=[("y0t", g, h) for g in range(NG) for h in range(2)], dma=f"c_y0{ti % 2}")

        a1f = t("a1f")
        scr = t("cscr").rearrange("p cc n -> p (cc n)")
        tp = taps
        for cc in range(NCH):
            p.emit("vector", lambda e, cc=cc: e.tensor_scalar(
                out=scr[:, cc * W2:cc * W2 + W2], in0=u_fm[:, cc, 0:W2],
                scalar1=tp[:, 0, cc, 0:1], scalar2=tp[:, 0, cc, 3:4], op0=ALU.mult, op1=ALU.add),
                reads=ufmk + [("taps", 0, "w"), ("taps", 0, "b")], writes=[("cscr", cc)])
            p.emit("vector", lambda e, cc=cc: e.scalar_tensor_tensor(
                out=scr[:, cc * W2:cc * W2 + W2], in0=u_fm[:, cc, 2:W],
                scalar=tp[:, 0, cc, 2:3], in1=scr[:, cc * W2:cc * W2 + W2],
                op0=ALU.mult, op1=ALU.add),
                reads=ufmk + [("taps", 0, "w"), ("cscr", cc)], writes=[("cscr", cc)])
            p.emit("vector", lambda e, cc=cc: e.scalar_tensor_tensor(
                out=a1f[:, cc, 1:W - 1], in0=u_fm[:, cc, 1:W - 1],
                scalar=tp[:, 0, cc, 1:2], in1=scr[:, cc * W2:cc * W2 + W2],
                op0=ALU.mult, op1=ALU.add),
                reads=ufmk + [("taps", 0, "w"), ("cscr", cc)], writes=[("a1f", cc)])
        a1k = [("a1f", cc) for cc in range(NCH)]
        p.emit("sync", lambda e: e.dma_start(
            out=a1sp[ti],
            in_=a1f[:, :, HL:HL + TT]),
            reads=a1k, dma=f"c_a1{ti % 2}")

        p.emit("gpsimd", lambda e: e.tensor_copy(
            out=t("a1bf")[:, :, 0:T4],
            in_=a1f[:, :, 2:W2]),
            reads=a1k, writes=["a1bf"])
        p.emit("scalar", lambda e: e.activation(
            out=t("sqbf")[:, :, :],
            in_=a1f[:, :, 2:W2], func=AF.Square),
            reads=a1k, writes=["sqbf"])
        ln_fm(t("a1bf"), ["a1bf"], "sqbf", T4, a1f, 2, a1k, t("xn1"), "xn1", PB0)
        xn1k = [("xn1", cc) for cc in range(NCH)]

        for m in range(NCH):
            bk = nbank(par)
            def mma(e, m=m, bk=bk):
                last = None
                for cc in range(NCH):
                    last = e.matmul(pw[bk][:, 0:TT], t("w_actp")[:, cc, m * P:(m + 1) * P],
                                    t("xn1")[:, cc, 2:T2], start=(cc == 0), stop=False)
                return e.matmul(pw[bk][:, 0:TT], t("r_actb")[0:1, m * P:(m + 1) * P],
                                t("ones_row")[0:1, 0:TT], start=False, stop=True)
            p.emit("tensor", mma, reads=xn1k + ["w_actp", ("r_actb",), "ones_row"],
                   writes=[(PB(bk),)])
            p.emit("scalar", lambda e, m=m, bk=bk: e.activation(
                out=t("acr")[:, m, :], in_=pw[bk][:, 0:TT], func=AF.Silu),
                reads=[(PB(bk),)], writes=[("acr", m)])
        p.emit("sync", lambda e: e.dma_start(
            out=arsp[ti], in_=t("acr")[:, :, :]),
            reads=[("acr", m) for m in range(NCH)], dma=f"c_ar{ti % 2}")

        for m in range(NCH):
            chunks = [(o, nn, nbank(par)) for (o, nn) in splits(0, T4)]
            def mmi(e, m=m, chunks=chunks):
                last = None
                for (o, nn, bk) in chunks:
                    for cc in range(NCH):
                        last = e.matmul(pw[bk][:, 0:nn], t("w_in")[:, cc, m * P:(m + 1) * P],
                                        t("xn1")[:, cc, o:o + nn], start=(cc == 0), stop=False)
                    last = e.matmul(pw[bk][:, 0:nn], t("r_inb")[0:1, m * P:(m + 1) * P],
                                    t("ones_row")[0:1, 0:nn], start=False, stop=True)
                return last
            p.emit("tensor", mmi, reads=xn1k + ["w_in", ("r_inb",), "ones_row"],
                   writes=[(PB(bk),) for (_o, _n, bk) in chunks])
            for (o, nn, bk) in chunks:
                p.emit("vector", lambda e, m=m, o=o, nn=nn, bk=bk: e.tensor_copy(
                    out=t("xi0")[:, m, o:o + nn], in_=pw[bk][:, 0:nn]),
                    reads=[(PB(bk),)], writes=[("xi0", m)])
        xi0k = [("xi0", m) for m in range(NCH)]
        if ti == 0:
            p.emit("vector", lambda e: e.tensor_scalar(
                out=t("xi0")[:, :, 0:2], in0=t("xi0")[:, :, 0:2],
                scalar1=t("lrmask")[:, 0:1], scalar2=None, op0=ALU.mult),
                reads=xi0k + ["lrmask"], writes=xi0k)
        if ti == NTILE - 1:
            p.emit("vector", lambda e: e.tensor_scalar(
                out=t("xi0")[:, :, T2:T4], in0=t("xi0")[:, :, T2:T4],
                scalar1=t("lrmask")[:, 1:2], scalar2=None, op0=ALU.mult),
                reads=xi0k + ["lrmask"], writes=xi0k)

        for cc in range(NCH):
            p.emit("vector", lambda e, cc=cc: e.tensor_scalar(
                out=scr[:, cc * W2:cc * W2 + T2], in0=t("xi0")[:, cc, 0:T2],
                scalar1=tp[:, 1, cc, 0:1], scalar2=tp[:, 1, cc, 3:4], op0=ALU.mult, op1=ALU.add),
                reads=xi0k + [("taps", 1, "w"), ("taps", 1, "b")], writes=[("cscr", cc)])
            p.emit("vector", lambda e, cc=cc: e.scalar_tensor_tensor(
                out=scr[:, cc * W2:cc * W2 + T2], in0=t("xi0")[:, cc, 2:T4],
                scalar=tp[:, 1, cc, 2:3], in1=scr[:, cc * W2:cc * W2 + T2],
                op0=ALU.mult, op1=ALU.add),
                reads=xi0k + [("taps", 1, "w"), ("cscr", cc)], writes=[("cscr", cc)])
            p.emit("vector", lambda e, cc=cc: e.scalar_tensor_tensor(
                out=t("xi")[:, cc, :], in0=t("xi0")[:, cc, 1:T2 + 1],
                scalar=tp[:, 1, cc, 1:2], in1=scr[:, cc * W2:cc * W2 + T2],
                op0=ALU.mult, op1=ALU.add),
                reads=xi0k + [("taps", 1, "w"), ("cscr", cc)], writes=[("xi", cc, "c")])
            p.emit("scalar", lambda e, cc=cc: e.activation(
                out=t("xi")[:, cc, :], in_=t("xi")[:, cc, :], func=AF.Silu),
                reads=[("xi", cc, "c")], writes=[("xi", cc)])
        xik = [("xi", cc) for cc in range(NCH)]
        if ti == 0:
            p.emit("vector", lambda e: e.tensor_scalar(
                out=t("xi")[:, :, 0:1], in0=t("xi")[:, :, 0:1],
                scalar1=t("lrmask")[:, 0:1], scalar2=None, op0=ALU.mult),
                reads=xik + ["lrmask"], writes=xik)
        if ti == NTILE - 1:
            p.emit("vector", lambda e: e.tensor_scalar(
                out=t("xi")[:, :, T2 - 1:T2], in0=t("xi")[:, :, T2 - 1:T2],
                scalar1=t("lrmask")[:, 1:2], scalar2=None, op0=ALU.mult),
                reads=xik + ["lrmask"], writes=xik)

        for cc in range(NCH):
            p.emit("vector", lambda e, cc=cc: e.tensor_scalar(
                out=scr[:, cc * W2:cc * W2 + TT], in0=t("xi")[:, cc, 0:TT],
                scalar1=tp[:, 2, cc, 0:1], scalar2=tp[:, 2, cc, 3:4], op0=ALU.mult, op1=ALU.add),
                reads=xik + [("taps", 2, "w"), ("taps", 2, "b")], writes=[("cscr", cc)])
            p.emit("vector", lambda e, cc=cc: e.scalar_tensor_tensor(
                out=scr[:, cc * W2:cc * W2 + TT], in0=t("xi")[:, cc, 2:T2],
                scalar=tp[:, 2, cc, 2:3], in1=scr[:, cc * W2:cc * W2 + TT],
                op0=ALU.mult, op1=ALU.add),
                reads=xik + [("taps", 2, "w"), ("cscr", cc)], writes=[("cscr", cc)])
            p.emit("vector", lambda e, cc=cc: e.scalar_tensor_tensor(
                out=t("lepe_sb")[:, cc, :], in0=t("xi")[:, cc, 1:TT + 1],
                scalar=tp[:, 2, cc, 1:2], in1=scr[:, cc * W2:cc * W2 + TT],
                op0=ALU.mult, op1=ALU.add),
                reads=xik + [("taps", 2, "w"), ("cscr", cc)], writes=[("lepe_sb", cc)])
        p.emit("sync", lambda e: e.dma_start(
            out=lpsp[ti], in_=t("lepe_sb")[:, :, :]),
            reads=[("lepe_sb", cc) for cc in range(NCH)], dma=f"c_lp{ti % 2}")

        return

    def qk_joint(ta, tb):
        for j in range(16):
            sl3 = j % 4
            p.curpar[0] = 0
            p.emit("sync", lambda e, j=j, sl3=sl3: e.dma_start(
                out=tens["w3s"][:, sl3, :, :], in_=qkbf[j]),
                reads=[("qkbf", cc2) for cc2 in range(NCH)],
                writes=[("w3s", sl3)], dma=f"c_qs{sl3}")
            for par in (0, 1):
                p.curpar[0] = par
                bk = nbank(par)
                def mmq(e, j=j, bk=bk, sl3=sl3):
                    last = None
                    for cc in range(NCH):
                        last = e.matmul(pw[bk][:, 0:TT], tens["w3s"][:, sl3, cc, :],
                                        t("xi")[:, cc, 1:TT + 1], start=(cc == 0), stop=False)
                    return e.matmul(pw[bk][:, 0:TT], t("r_qkb")[0:1, j * P:(j + 1) * P],
                                    t("ones_row")[0:1, 0:TT], start=False, stop=True)
                p.emit("tensor", mmq,
                       reads=[("xi", cc) for cc in range(NCH)] +
                             [("w3s", sl3), ("r_qkb",), "ones_row"],
                       writes=[(PB(bk),)])
                p.emit("scalar", lambda e, bk=bk: e.activation(
                    out=t("scab")[:, 0, :], in_=pw[bk][:, 0:TT], func=AF.Relu,
                    scale=-1.0),
                    reads=[(PB(bk),)], writes=[("scab", 0)])
                p.emit("scalar", lambda e: e.activation(
                    out=t("scab")[:, 1, :], in_=t("scab")[:, 0, :], func=AF.Exp,
                    scale=-1.0),
                    reads=[("scab", 0)], writes=[("scab", 1)])
                p.emit("vector", lambda e, j=j, bk=bk: e.scalar_tensor_tensor(
                    out=t("qke")[:, j, :], in0=pw[bk][:, 0:TT], scalar=0.0,
                    in1=t("scab")[:, 1, :], op0=ALU.max, op1=ALU.add),
                    reads=[(PB(bk),), ("scab", 1)], writes=[("qke", j)])
        p.curpar[0] = 0

    def pass1_tail(ti):
        p.curpar[0] = ti % 2
        par = ti % 2
        t0 = ti * TT
        xik = [("xi", cc) for cc in range(NCH)]

        for cc in range(NCH):
            p.emit("vector", lambda e, cc=cc: e.reduce_sum(
                out=t("red1")[:, 0:1], in_=t("qke")[:, 8 + cc, :], axis=AX.X),
                reads=[("qke", 8 + cc)], writes=[("red1", "k")])
            p.emit("vector", lambda e, cc=cc: e.tensor_tensor(
                out=t("ksum")[:, cc:cc + 1], in0=t("ksum")[:, cc:cc + 1],
                in1=t("red1")[:, 0:1], op=ALU.add),
                reads=[("red1", "k"), "ksum"], writes=["ksum"])

        for j in range(16):
            bk = nbank(par)
            p.emit("tensor", lambda e, j=j, bk=bk: e.matmul(
                pw[bk][:, 0:TT], t("Rm")[:, j % 8, :], t("qke")[:, j, :], start=True, stop=True),
                reads=[("qke", j), ("Rm", j % 8)], writes=[(PB(bk),)])
            dst = t("qro") if j < 8 else t("kro")
            key = ("qro", j) if j < 8 else ("kro", j - 8)
            p.emit("scalar", lambda e, j=j, bk=bk, dst=dst: e.activation(
                out=dst[:, j % 8, :], in_=pw[bk][:, 0:TT], func=AF.Identity,
                bias=t("crope_col")[:, j % 8:j % 8 + 1]),
                reads=[(PB(bk),), "crope_col"], writes=[key])
        p.emit("sync", lambda e: e.dma_start(
            out=qrsp[ti], in_=t("qro")[:, :, :]),
            reads=[("qro", j) for j in range(8)], dma=f"c_qr{ti % 2}")

        pt = pt4 if par == 0 else pt5
        ptk = "pt4" if par == 0 else "pt5"
        for g in range(NG):
            def trk(e, g=g):
                last = None
                for cc in range(NCH):
                    last = e.transpose(pt[:, cc * P:(cc + 1) * P],
                                       t("kro")[:, cc, g * P:(g + 1) * P], t("ident_b")[:, :])
                return last
            p.emit("tensor", trk, reads=[("kro", cc) for cc in range(NCH)] + ["ident_b"],
                   writes=[(ptk,)])
            p.emit("scalar", lambda e, g=g: e.activation(
                out=t("krtvh")[:, 0, g, :], in_=pt[:, :], func=AF.Identity),
                reads=[(ptk,)], writes=[("krtvh", "k", g)])
            def trv(e, g=g):
                last = None
                for cc in range(NCH):
                    last = e.transpose(pt[:, cc * P:(cc + 1) * P],
                                       t("xi")[:, cc, 1 + g * P:1 + (g + 1) * P], t("ident_b")[:, :])
                return last
            p.emit("tensor", trv, reads=xik + ["ident_b", ("krtvh", "k", g)], writes=[(ptk, "v")])
            p.emit("scalar", lambda e, g=g: e.activation(
                out=t("krtvh")[:, 1, g, :], in_=pt[:, :], func=AF.Identity),
                reads=[(ptk, "v")], writes=[("krtvh", "v", g)])
            def mmkv(e, g=g, ti=ti):
                last = None
                for h in range(H):
                    last = e.matmul(pkv[0:64, h * 64:(h + 1) * 64],
                                    t("krtvh")[:, 0, g, h * 64:(h + 1) * 64],
                                    t("krtvh")[:, 1, g, h * 64:(h + 1) * 64],
                                    start=False, stop=False)
                return last
            p.emit("tensor", mmkv, reads=[("krtvh", "k", g), ("krtvh", "v", g)], writes=[("pkv",)])

    assert NTILE % 2 == 0

    def pair_run(tile_fn, ta, tb):
        p.record()
        tile_fn(ta)
        A = p.stop_record()
        p.record()
        tile_fn(tb)
        B = p.stop_record()
        for i in range(max(len(A), len(B))):
            if i < len(A):
                p.commit(A[i])
            if i < len(B):
                p.commit(B[i])

    for tj in range(NTILE // 2):
        pair_run(pass1_head, 2 * tj, 2 * tj + 1)
        qk_joint(2 * tj, 2 * tj + 1)
        pair_run(pass1_tail, 2 * tj, 2 * tj + 1)
    p.curpar[0] = 0

    def pkv_close(e):
        e.matmul(pkv[:, 0:512], t("zrow")[0:1, :], t("ones_row")[0:1, 0:512],
                 start=False, stop=True)
        return e.matmul(pkv[:, 512:1024], t("zrow")[0:1, :], t("ones_row")[0:1, 0:512],
                        start=False, stop=True)
    p.emit("tensor", pkv_close, reads=["zrow", "ones_row", ("pkv",)], writes=[("pkv",)])
    p.emit("vector", lambda e: e.tensor_copy(out=t("kv_raw")[0:64, :], in_=pkv[0:64, :]),
           reads=[("pkv",)], writes=[("kv_raw", "kv")])
    p.emit("tensor", lambda e: e.transpose(pw[0][0:NCH, 0:P], t("ksum")[:, :], t("ident_f")[:, :]),
           reads=["ksum", "ident_f"], writes=[(PB(0),)])
    p.emit("vector", lambda e: e.tensor_copy(out=t("lscr")[0:NCH, 0:P], in_=pw[0][0:NCH, 0:P]),
           reads=[(PB(0),)], writes=[("lscr", "kst")])
    p.emit("sync", lambda e: e.dma_start(out=kmb.rearrange("(a b) -> a b", a=NCH), in_=t("lscr")[0:NCH, 0:P]),
           reads=[("lscr", "kst")], writes=["kmb"], dma="c_km")
    p.emit("sync", lambda e: e.dma_start(out=t("kv_raw")[64:65, :], in_=kmb[0:C][None, :]),
           reads=["kmb"], writes=[("kv_raw", "ks")], dma="c_km")
    p.emit("gpsimd", lambda e: e.indirect_dma_start(
        out=S_kv[:, :], out_offset=IndirectOffsetOnAxis(ap=t("idx")[0:65, 0:1], axis=0),
        in_=t("kv_raw")[:, :], in_offset=None),
        reads=[("kv_raw", "kv"), ("kv_raw", "ks"), ("idx", 0)], writes=[("S_kv",)], dma="c_hs1")
    p.emit("gpsimd", lambda e: e.indirect_dma_start(
        out=S_fl[:, :], out_offset=IndirectOffsetOnAxis(ap=t("idx")[0:2, 2:3], axis=0),
        in_=t("magic")[:, :], in_offset=None),
        reads=["magic", ("idx", 2), ("S_kv",)], writes=[("S_fl", "kv")], dma="c_hs1f")

    def make_poll(sem, rows, tag, prev_keys):
        def fn(e):
            with e.register(f"tg_{tag}") as tg:
                e.reg_mov(tg, 0)
                for i, r in enumerate(rows):
                    e.dma_start(out=t("fbuf")[0:1, i * 16:(i + 1) * 16],
                                in_=S_fl[r:r + 1, :]).then_inc(sem, 16)
                    e.reg_alu(tg, tg, 16, ALU.add)
                    e.wait_ge(sem, tg)
                    with e.While(lambda i=i: e.value_load(t("fbuf")[0:1, i * 16:i * 16 + 1]) != MAGIC):
                        e.dma_start(out=t("fbuf")[0:1, i * 16:(i + 1) * 16],
                                    in_=S_fl[r:r + 1, :]).then_inc(sem, 16)
                        e.reg_alu(tg, tg, 16, ALU.add)
                        e.wait_ge(sem, tg)
            return e.nop()
        p.emit("gpsimd", fn, reads=prev_keys, writes=[("fbuf", tag)])

    make_poll(poll1, (0, 1), "kv", [("S_fl", "kv"), "fbuf"])
    p.emit("sync", lambda e: e.dma_start(
        out=t("kv_cmb")[0:65, :], in_=S_kv[0:65, :]),
        reads=[("fbuf", "kv"), ("S_kv",)], writes=[("kv_cmb", "rd")], dma="c_hs1r")
    p.emit("vector", lambda e: e.tensor_copy(out=t("kv_raw")[0:65, :], in_=t("kv_cmb")[0:65, :]),
           reads=[("kv_cmb", "rd"), ("kv_raw", "kv"), ("kv_raw", "ks")], writes=[("kv_raw", "s0")])
    p.emit("sync", lambda e: e.dma_start(
        out=t("kv_cmb")[0:65, :], in_=S_kv[65:130, :]),
        reads=[("kv_raw", "s0"), ("S_kv",)], writes=[("kv_cmb", "rd2")], dma="c_hs1r2")
    p.emit("vector", lambda e: e.tensor_tensor(
        out=t("kv_raw")[0:65, :], in0=t("kv_raw")[0:65, :], in1=t("kv_cmb")[0:65, :],
        op=ALU.add),
        reads=[("kv_cmb", "rd2"), ("kv_raw", "s0")], writes=[("kv_raw", "tot")])
    p.emit("vector", lambda e: e.tensor_scalar(
        out=t("kv_raw")[64:65, :], in0=t("kv_raw")[64:65, :], scalar1=1.0 / SEQ,
        scalar2=1.0, op0=ALU.mult, op1=ALU.add),
        reads=[("kv_raw", "tot")], writes=[("kv_raw", "km")])
    p.emit("sync", lambda e: e.dma_start(out=kmb[None, 0:C], in_=t("kv_raw")[64:65, :]),
           reads=[("kv_raw", "km")], writes=["kmb"], dma="c_km")
    p.emit("sync", lambda e: e.dma_start(out=t("km_col")[:, :], in_=kmb.rearrange("(cc p) -> p cc", p=P),
           allow_slow_non_contiguous=True),
           reads=["kmb"], writes=["km_col"], dma="c_km")
    p.emit("vector", lambda e: e.tensor_copy(out=t("km_bf")[:, :], in_=t("km_col")[:, :]),
           reads=["km_col"], writes=["km_bf"])
    def mmkm(e):
        last = None
        for cc in range(NCH):
            last = e.matmul(pw[0][:, cc:cc + 1], t("Rm")[:, cc, :], t("km_bf")[:, cc:cc + 1],
                            start=True, stop=True)
        return last
    p.emit("tensor", mmkm, reads=["km_bf"] + [("Rm", cc) for cc in range(NCH)], writes=[(PB(0),)])
    p.emit("vector", lambda e: e.tensor_copy(out=t("km_bf")[:, :], in_=pw[0][:, 0:NCH]),
           reads=[(PB(0),)], writes=[("km_bf", "r")])
    p.emit("gpsimd", lambda e: e.memset(t("kmr_bd")[:, :, :], 0.0), writes=[("kmr_bd", "z")])

    kmrflat = t("kmr_bd").rearrange("p cc m -> p (cc m)")
    p.emit("gpsimd", lambda e: e.dma_start(
        out=kmrflat[0:64, 0:127:18], in_=t("km_bf")[0:64, :],
        allow_slow_non_contiguous=True),
        reads=[("km_bf", "r"), ("kmr_bd", "z")], writes=[("kmr_bd", "a")], dma="c_kbd")
    p.emit("gpsimd", lambda e: e.dma_start(
        out=kmrflat[64:128, 1:128:18], in_=t("km_bf")[64:128, :],
        allow_slow_non_contiguous=True),
        reads=[("km_bf", "r"), ("kmr_bd", "z"), ("kmr_bd", "a")],
        writes=[("kmr_bd", "b")], dma="c_kbd2")
    p.emit("vector", lambda e: e.tensor_scalar(
        out=t("kv_raw")[0:64, :], in0=t("kv_raw")[0:64, :], scalar1=1.0 / SEQ,
        scalar2=None, op0=ALU.mult),
        reads=[("kv_raw", "tot")], writes=[("kv_raw", "sc")])
    p.emit("vector", lambda e: e.tensor_copy(
        out=t("qr2").rearrange("p cc n -> p (cc n)")[0:64, 0:C], in_=t("kv_raw")[0:64, :]),
        reads=[("kv_raw", "sc")], writes=[("qr2", "kvbf")])
    p.emit("gpsimd", lambda e: e.memset(t("kv_bd")[:, :, :], 0.0), writes=[("kv_bd", "z")])
    qr2flat = t("qr2").rearrange("p cc n -> p (cc n)")
    p.emit("gpsimd", lambda e: e.dma_start(
        out=t("kv_bd")[0:64, :, 0:64],
        in_=qr2flat[0:64, 0:C].rearrange("p (cc hf ee) -> p cc hf ee", cc=NCH, hf=2)[:, :, 0, :]),
        reads=[("qr2", "kvbf"), ("kv_bd", "z")], writes=[("kv_bd", "a")], dma="c_kvb")
    p.emit("gpsimd", lambda e: e.dma_start(
        out=t("kv_bd")[64:128, :, 64:128],
        in_=qr2flat[0:64, 0:C].rearrange("p (cc hf ee) -> p cc hf ee", cc=NCH, hf=2)[:, :, 1, :]),
        reads=[("qr2", "kvbf"), ("kv_bd", "z"), ("kv_bd", "a")], writes=[("kv_bd", "b")],
        dma="c_kvb2")
    selz = t("selz")
    p.emit("gpsimd", lambda e: e.memset(selz[:, :], 0.0), writes=["selz"])
    p.emit("gpsimd", lambda e: e.affine_select(
        out=selz.rearrange("h (cc hf m) -> h cc hf m", cc=NCH, hf=2),
        in_=selz.rearrange("h (cc hf m) -> h cc hf m", cc=NCH, hf=2),
        compare_op=ALU.not_equal, fill=1.0, base=0,
        pattern=[[-2, NCH], [-1, 2], [0, 64]], channel_multiplier=1),
        reads=["selz"], writes=["selz"])

    kvbdk = [("kv_bd", "a"), ("kv_bd", "b")]
    kmrk = [("kmr_bd", "a"), ("kmr_bd", "b")]

    def load_w2(wname, dst, nsrc, ncols, wkey, gfold=None):
        for cc in range(nsrc):
            p.emit("sync", lambda e, cc=cc: e.dma_start(
                out=stg[:, 0:ncols], in_=wn[wname][cc * P:(cc + 1) * P, :]),
                writes=[("w_stage", wname, cc)], dma="c_w2")
            if gfold is None:
                p.emit("vector", lambda e, cc=cc: e.tensor_copy(
                    out=dst[:, cc, :], in_=stg[:, 0:ncols]),
                    reads=[("w_stage", wname, cc)], writes=[(wkey, cc)])
            else:
                p.emit("vector", lambda e, cc=cc: e.tensor_scalar(
                    out=dst[:, cc, :], in0=stg[:, 0:ncols],
                    scalar1=t("g_cols")[:, gfold, cc:cc + 1], scalar2=None, op0=ALU.mult),
                    reads=[("w_stage", wname, cc), ("g_cols", gfold)], writes=[(wkey, cc)])

    load_w2("out_w", t("w_out"), NCH, C, "w_out")
    woutk = [("w_out", cc) for cc in range(NCH)]
    p.emit("sync", lambda e: e.dma_start(out=stg[0:1, 3 * C:4 * C], in_=wn["out_b"][None, :]),
           writes=[("w_stage", "r_out_b")], dma="c_pr")
    p.emit("vector", lambda e: e.tensor_copy(out=t("rows2")[0:1, 0:C], in_=stg[0:1, 3 * C:4 * C]),
           reads=[("w_stage", "r_out_b")], writes=[("rows2", "out_b")])

    def pass2a_tile(ti):
        p.curpar[0] = ti % 2
        par = ti % 2
        PB0, PB1 = 2 * par, 2 * par + 1
        t0 = ti * TT
        p.emit("sync", lambda e: e.dma_start(
            out=t("qr2")[:, :, :], in_=qrsp[ti]),
            writes=["qr2"], dma=f"c_qr{ti % 2}")
        p.emit("sync", lambda e: e.dma_start(
            out=t("ar2")[:, :, :], in_=arsp[ti]),
            writes=["ar2"], dma=f"c_ar{ti % 2}")
        p.emit("sync", lambda e: e.dma_start(
            out=t("lp2")[:, :, :], in_=lpsp[ti]),
            writes=["lp2"], dma=f"c_lp{ti % 2}")
        p.emit("sync", lambda e: e.dma_start(
            out=t("a1l")[:, :, :], in_=a1sp[ti]),
            writes=["a1l"], dma=f"c_a1{ti % 2}")
        def mms(e):
            last = None
            for cc in range(NCH):
                last = e.matmul(pw[PB0][0:16, 0:TT], t("kmr_bd")[:, cc, :], t("qr2")[:, cc, :],
                                start=(cc == 0), stop=(cc == NCH - 1))
            return last
        p.emit("tensor", mms, reads=["qr2"] + kmrk, writes=[(PB(PB0),)])
        p.emit("vector", lambda e: e.tensor_scalar(out=t("zsh")[:, :], in0=pw[PB0][0:16, 0:TT],
                                                   scalar1=1.0, scalar2=1e-6, op0=ALU.mult, op1=ALU.add),
               reads=[(PB(PB0),)], writes=["zsh"])
        p.emit("vector", lambda e: e.reciprocal(out=t("zsh")[:, :], in_=t("zsh")[:, :]),
               reads=["zsh"], writes=["zsh"])
        p.emit("vector", lambda e: e.tensor_copy(out=t("zbf")[:, :], in_=t("zsh")[:, :]),
               reads=["zsh"], writes=["zbf"])
        for cc in range(NCH):
            p.emit("tensor", lambda e, cc=cc: e.matmul(
                pw[PB0][:, 0:TT], t("selz")[:, cc * P:(cc + 1) * P], t("zbf")[:, :],
                start=True, stop=True),
                reads=["zbf", "selz"], writes=[(PB(PB0),)])
            p.emit("vector", lambda e, cc=cc: e.tensor_tensor(
                out=t("zae")[:, :], in0=pw[PB0][:, 0:TT], in1=t("ar2")[:, cc, :], op=ALU.mult),
                reads=[(PB(PB0),), "ar2"], writes=["zae"])
            p.emit("vector", lambda e, cc=cc: e.tensor_tensor(
                out=t("lab")[:, :], in0=t("lp2")[:, cc, :], in1=t("ar2")[:, cc, :], op=ALU.mult),
                reads=["lp2", "ar2"], writes=[("lab",)])
            bk = PB1
            p.emit("tensor", lambda e, cc=cc, bk=bk: e.matmul(
                pw[bk][:, 0:TT], t("kv_bd")[:, cc, :], t("qr2")[:, cc, :], start=True, stop=True),
                reads=["qr2"] + kvbdk, writes=[(PB(bk),)])
            p.emit("vector", lambda e, bk=bk: e.tensor_tensor(
                out=t("zscr")[:, :], in0=pw[bk][:, 0:TT], in1=t("zae")[:, :], op=ALU.mult),
                reads=[(PB(bk),), "zae"], writes=["zscr"])
            p.emit("vector", lambda e, cc=cc: e.tensor_tensor(
                out=t("aab")[:, cc, :], in0=t("zscr")[:, :], in1=t("lab")[:, :], op=ALU.add),
                reads=["zscr", ("lab",)], writes=[("aab", cc)])
        aak = [("aab", cc) for cc in range(NCH)]
        for m in range(NCH):
            bk = nbank(par)
            def mmo(e, m=m, bk=bk):
                last = None
                for cc in range(NCH):
                    last = e.matmul(pw[bk][:, 0:TT], t("w_out")[:, cc, m * P:(m + 1) * P],
                                    t("aab")[:, cc, :], start=(cc == 0), stop=False)
                return e.matmul(pw[bk][:, 0:TT], t("rows2")[0:1, m * P:(m + 1) * P],
                                t("ones_row")[0:1, 0:TT], start=False, stop=True)
            p.emit("tensor", mmo, reads=aak + woutk + [("rows2", "out_b"), "ones_row"],
                   writes=[(PB(bk),)])
            p.emit("scalar", lambda e, m=m, bk=bk: e.activation(
                out=t("evs")[:, m % 2, 0:TT], in_=pw[bk][:, 0:TT], func=AF.Identity),
                reads=[(PB(bk),)], writes=[("evs", m % 2)])
            p.emit("gpsimd", lambda e, m=m: e.tensor_tensor(
                out=t("a2w")[:, m, :], in0=t("a1l")[:, m, :], in1=t("evs")[:, m % 2, 0:TT],
                op=ALU.add),
                reads=["a1l", ("evs", m % 2)], writes=[("a2w", m)])
        a2k = [("a2w", m) for m in range(NCH)]
        p.emit("sync", lambda e: e.dma_start(
            out=a2sp[ti], in_=t("a2w")[:, :, :]),
            reads=a2k, dma=f"c_s2{ti % 2}")
        p.emit("vector", lambda e: e.tensor_copy(
            out=t("a2eg")[:, ti, :, 0:1], in_=t("a2w")[:, :, 0:1]),
            reads=a2k, writes=[("a2eg", ti, 0)])
        p.emit("vector", lambda e: e.tensor_copy(
            out=t("a2eg")[:, ti, :, 1:2], in_=t("a2w")[:, :, TT - 1:TT]),
            reads=a2k, writes=[("a2eg", ti, 1)])
        if ti == 0:
            p.emit("vector", lambda e: e.tensor_copy(
                out=t("a2keep")[:, 0, :], in_=t("a2w")[:, :, 0:1].rearrange("p cc o -> p (cc o)")),
                reads=a2k, writes=[("a2keep", 0)])
        if ti == NTILE - 1:
            p.emit("vector", lambda e: e.tensor_copy(
                out=t("a2keep")[:, 1, :], in_=t("a2w")[:, :, TT - 1:TT].rearrange("p cc o -> p (cc o)")),
                reads=a2k, writes=[("a2keep", 1)])

    for tj in range(NTILE // 2):
        pair_run(pass2a_tile, 2 * tj, 2 * tj + 1)
    p.curpar[0] = 0

    p.emit("sync", lambda e: e.dma_start(
        out=hrow.rearrange("(a p) b -> p a b", p=P), in_=t("zero8")[:, :, :]),
        reads=["zero8"], writes=[("hrow", "z")], dma="c_hz")
    p.emit("gpsimd", lambda e: e.indirect_dma_start(
        out=S_a2[:, :], out_offset=IndirectOffsetOnAxis(ap=t("idx")[:, 1:2], axis=0),
        in_=t("a2keep")[:, 0, :], in_offset=None),
        reads=[("a2keep", 0), ("idx", 1)], writes=[("S_a2", "a")], dma="c_h2a")
    p.emit("gpsimd", lambda e: e.indirect_dma_start(
        out=S_a2[:, :], out_offset=IndirectOffsetOnAxis(ap=t("idx")[:, 6:7], axis=0),
        in_=t("a2keep")[:, 1, :], in_offset=None),
        reads=[("a2keep", 1), ("idx", 6), ("S_a2", "a")], writes=[("S_a2", "b")], dma="c_h2b")
    p.emit("gpsimd", lambda e: e.indirect_dma_start(
        out=S_fl[:, :], out_offset=IndirectOffsetOnAxis(ap=t("idx")[0:2, 3:4], axis=0),
        in_=t("magic")[:, :], in_offset=None),
        reads=["magic", ("idx", 3), ("S_a2", "a"), ("S_a2", "b")], writes=[("S_fl", "a2")],
        dma="c_h2f")
    make_poll(poll2, (2, 3), "a2", [("S_fl", "a2"), "fbuf"])
    p.emit("sync", lambda e: e.dma_start(
        out=t("a2rd")[:, :, :], in_=S_a2[0:2 * P, :].rearrange("(s p) n -> p s n", s=2)),
        reads=[("fbuf", "a2"), ("S_a2", "a"), ("S_a2", "b")], writes=["a2rd"], dma="c_h2r")
    p.emit("gpsimd", lambda e: e.indirect_dma_start(
        out=hrow[:, :], out_offset=IndirectOffsetOnAxis(ap=t("idx")[:, 4:5], axis=0),
        in_=t("a2rd")[:, 0, :], in_offset=None),
        reads=["a2rd", ("idx", 4), ("hrow", "z")], writes=[("hrow", "a")], dma="c_h3a")
    p.emit("gpsimd", lambda e: e.indirect_dma_start(
        out=hrow[:, :], out_offset=IndirectOffsetOnAxis(ap=t("idx")[:, 5:6], axis=0),
        in_=t("a2rd")[:, 1, :], in_offset=None),
        reads=["a2rd", ("idx", 5), ("hrow", "z"), ("hrow", "a")], writes=[("hrow", "b")],
        dma="c_h3b")
    hrk = [("hrow", "a"), ("hrow", "b")]
    p.emit("sync", lambda e: e.dma_start(
        out=t("hpar")[:, :, :], in_=hrow[0:2 * P, :].rearrange("(s p) b -> p s b", s=2)),
        reads=hrk, writes=["hpar"], dma="c_hz")

    cvf = tens["qke~1"].rearrange("p a b -> p (a b)")
    qkf = tens["qke~0"].rearrange("p a b -> p (a b)")
    for cc in range(NCH):
        p.emit("sync", lambda e, cc=cc: e.dma_start(
            out=stg[:, 0:HID], in_=wn["fc1_w"][cc * P:(cc + 1) * P, :]),
            writes=[("w_stage", "f1", cc)], dma="c_w2")
        p.emit("vector", lambda e, cc=cc: e.tensor_scalar(
            out=cvf[:, 0:HID], in0=stg[:, 0:HID],
            scalar1=t("g_cols")[:, 1, cc:cc + 1], scalar2=None, op0=ALU.mult),
            reads=[("w_stage", "f1", cc), ("g_cols", 1)], writes=[("qke~1", "fold")])
        p.emit("sync", lambda e, cc=cc: e.dma_start(
            out=fc1bf[:, :, cc, :].rearrange("hcc p m -> p hcc m"),
            in_=cvf[:, 0:HID].rearrange("p (hcc m) -> p hcc m", m=P)),
            reads=[("qke~1", "fold")], writes=[("fc1bf", cc)], dma="c_w2o")
        p.emit("vector", lambda e, cc=cc: e.tensor_copy(out=qkf[:, 0:HID], in_=stg[:, 0:HID]),
               reads=[("w_stage", "f1", cc)], writes=[("qke~0", "raw")])
        if cc == 0:
            def mmb2open(e):
                last = None
                for i in range(4):
                    last = e.matmul(pw[i][:, 0:512], t("zrow")[0:1, :],
                                    t("ones_row")[0:1, 0:512], start=True, stop=False)
                return last
            p.emit("tensor", mmb2open, reads=["zrow", "ones_row"],
                   writes=[(PB(i),) for i in range(4)])
        def mmb2(e, cc=cc):
            last = None
            for oi, (o, nn) in enumerate(splits(0, HID)):
                bank = pw[oi % 4]
                row = 0 if oi < 4 else 32
                last = e.matmul(bank[row:row + 1, 0:nn], t("b_colsb")[:, 1, cc:cc + 1],
                                qkf[:, o:o + nn], start=False, stop=False)
            return last
        p.emit("tensor", mmb2, reads=[("qke~0", "raw"), "b_colsb"],
               writes=[(PB(i),) for i in range(4)])
    def mmb2close(e):
        last = None
        for i in range(4):
            last = e.matmul(pw[i][:, 0:512], t("zrow")[0:1, :],
                            t("ones_row")[0:1, 0:512], start=False, stop=True)
        return last
    p.emit("tensor", mmb2close, reads=["zrow", "ones_row"] + [(PB(i),) for i in range(4)],
           writes=[(PB(i),) for i in range(4)])
    p.emit("sync", lambda e: e.dma_start(
        out=stg[0:1, 0:HID], in_=wn["fc1_b"][None, :]),
        writes=[("w_stage", "f1b0")], dma="c_pr")
    for oi, (o, nn) in enumerate(splits(0, HID)):
        bank = pw[oi % 4]
        row = 0 if oi < 4 else 32
        p.emit("vector", lambda e, o=o, nn=nn, bank=bank, row=row: e.tensor_tensor(
            out=t("lscr")[0:1, 0:nn], in0=bank[row:row + 1, 0:nn],
            in1=stg[0:1, o:o + nn], op=ALU.add),
            reads=[(PB(oi % 4),), ("w_stage", "f1b0")], writes=[("lscr", "f")])
        p.emit("vector", lambda e, o=o, nn=nn: e.tensor_copy(
            out=t("r_fc1b")[0:1, o:o + nn], in_=t("lscr")[0:1, 0:nn]),
            reads=[("lscr", "f")], writes=[("r_fc1b",)])
    for hcc in range(NHID):
        p.emit("sync", lambda e, hcc=hcc: e.dma_start(
            out=stg[:, 0:C], in_=wn["fc2_w"][hcc * P:(hcc + 1) * P, :]),
            writes=[("w_stage", "f2", hcc)], dma="c_w2")
        p.emit("vector", lambda e, hcc=hcc: e.tensor_copy(out=cvf[:, 0:C], in_=stg[:, 0:C]),
               reads=[("w_stage", "f2", hcc)], writes=[("qke~1", "f2")])
        p.emit("sync", lambda e, hcc=hcc: e.dma_start(
            out=fc2bf[hcc, :, :], in_=cvf[:, 0:C]),
            reads=[("qke~1", "f2")], writes=[("fc2bf", hcc)], dma="c_w2o")
    p.emit("sync", lambda e: e.dma_start(out=stg[0:1, 3 * C:4 * C], in_=wn["fc2_b"][None, :]),
           writes=[("w_stage", "r_fc2_b")], dma="c_pr")
    p.emit("vector", lambda e: e.tensor_copy(out=t("rows2")[0:1, C:2 * C], in_=stg[0:1, 3 * C:4 * C]),
           reads=[("w_stage", "r_fc2_b")], writes=[("rows2", "fc2_b")])

    def fc2bank(m):
        return pw[m % 4][:, 0:TT], (PB(m % 4),)

    def pass2b_head(ti):
        p.curpar[0] = ti % 2
        par = ti % 2
        PB0, PB1 = 2 * par, 2 * par + 1
        t0 = ti * TT
        a2m = t("a2m")
        p.emit("sync", lambda e: e.dma_start(
            out=a2m[:, :, :], in_=a2sp[ti]),
            writes=["a2m"], dma=f"c_s2{ti % 2}")
        a2tk = ["a2m"]
        if ti == 0:
            Lk, Lap = ["hpar"], lambda cc: t("hpar")[:, 0, cc:cc + 1]
        else:
            Lk = [("a2eg", ti - 1, 1)]
            Lap = lambda cc: t("a2eg")[:, ti - 1, cc, 1:2]
        if ti == NTILE - 1:
            Rk, Rap = ["hpar"], lambda cc: t("hpar")[:, 1, cc:cc + 1]
        else:
            Rk = [("a2eg", ti + 1, 0)]
            Rap = lambda cc: t("a2eg")[:, ti + 1, cc, 0:1]
        p.emit("sync", lambda e: e.dma_start(
            out=t("y0l")[:, :, :], in_=y0sp[t0:t0 + TT, :].rearrange("(g p) c -> p g c", p=P)),
            writes=["y0l"], dma=f"c_y0{ti % 2}")
        cv = t("cvs")
        tp = taps
        for cc in range(NCH):
            p.emit("vector", lambda e, cc=cc: e.tensor_scalar(
                out=cv[:, cc, 1:TT], in0=a2m[:, cc, 0:TT - 1], scalar1=tp[:, 3, cc, 0:1],
                scalar2=tp[:, 3, cc, 3:4], op0=ALU.mult, op1=ALU.add),
                reads=a2tk + [("taps", 3, "w"), ("taps", 3, "b")], writes=[("cvs", cc)])
            p.emit("vector", lambda e, cc=cc: e.tensor_scalar(
                out=cv[:, cc, 0:1], in0=Lap(cc), scalar1=tp[:, 3, cc, 0:1],
                scalar2=tp[:, 3, cc, 3:4], op0=ALU.mult, op1=ALU.add),
                reads=Lk + [("taps", 3, "w"), ("taps", 3, "b")], writes=[("cvs", cc)])
            p.emit("vector", lambda e, cc=cc: e.scalar_tensor_tensor(
                out=cv[:, cc, 0:TT - 1], in0=a2m[:, cc, 1:TT], scalar=tp[:, 3, cc, 2:3],
                in1=cv[:, cc, 0:TT - 1], op0=ALU.mult, op1=ALU.add),
                reads=a2tk + [("taps", 3, "w"), ("cvs", cc)], writes=[("cvs", cc)])
            p.emit("vector", lambda e, cc=cc: e.scalar_tensor_tensor(
                out=cv[:, cc, TT - 1:TT], in0=Rap(cc), scalar=tp[:, 3, cc, 2:3],
                in1=cv[:, cc, TT - 1:TT], op0=ALU.mult, op1=ALU.add),
                reads=Rk + [("taps", 3, "w"), ("cvs", cc)], writes=[("cvs", cc)])
            p.emit("vector", lambda e, cc=cc: e.scalar_tensor_tensor(
                out=t("a3")[:, cc, :], in0=a2m[:, cc, :], scalar=tp[:, 3, cc, 1:2],
                in1=cv[:, cc, :], op0=ALU.mult, op1=ALU.add),
                reads=a2tk + [("taps", 3, "w"), ("cvs", cc)], writes=[("a3", cc)])
        a3k = [("a3", cc) for cc in range(NCH)]
        p.emit("gpsimd", lambda e: e.tensor_copy(
            out=t("a3bf")[:, :, :],
            in_=t("a3")[:, :, :]),
            reads=a3k, writes=["a3bf"])
        p.emit("scalar", lambda e: e.activation(
            out=t("sqbf")[:, :, 0:TT],
            in_=t("a3")[:, :, :], func=AF.Square),
            reads=a3k, writes=["sqbf"])
        ln_fm(t("a3bf"), ["a3bf"], "sqbf", TT, t("a3"), 0, a3k, t("xn2"), "xn2", PB0)
        xn2k = [("xn2", cc) for cc in range(NCH)]
        return

    def mlp_joint(ta, tb):
        assert 2 * TT <= 512
        for r in range(2):
            for hh in range(16):
                hcc = 16 * r + hh
                sl = hh % 4
                p.curpar[0] = 0
                p.emit("sync", lambda e, hcc=hcc, sl=sl: e.dma_start(
                    out=t("w1s")[:, sl, :, :], in_=fc1bf[hcc]),
                    reads=[("fc1bf", cc2) for cc2 in range(NCH)],
                    writes=[("w1s", sl)], dma=f"c_f1{sl}")
                for par in (0, 1):
                    p.curpar[0] = par
                    PB0 = 2 * par
                    slp = PB0 + (hh % 2)
                    def mmf1(e, hcc=hcc, sl=sl, slp=slp):
                        last = None
                        for cc in range(NCH):
                            last = e.matmul(pw[slp][:, 0:TT], t("w1s")[:, sl, cc, :],
                                            t("xn2")[:, cc, :], start=(cc == 0), stop=False)
                        return e.matmul(pw[slp][:, 0:TT],
                                        t("r_fc1b")[0:1, hcc * P:(hcc + 1) * P],
                                        t("ones_row")[0:1, 0:TT], start=False, stop=True)
                    p.emit("tensor", mmf1,
                           reads=[("xn2", cc) for cc in range(NCH)] +
                                 [("w1s", sl), ("r_fc1b",), "ones_row"],
                           writes=[(PB(slp),)])
                    p.emit("scalar", lambda e, hh=hh, slp=slp: e.activation(
                        out=t("g1")[:, hh, :], in_=pw[slp][:, 0:TT], func=AF.Gelu),
                        reads=[(PB(slp),)], writes=[("g1", hh)])
            for mh in range(2):
                for par in (0, 1):
                    p.curpar[0] = par
                    PB0, PB1 = 2 * par, 2 * par + 1
                    def f2open(e, PB0=PB0, PB1=PB1):
                        e.matmul(pw[PB0][:, 0:512], t("zrow")[0:1, :],
                                 t("ones_row")[0:1, 0:512], start=True, stop=False)
                        return e.matmul(pw[PB1][:, 0:512], t("zrow")[0:1, :],
                                        t("ones_row")[0:1, 0:512], start=True, stop=False)
                    p.emit("tensor", f2open, reads=["zrow", "ones_row"],
                           writes=[(PB(PB0),), (PB(PB1),)])
                for hh in range(16):
                    hcc = 16 * r + hh
                    sl = hh % 4
                    p.curpar[0] = 0
                    p.emit("sync", lambda e, hcc=hcc, sl=sl, mh=mh: e.dma_start(
                        out=t("w2s")[:, sl, 0:512],
                        in_=fc2bf[hcc, :, mh * 512:(mh + 1) * 512]),
                        reads=[("fc2bf", hcc)], writes=[("w2s", sl)], dma=f"c_f2{sl}")
                    for par in (0, 1):
                        p.curpar[0] = par
                        PB0, PB1 = 2 * par, 2 * par + 1
                        def mmf2(e, hh=hh, sl=sl, PB0=PB0):
                            last = None
                            for mm in range(4):
                                bnk = pw[PB0 + mm // 2]
                                last = e.matmul(bnk[:, (mm % 2) * TT:(mm % 2 + 1) * TT],
                                                t("w2s")[:, sl, mm * P:(mm + 1) * P],
                                                t("g1")[:, hh, :], start=False, stop=False)
                            return last
                        p.emit("tensor", mmf2, reads=[("g1", hh), ("w2s", sl)],
                               writes=[(PB(PB0),), (PB(PB1),)])
                for par in (0, 1):
                    p.curpar[0] = par
                    PB0, PB1 = 2 * par, 2 * par + 1
                    def f2close(e, r=r, mh=mh, PB0=PB0, PB1=PB1):
                        if r == 0:
                            for mm in range(4):
                                m = mh * 4 + mm
                                bnk = pw[PB0 + mm // 2]
                                e.matmul(bnk[:, (mm % 2) * TT:(mm % 2 + 1) * TT],
                                         t("rows2")[0:1, C + m * P:C + (m + 1) * P],
                                         t("ones_row")[0:1, 0:TT], start=False, stop=False)
                        e.matmul(pw[PB0][:, 0:512], t("zrow")[0:1, :],
                                 t("ones_row")[0:1, 0:512], start=False, stop=True)
                        return e.matmul(pw[PB1][:, 0:512], t("zrow")[0:1, :],
                                        t("ones_row")[0:1, 0:512], start=False, stop=True)
                    p.emit("tensor", f2close,
                           reads=[("rows2", "fc2_b"), "ones_row", "zrow",
                                  (PB(PB0),), (PB(PB1),)],
                           writes=[(PB(PB0),), (PB(PB1),)])
                    for mm in range(4):
                        m = mh * 4 + mm
                        p.emit("scalar", lambda e, m=m, mm=mm, PB0=PB0: e.activation(
                            out=t("evs")[:, mm % 2, 0:TT],
                            in_=pw[PB0 + mm // 2][:, (mm % 2) * TT:(mm % 2 + 1) * TT],
                            func=AF.Identity),
                            reads=[(PB(PB0 + mm // 2),)], writes=[("evs", mm % 2)])
                        p.emit("gpsimd", lambda e, m=m, mm=mm: e.tensor_tensor(
                            out=t("a3")[:, m, :], in0=t("a3")[:, m, :],
                            in1=t("evs")[:, mm % 2, 0:TT], op=ALU.add),
                            reads=[("evs", mm % 2), ("a3", m)], writes=[("a3", m)])
        p.curpar[0] = 0

    def pass2b_exit(ti):
        p.curpar[0] = ti % 2
        par = ti % 2
        PB0, PB1 = 2 * par, 2 * par + 1
        t0 = ti * TT
        a3k = [("a3", cc) for cc in range(NCH)]
        for g in range(NG):
            def trx(e, g=g):
                last = None
                for cc in range(NCH):
                    bank = pw[PB0 + cc // 4]
                    last = e.transpose(bank[:, (cc % 4) * P:(cc % 4 + 1) * P],
                                       t("a3")[:, cc, g * P:(g + 1) * P], t("ident_f")[:, :])
                return last
            p.emit("tensor", trx, reads=a3k + ["ident_f"],
                   writes=[(PB(PB0),), (PB(PB1),)])
            for half in range(2):
                p.emit("scalar", lambda e, half=half: e.activation(
                    out=t("evs")[:, half, :], in_=pw[PB0 + half][:, 0:512],
                    func=AF.Identity),
                    reads=[(PB(PB0 + half),)], writes=[("evs", half)])
                p.emit("gpsimd", lambda e, g=g, half=half: e.tensor_tensor(
                    out=t("y0l")[:, g, half * 512:(half + 1) * 512],
                    in0=t("y0l")[:, g, half * 512:(half + 1) * 512],
                    in1=t("evs")[:, half, :], op=ALU.add),
                    reads=["y0l", ("evs", half)], writes=[("y0l", g, half)])
        p.emit("sync", lambda e: e.dma_start(
            out=out_my[t0:t0 + TT, :].rearrange("(g p) c -> p g c", p=P), in_=t("y0l")[:, :, :]),
            reads=[("y0l", g, h) for g in range(NG) for h in range(2)], dma=f"c_o{ti % 2}")

    for tj in range(NTILE // 2):
        pair_run(pass2b_head, 2 * tj, 2 * tj + 1)
        mlp_joint(2 * tj, 2 * tj + 1)
        pair_run(pass2b_exit, 2 * tj, 2 * tj + 1)

    p.barrier_on("gpsimd", [(ch, p.count[ch]) for ch in ("c_o0", "c_o1") if ch in p.count])

    with nc.Block() as block:
        p.seal(block)
    es.close()
    return nc


def kernel(**inputs):
    B, L, Cc = 4, 8192, 1024
    NT = L // 2
    HL = 4
    x = np.asarray(inputs["L0_attn_x"], dtype=np.float32)
    nc = build_program(NT=NT, TT=256)

    wmap = {
        "cpe1_w": inputs["L0_attn_cpe1_w"].reshape(Cc, 3), "cpe1_b": inputs["L0_attn_cpe1_b"],
        "norm1_g": inputs["L0_attn_norm1_g"], "norm1_b": inputs["L0_attn_norm1_b"],
        "in_w": inputs["L0_attn_in_w"], "in_b": inputs["L0_attn_in_b"],
        "actp_w": inputs["L0_attn_actp_w"], "actp_b": inputs["L0_attn_actp_b"],
        "dwc_w": inputs["L0_attn_dwc_w"].reshape(Cc, 3), "dwc_b": inputs["L0_attn_dwc_b"],
        "qk_w": inputs["L0_attn_qk_w"], "qk_b": inputs["L0_attn_qk_b"],
        "lepe_w": inputs["L0_attn_lepe_w"].reshape(Cc, 3), "lepe_b": inputs["L0_attn_lepe_b"],
        "out_w": inputs["L0_attn_out_w"], "out_b": inputs["L0_attn_out_b"],
        "cpe2_w": inputs["L0_attn_cpe2_w"].reshape(Cc, 3), "cpe2_b": inputs["L0_attn_cpe2_b"],
        "norm2_g": inputs["L0_attn_norm2_g"], "norm2_b": inputs["L0_attn_norm2_b"],
        "fc1_w": inputs["L0_attn_fc1_w"], "fc1_b": inputs["L0_attn_fc1_b"],
        "fc2_w": inputs["L0_attn_fc2_w"], "fc2_b": inputs["L0_attn_fc2_b"],
        "fWin": inputs["L0_ffn_Win"], "fbin": inputs["L0_ffn_bin_"],
        "fW1": inputs["L0_ffn_W1"], "fb1": inputs["L0_ffn_b1"],
        "fW2": inputs["L0_ffn_W2"], "fb2": inputs["L0_ffn_b2"],
        "fWout": inputs["L0_ffn_Wout"], "fbout": inputs["L0_ffn_bout"],
    }
    wmap = {k: np.ascontiguousarray(np.asarray(v, dtype=np.float32)) for k, v in wmap.items()}

    ar = np.arange(128, dtype=np.int32)
    in_maps = []
    for core in range(8):
        b, half = core // 2, core % 2
        xm = np.zeros((NT + 2 * HL, Cc), dtype=np.float32)
        s = half * NT
        lo, hi = s - HL, s + NT + HL
        clo, chi = max(lo, 0), min(hi, L)
        xm[clo - lo:clo - lo + (chi - clo)] = x[b, clo:chi]
        lr = np.zeros((128, 2), dtype=np.float32)
        lr[:, 0] = 1.0 if half == 1 else 0.0
        lr[:, 1] = 1.0 if half == 0 else 0.0
        m = {"x_my": xm, "lrmask": lr,
             "kv_idx": (np.arange(65, dtype=np.int32) + 65 * half).reshape(65, 1),
             "fkv_idx": np.array([[half], [half]], dtype=np.int32),
             "fa2_idx": np.array([[2 + half], [2 + half]], dtype=np.int32)}
        if half == 0:
            m["a2a_idx"] = (ar + 2 * 128).reshape(128, 1)
            m["a2b_idx"] = (ar + 0).reshape(128, 1)
            m["haloA_idx"] = (ar + 2 * 128).reshape(128, 1)
            m["haloB_idx"] = (ar + 128).reshape(128, 1)
        else:
            m["a2a_idx"] = (ar + 128).reshape(128, 1)
            m["a2b_idx"] = (ar + 2 * 128).reshape(128, 1)
            m["haloA_idx"] = (ar + 0).reshape(128, 1)
            m["haloB_idx"] = (ar + 2 * 128).reshape(128, 1)
        m.update(wmap)
        in_maps.append(m)

    res = run_bass_kernel_spmd(nc, in_maps, core_ids=list(range(8)))
    out = np.empty((B, L, Cc), dtype=np.float32)
    for core in range(8):
        b, half = core // 2, core % 2
        out[b, half * NT:(half + 1) * NT] = np.asarray(res.results[core]["out_my"])
    return out


# version 09
# speedup vs baseline: 1.4010x; 1.1904x over previous
import contextlib
import math

import numpy as np
import concourse.bass as bass
import concourse.mybir as mybir
from concourse.bass_utils import run_bass_kernel_spmd
from concourse.bass_core_rust import IndirectOffsetOnAxis

F32 = mybir.dt.float32
BF16 = mybir.dt.bfloat16
I32 = mybir.dt.int32
AF = mybir.ActivationFunctionType
ALU = mybir.AluOpType
AX = mybir.AxisListType

P = 128
C = 1024
NCH = 8
H = 16
HID = 4096
NHID = 32
MAGIC = 0x12345678
LN1E4 = math.log(10000.0)

ENGINES = ("sync", "vector", "scalar", "tensor", "gpsimd")


class Prog:

    def __init__(self, nc):
        self.nc = nc
        self.sems = {}
        self.queue = {e: [] for e in ENGINES}
        self.count = {}
        self.last_writer = {}
        self.readers = {}
        self.clock = {e: {} for e in ENGINES}
        self.es = None
        self.arena_of = {}
        self.dup = set()
        self.curpar = [None]
        self.recording = False
        self.rec = []

    def _resolve(self, base):
        if base in self.dup:
            par = self.curpar[0]
            assert par is not None, f"parity tensor {base} used outside tile context"
            return f"{base}~{par}"
        return base

    def init_sems(self, es):
        self.es = es
        for e in ENGINES:
            self.sems[e] = es.enter_context(self.nc.semaphore(f"d_{e}"))
            self.count[e] = 0

    def chan(self, name):
        if name not in self.sems:
            self.sems[name] = self.es.enter_context(self.nc.semaphore(name))
            self.count[name] = 0
        return self.sems[name]

    def _expand(self, keys):
        out = []
        for k in keys:
            if isinstance(k, tuple):
                base = self._resolve(k[0])
                k = (base,) + tuple(k[1:])
            else:
                base = self._resolve(k)
                k = base
            out.append(k)
            ar = self.arena_of.get(base)
            if ar is None:
                continue
            if isinstance(ar, (list, tuple)):
                for a in ar:
                    out.append(("@", a))
            else:
                out.append(("@", ar))
        return out

    def emit(self, engine, fn, reads=(), writes=(), dma=None):
        if self.recording:
            self.rec.append((engine, fn, tuple(reads), tuple(writes), dma,
                             self.curpar[0]))
            return None
        reads = self._expand(reads)
        writes = self._expand(writes)
        need = {}

        def _need(s, v):
            if v > need.get(s, 0):
                need[s] = v

        for k in reads:
            w = self.last_writer.get(k)
            if w is not None:
                _need(*w)
        for k in writes:
            w = self.last_writer.get(k)
            if w is not None:
                _need(*w)
            for s, v in self.readers.get(k, {}).items():
                _need(s, v)

        myclock = self.clock[engine]
        waits = []
        for s, v in sorted(need.items(), key=lambda kv: str(kv[0])):
            if myclock.get(s, 0) >= v:
                continue
            waits.append((s, v))
            myclock[s] = v

        if dma is None:
            sem_name, inc = engine, 1
        else:
            self.chan(dma)
            sem_name, inc = dma, 16
        self.count[sem_name] += inc
        cnt = self.count[sem_name]
        sems = self.sems
        sem = sems[sem_name]

        par = self.curpar[0]

        def run(eng):
            self.curpar[0] = par
            for (s, v) in waits:
                eng.wait_ge(sems[s], v)
            last = fn(eng)
            assert last is not None, f"emit fn on {engine} returned None"
            last.then_inc(sem, inc)

        self.queue[engine].append(run)
        for k in reads:
            self.readers.setdefault(k, {})[sem_name] = cnt
        for k in writes:
            self.last_writer[k] = (sem_name, cnt)
            self.readers[k] = {}
        return (sem_name, cnt)

    def barrier_on(self, engine, deps):
        sems = self.sems
        myclock = self.clock[engine]
        waits = []
        for s, v in deps:
            if myclock.get(s, 0) >= v:
                continue
            waits.append((s, v))
            myclock[s] = v

        def run(eng):
            for (s, v) in waits:
                eng.wait_ge(sems[s], v)

        self.queue[engine].append(run)

    def record(self):
        self.recording = True
        self.rec = []

    def stop_record(self):
        self.recording = False
        out = self.rec
        self.rec = []
        return out

    def commit(self, item):
        engine, fn, reads, writes, dma, par = item
        sav = self.curpar[0]
        self.curpar[0] = par
        self.emit(engine, fn, reads=reads, writes=writes, dma=dma)
        self.curpar[0] = sav

    def seal(self, block):
        for e in ENGINES:
            q = self.queue[e]
            if not q:
                continue
            section = getattr(block, e)

            def make(q=q):
                def _(eng):
                    for run in q:
                        run(eng)
                return _

            section(make())


def splits(s, e_, mx=512):
    o = s
    while o < e_:
        nn = min(mx, e_ - o)
        yield (o, nn)
        o += nn


def build_program(NT=4096, TT=512, dbg=None):
    HL = 4
    W = TT + 2 * HL
    NG = TT // P
    NTILE = NT // TT
    SEQ = 2 * NT
    T2, T4, TH, W2 = TT + 2, TT + 4, TT // 2, TT + 2 * HL - 2
    LNW = ((T4 + 31) & ~31) + 16
    assert TT % P == 0 and NT % TT == 0

    nc = bass.Bass("TRN2", target_bir_lowering=False, detect_race_conditions=False)

    x_my = nc.dram_tensor("x_my", [NT + 2 * HL, C], F32, kind="ExternalInput").ap()
    out_my = nc.dram_tensor("out_my", [NT, C], F32, kind="ExternalOutput").ap()
    wn = {}
    for nm, shp in [
        ("cpe1_w", [C, 3]), ("cpe1_b", [C]), ("norm1_g", [C]), ("norm1_b", [C]),
        ("in_w", [C, C]), ("in_b", [C]), ("actp_w", [C, C]), ("actp_b", [C]),
        ("dwc_w", [C, 3]), ("dwc_b", [C]), ("qk_w", [C, 2 * C]), ("qk_b", [2 * C]),
        ("lepe_w", [C, 3]), ("lepe_b", [C]), ("out_w", [C, C]), ("out_b", [C]),
        ("cpe2_w", [C, 3]), ("cpe2_b", [C]), ("norm2_g", [C]), ("norm2_b", [C]),
        ("fc1_w", [C, HID]), ("fc1_b", [HID]), ("fc2_w", [HID, C]), ("fc2_b", [C]),
        ("fWin", [C, 64]), ("fbin", [64]), ("fW1", [64, 16]), ("fb1", [64, 16]),
        ("fW2", [64, 16]), ("fb2", [64]), ("fWout", [64, C]), ("fbout", [C]),
    ]:
        wn[nm] = nc.dram_tensor(nm, list(shp), F32, kind="ExternalInput").ap()
    idx_in = {}
    for nm, n in (("kv_idx", 65), ("a2a_idx", P), ("a2b_idx", P), ("fkv_idx", 2),
                  ("fa2_idx", 2), ("haloA_idx", P), ("haloB_idx", P)):
        idx_in[nm] = nc.dram_tensor(nm, [n, 1], I32, kind="ExternalInput").ap()
    lrmask_in = nc.dram_tensor("lrmask", [P, 2], F32, kind="ExternalInput").ap()

    a1sp = nc.dram_tensor("a1sp", [NTILE, P, NCH, TT], F32).ap()
    a2sp = nc.dram_tensor("a2sp", [NTILE, P, NCH, TT], F32).ap()
    y0sp = nc.dram_tensor("y0sp", [NT, C], F32).ap()
    qrsp = nc.dram_tensor("qrsp", [NTILE, P, NCH, TT], BF16).ap()
    arsp = nc.dram_tensor("arsp", [NTILE, P, NCH, TT], BF16).ap()
    lpsp = nc.dram_tensor("lpsp", [NTILE, P, NCH, TT], BF16).ap()
    hrow = nc.dram_tensor("hrow", [3 * P, NCH], F32).ap()
    fc1bf = nc.dram_tensor("fc1bf", [NHID, P, NCH, P], BF16).ap()
    fc2bf = nc.dram_tensor("fc2bf", [NHID, P, C], BF16).ap()
    qkbf = nc.dram_tensor("qkbf", [16, P, NCH, P], BF16).ap()
    kmb = nc.dram_tensor("kmb", [C], F32).ap()
    S_kv = nc.dram_tensor("S_kv", [130, C], F32, addr_space="Shared").ap()
    S_a2 = nc.dram_tensor("S_a2", [3 * P, NCH], F32, addr_space="Shared").ap()
    S_fl = nc.dram_tensor("S_fl", [8, 16], I32, addr_space="Shared").ap()

    es = contextlib.ExitStack()
    BUDGET = 207 * 1024
    arena = nc.alloc_sbuf_tensor("ARENA", [P, BUDGET // 4], F32)
    abase = nc.lookup_mloc(arena).addr
    cur = [0]
    tens = {}
    arena_of = {}
    groups = {}

    def _nbytes(shape, dtype):
        esz = {F32: 4, BF16: 2, I32: 4}[dtype]
        n = esz
        for s in shape[1:]:
            n *= s
        return (n + 63) & ~63

    def group(gname, nbytes):
        nbytes = (nbytes + 63) & ~63
        groups[gname] = (abase + cur[0], nbytes)
        cur[0] += nbytes
        assert cur[0] <= BUDGET, f"SBUF over budget at group {gname}: {cur[0]}"

    def alloc(name, shape, dtype, g=None, suboff=0):
        n = _nbytes(shape, dtype)
        if g is None:
            addr = abase + cur[0]
            cur[0] += n
            assert cur[0] <= BUDGET, f"SBUF over budget at {name}: {cur[0]}"
            arena_of[name] = name
        else:
            gaddr, gsize = groups[g]
            assert suboff + n <= gsize + 63, f"{name} exceeds group {g}"
            addr = gaddr + suboff
            arena_of[name] = g
        tens[name] = nc.alloc_sbuf_tensor_at(name, list(shape), dtype, offset=addr)
        return tens[name]

    DUPG = []

    def dgroup(gname, nbytes):
        DUPG.append(gname)
        group(gname + "~0", nbytes)
        group(gname + "~1", nbytes)

    dgroup("GA", max(NCH * W * 4, 16 * TT * 2))
    dgroup("GB", max(NCH * (TT + 2) * 4, 16 * TT * 2))
    dgroup("GC", NG * C * 4)
    dgroup("G1", NCH * W * 2)
    dgroup("G2", NCH * T4 * 2)
    dgroup("G3", NCH * T4 * 2)
    dgroup("G4", max(NCH * T4 * 2, 16 * TH * 2))
    dgroup("GLN", (3 * LNW + max(LNW, 512)) * 4)
    dgroup("GKV", 4 * TT * 4)
    dgroup("GSC", 2 * TT * 2)
    group("GR1", 8 * 1024)

    DUPN = set()

    def dalloc(name, shape, dtype, g=None, suboff=0):
        DUPN.add(name)
        for par in range(2):
            gg = (g + f"~{par}") if g is not None and g in DUPG else g
            alloc(f"{name}~{par}", shape, dtype, gg, suboff)

    dalloc("a1f", [P, NCH, W], F32)
    alloc("w_in", [P, NCH, C], BF16)
    alloc("w_actp", [P, NCH, C], BF16)
    alloc("ident_b", [P, P], BF16)
    alloc("ident_f", [P, P], F32)
    alloc("maskU", [P, P], BF16)
    alloc("maskL", [P, P], BF16)
    alloc("Rm", [P, NCH, P], BF16)
    alloc("ones_row", [1, 512], BF16)
    alloc("zrow", [1, P], BF16)
    alloc("ones_col", [P, 1], BF16)
    alloc("invC_col", [1, P], F32)
    alloc("taps", [P, 4, NCH, 4], F32)
    alloc("cst", [P, 40], F32)
    alloc("crope_col", [P, NCH], F32)
    alloc("halfpi", [P, 1], F32)
    alloc("eps5", [P, 1], F32)
    alloc("magic", [2, 16], I32)
    alloc("fbuf", [1, 64], I32)
    alloc("idx", [P, 8], I32)
    alloc("lrmask", [P, 2], F32)
    alloc("zero8", [P, 3, NCH], F32)
    alloc("g_cols", [P, 2, NCH], F32)
    alloc("b_cols", [P, 2, NCH], F32)
    alloc("b_colsb", [P, 2, NCH], BF16)
    alloc("rows2", [1, 2 * C], BF16)
    alloc("binrow", [1, 64], BF16)
    alloc("ffn_c", [P, 52], F32)
    alloc("w_Win", [P, NCH, 64], BF16)
    alloc("w_Wout", [P, C], BF16)
    dalloc("s2g2", [P, 2, TH], BF16)
    alloc("usq", [P, C], BF16)
    dalloc("stats", [P, 16], F32)
    dalloc("red1", [P, 1], F32)
    alloc("ksum", [P, NCH], F32)
    alloc("kv_raw", [65, C], F32)
    alloc("kv_cmb", [P, C], F32)
    alloc("kv_bd", [P, NCH, P], BF16)
    alloc("kmr_bd", [P, NCH, 16], BF16)
    alloc("km_col", [P, NCH], F32)
    alloc("km_bf", [P, NCH], BF16)
    alloc("a2keep", [P, 2, NCH], F32)
    alloc("a2rd", [P, 2, NCH], F32)
    alloc("a2eg", [P, NTILE, NCH, 2], F32)
    alloc("hpar", [P, 2, NCH], F32)
    dalloc("evs", [P, 2, 512], F32)
    dalloc("u_fm", [P, NCH, W], F32, "GA")
    dalloc("xi0", [P, NCH, T4], BF16, "GA")
    dalloc("qke", [P, 16, TT], BF16, "GA")
    dalloc("a1l", [P, NCH, TT], F32, "GA")
    dalloc("y0l", [P, NG, C], F32, "GA")
    dalloc("u_tm", [P, NG, C], F32, "GB")
    dalloc("y0t", [P, NG, C], F32, "GB")
    dalloc("krtvh", [P, 2, NG, C], BF16, "GB")
    dalloc("a2w", [P, NCH, TT], F32, "GB")
    dalloc("a2m", [P, NCH, TT], F32, "GB")
    dalloc("g1", [P, 16, TT], BF16, "GB")
    dalloc("x0", [P, NG, C], F32, "GC")
    dalloc("qro", [P, NCH, TT], BF16, "GC")
    dalloc("kro", [P, NCH, TT], BF16, "GC", suboff=NCH * TT * 2)
    dalloc("a3", [P, NCH, TT], F32, "GC")
    dalloc("u_bf", [P, NCH, W], BF16, "G1")
    dalloc("xed", [P, C], F32, "G1")
    dalloc("cscr", [P, NCH, W2], BF16, "G1")
    dalloc("lepe_sb", [P, NCH, TT], BF16, "G1")
    dalloc("qr2", [P, NCH, TT], BF16, "G1")
    dalloc("cvs", [P, NCH, TT], BF16, "G1")
    dalloc("a1bf", [P, NCH, T4], BF16, "G2")
    dalloc("xi", [P, NCH, T2], BF16, "G2")
    dalloc("ar2", [P, NCH, TT], BF16, "G2")
    dalloc("a3bf", [P, NCH, TT], BF16, "G2")
    dalloc("sqbf", [P, NCH, T4], BF16, "G3")
    dalloc("acr", [P, NCH, TT], BF16, "G3")
    dalloc("lp2", [P, NCH, TT], BF16, "G3")
    dalloc("hscr", [P, 16, TH], BF16, "G4")
    dalloc("xn1", [P, NCH, T4], BF16, "G4")
    dalloc("aab", [P, NCH, TT], BF16, "G4")
    dalloc("xn2", [P, NCH, TT], BF16, "G4")
    dalloc("mur", [P, LNW], F32, "GLN")
    dalloc("nmu2", [P, LNW], F32, "GLN", suboff=LNW * 4)
    dalloc("rstd", [P, LNW], F32, "GLN", suboff=2 * LNW * 4)
    dalloc("lscr", [P, max(LNW, 512)], F32, "GLN", suboff=3 * LNW * 4)
    dalloc("srow", [1, 2, LNW], F32, "GLN")
    dalloc("zsh", [16, TT], F32, "GKV")
    dalloc("zscr", [P, TT], F32, "GKV", suboff=TT * 4)
    dalloc("zbf", [16, TT], BF16, "GKV", suboff=2 * TT * 4)
    dalloc("zae", [P, TT], BF16, "GKV", suboff=2 * TT * 4 + TT * 2)
    dalloc("lab", [P, TT], BF16, "GKV", suboff=3 * TT * 4)
    dalloc("scab", [P, 2, TT], BF16, "GSC")
    alloc("r_qkb", [1, 2 * C], BF16, "GR1")
    alloc("r_inb", [1, C], BF16, "GR1", suboff=2 * C * 2)
    alloc("r_actb", [1, C], BF16, "GR1", suboff=3 * C * 2)
    alloc("r_fc1b", [1, HID], BF16, "GR1")
    alloc("selz", [16, NCH * P], BF16, "GR1", suboff=3 * C * 2)
    a1f_addr = nc.lookup_mloc(tens["a1f~0"]).addr
    tens["w_out"] = nc.alloc_sbuf_tensor_at("w_out", [P, NCH, C], BF16, offset=a1f_addr)
    arena_of["w_out"] = ["a1f~0", "a1f~1"]
    gc_addr = groups["GC~0"][0]
    tens["w_stage"] = nc.alloc_sbuf_tensor_at("w_stage", [P, 4, C], F32, offset=gc_addr)
    arena_of["w_stage"] = ["GC~0", "GC~1"]
    w_in_addr = nc.lookup_mloc(tens["w_in"]).addr
    tens["w1s"] = nc.alloc_sbuf_tensor_at("w1s", [P, 4, NCH, P], BF16, offset=w_in_addr)
    tens["w2s"] = nc.alloc_sbuf_tensor_at("w2s", [P, 4, C], BF16,
                                          offset=w_in_addr + 4 * NCH * P * 2)
    alloc("w3s", [P, 4, NCH, P], BF16)
    arena_of["w1s"] = "WP1"
    arena_of["w2s"] = "WP1"
    for nm2 in ("w_in", "w_actp"):
        arena_of[nm2] = "WP1"
    used_kb = cur[0] / 1024

    pw = [es.enter_context(nc.psum_tensor(f"pw{i}", [P, 512], F32)) for i in range(4)]
    pt4 = es.enter_context(nc.psum_tensor("pt4", [P, 1024], BF16))
    pt5 = es.enter_context(nc.psum_tensor("pt5", [P, 1024], BF16))
    pkv = es.enter_context(nc.psum_tensor("pkv", [P, 1024], F32))
    for i in range(4):
        arena_of[f"pw{i}"] = f"pw{i}"
    arena_of["pkv"] = "pkv"
    arena_of["pt4"] = "pt4"
    arena_of["pt5"] = "pt5"

    p = Prog(nc)
    p.arena_of = arena_of
    p.dup = DUPN
    p.init_sems(es)
    poll1 = es.enter_context(nc.semaphore("poll1"))
    poll2 = es.enter_context(nc.semaphore("poll2"))

    t = lambda nm: tens[p._resolve(nm)]
    p.curpar[0] = 0
    PB = lambda i: f"pw{i}"

    for nm, b in (("ident_b", 0), ("ident_f", 0), ("maskU", 1), ("maskL", -1)):
        tt = t(nm)
        p.emit("gpsimd", lambda e, tt=tt: e.memset(tt[:, :], 0.0), writes=[nm])
        p.emit("gpsimd", lambda e, tt=tt, b=b: e.affine_select(
            out=tt[:, :], in_=tt[:, :], compare_op=ALU.not_equal, fill=1.0,
            base=b, pattern=[[-1, P]], channel_multiplier=1),
            reads=[nm], writes=[nm])
    for nm, val in (("ones_row", 1.0), ("ones_col", 1.0), ("zrow", 0.0), ("invC_col", 1.0 / C),
                    ("halfpi", float(np.pi / 2)), ("zero8", 0.0), ("ksum", 0.0),
                    ("eps5", 1e-5)):
        tt = t(nm)
        p.emit("vector", lambda e, tt=tt, val=val: e.memset(tt[:], val), writes=[nm])
    p.emit("vector", lambda e: e.memset(t("magic")[:, :], MAGIC), writes=["magic"])
    p.emit("vector", lambda e: e.memset(t("fbuf")[:, :], 0), writes=["fbuf"])
    ix = t("idx")
    for col, (nm, n) in enumerate((("kv_idx", 65), ("a2a_idx", P), ("fkv_idx", 2),
                                   ("fa2_idx", 2), ("haloA_idx", P), ("haloB_idx", P),
                                   ("a2b_idx", P))):
        p.emit("sync", lambda e, col=col, nm=nm, n=n: e.dma_start(
            out=ix[0:n, col:col + 1], in_=idx_in[nm]), writes=[("idx", col)], dma="c_pr")
    p.emit("sync", lambda e: e.dma_start(out=t("lrmask")[:, :], in_=lrmask_in),
           writes=["lrmask"], dma="c_pr")

    cs = t("cst")
    p.emit("gpsimd", lambda e: e.memset(t("mur")[:, 0:64], 0.0), writes=[("mur", "pp")])
    p.emit("gpsimd", lambda e: e.affine_select(
        out=t("mur")[:, 0:64], in_=t("mur")[:, 0:64], compare_op=ALU.not_equal,
        fill=1.0, base=0, pattern=[[-2, 64]], channel_multiplier=1),
        reads=[("mur", "pp")], writes=[("mur", "pp")])
    p.emit("vector", lambda e: e.reduce_sum(out=cs[:, 17:18], in_=t("mur")[:, 0:64], axis=AX.X),
           reads=[("mur", "pp")], writes=[("cst", "even")])
    p.emit("vector", lambda e: e.tensor_scalar(out=cs[:, 16:17], in0=cs[:, 17:18],
                                               scalar1=-2.0, scalar2=1.0, op0=ALU.mult, op1=ALU.add),
           reads=[("cst", "even")], writes=[("cst", "sgn")])
    p.emit("gpsimd", lambda e: e.iota(cs[:, 18:19], pattern=[[0, 1]], base=0,
                                      channel_multiplier=1, allow_small_or_imprecise_dtypes=True),
           writes=[("cst", "p")])
    p.emit("vector", lambda e: e.tensor_scalar(out=cs[:, 19:20], in0=cs[:, 18:19],
                                               scalar1=0.5, scalar2=-0.25, op0=ALU.mult, op1=ALU.add),
           reads=[("cst", "p")], writes=[("cst", "fl0")])
    p.emit("vector", lambda e: e.scalar_tensor_tensor(out=cs[:, 20:21], in0=cs[:, 16:17],
                                                      scalar=-0.25, in1=cs[:, 19:20],
                                                      op0=ALU.mult, op1=ALU.add),
           reads=[("cst", "sgn"), ("cst", "fl0")], writes=[("cst", "fl")])
    p.emit("gpsimd", lambda e: e.iota(cs[:, 21:29], pattern=[[64, NCH]], base=0,
                                      channel_multiplier=0, allow_small_or_imprecise_dtypes=True),
           writes=[("cst", "jb")])
    p.emit("vector", lambda e: e.tensor_scalar(out=cs[:, 21:29], in0=cs[:, 21:29],
                                               scalar1=cs[:, 20:21], scalar2=None, op0=ALU.add),
           reads=[("cst", "jb"), ("cst", "fl")], writes=[("cst", "jm")])
    p.emit("scalar", lambda e: e.activation(out=cs[:, 29:37], in_=cs[:, 21:29],
                                            func=AF.Exp, scale=-LN1E4 / 512.0),
           reads=[("cst", "jm")], writes=[("cst", "th")])
    p.emit("scalar", lambda e: e.activation(out=cs[:, 0:8], in_=cs[:, 29:37],
                                            func=AF.Sin, bias=t("halfpi")[:, 0:1]),
           reads=[("cst", "th"), "halfpi"], writes=[("cst", "cos")])
    p.emit("scalar", lambda e: e.activation(out=cs[:, 8:16], in_=cs[:, 29:37], func=AF.Sin),
           reads=[("cst", "th")], writes=[("cst", "sin")])
    p.emit("vector", lambda e: e.scalar_tensor_tensor(out=t("crope_col")[:, :], in0=cs[:, 8:16],
                                                      scalar=cs[:, 16:17], in1=cs[:, 0:8],
                                                      op0=ALU.mult, op1=ALU.add),
           reads=[("cst", "cos"), ("cst", "sin"), ("cst", "sgn")], writes=["crope_col"])
    mr = t("mur")
    p.emit("vector", lambda e: e.tensor_scalar(out=mr[:, 64:72], in0=cs[:, 8:16],
                                               scalar1=cs[:, 16:17], scalar2=-0.5,
                                               op0=ALU.mult, op1=ALU.mult),
           reads=[("cst", "sin"), ("cst", "sgn")], writes=[("mur", "ss")])
    p.emit("vector", lambda e: e.scalar_tensor_tensor(out=mr[:, 72:80], in0=cs[:, 8:16],
                                                      scalar=0.5, in1=mr[:, 64:72],
                                                      op0=ALU.mult, op1=ALU.add),
           reads=[("cst", "sin"), ("mur", "ss")], writes=[("mur", "su")])
    p.emit("vector", lambda e: e.scalar_tensor_tensor(out=mr[:, 80:88], in0=cs[:, 8:16],
                                                      scalar=-0.5, in1=mr[:, 64:72],
                                                      op0=ALU.mult, op1=ALU.add),
           reads=[("cst", "sin"), ("mur", "ss")], writes=[("mur", "so")])
    Rm = t("Rm")
    for cc in range(NCH):
        p.emit("vector", lambda e, cc=cc: e.tensor_scalar(
            out=Rm[:, cc, :], in0=t("ident_b")[:, :], scalar1=cs[:, cc:cc + 1],
            scalar2=None, op0=ALU.mult),
            reads=["ident_b", ("cst", "cos")], writes=[("Rm", cc)])
        p.emit("vector", lambda e, cc=cc: e.scalar_tensor_tensor(
            out=Rm[:, cc, :], in0=t("maskU")[:, :], scalar=mr[:, 72 + cc:73 + cc],
            in1=Rm[:, cc, :], op0=ALU.mult, op1=ALU.add),
            reads=["maskU", ("mur", "su"), ("Rm", cc)], writes=[("Rm", cc)])
        p.emit("vector", lambda e, cc=cc: e.scalar_tensor_tensor(
            out=Rm[:, cc, :], in0=t("maskL")[:, :], scalar=mr[:, 80 + cc:81 + cc],
            in1=Rm[:, cc, :], op0=ALU.mult, op1=ALU.add),
            reads=["maskL", ("mur", "so"), ("Rm", cc)], writes=[("Rm", cc)])
    taps = t("taps")
    for i, (wnm, bnm, plus1) in enumerate((("cpe1_w", "cpe1_b", True),
                                           ("dwc_w", "dwc_b", False),
                                           ("lepe_w", "lepe_b", False),
                                           ("cpe2_w", "cpe2_b", True))):
        p.emit("sync", lambda e, i=i, wnm=wnm: e.dma_start(
            out=taps[:, i, :, 0:3], in_=wn[wnm].rearrange("(cc p) k -> p cc k", p=P)),
            writes=[("taps", i, "w")], dma="c_pr")
        p.emit("sync", lambda e, i=i, bnm=bnm: e.dma_start(
            out=taps[:, i, :, 3:4], in_=wn[bnm].rearrange("(cc p) -> p cc", p=P)[:, :, None],
            allow_slow_non_contiguous=True),
            writes=[("taps", i, "b")], dma="c_pr")
        if plus1:
            p.emit("vector", lambda e, i=i: e.tensor_scalar(
                out=taps[:, i, :, 1:2], in0=taps[:, i, :, 1:2], scalar1=1.0,
                scalar2=None, op0=ALU.add),
                reads=[("taps", i, "w")], writes=[("taps", i, "w")])
    for col, nm in ((0, "norm1_g"), (1, "norm2_g")):
        p.emit("sync", lambda e, col=col, nm=nm: e.dma_start(
            out=t("g_cols")[:, col, :], in_=wn[nm].rearrange("(cc p) -> p cc", p=P),
            allow_slow_non_contiguous=True),
            writes=[("g_cols", col)], dma="c_pr")
    for col, nm in ((0, "norm1_b"), (1, "norm2_b")):
        p.emit("sync", lambda e, col=col, nm=nm: e.dma_start(
            out=t("b_cols")[:, col, :], in_=wn[nm].rearrange("(cc p) -> p cc", p=P),
            allow_slow_non_contiguous=True),
            writes=[("b_cols", col)], dma="c_pr")
    p.emit("vector", lambda e: e.tensor_copy(out=t("b_colsb")[:, :, :], in_=t("b_cols")[:, :, :]),
           reads=[("b_cols", 0), ("b_cols", 1)], writes=["b_colsb"])

    fcn = t("ffn_c")
    for sl0, nm in ((0, "fW1"), (32, "fW2")):
        for hh in range(2):
            p.emit("sync", lambda e, sl0=sl0, nm=nm, hh=hh: e.dma_start(
                out=fcn[hh * 64:(hh + 1) * 64, sl0:sl0 + 16], in_=wn[nm]),
                writes=[("ffn_c", nm, hh)], dma="c_pr")
    for hh in range(2):
        p.emit("sync", lambda e, hh=hh: e.dma_start(
            out=fcn[hh * 64:(hh + 1) * 64, 16:32], in_=wn["fb1"]),
            writes=[("ffn_c", "fb1", hh)], dma="c_pr")
    stg = t("w_stage").rearrange("p a b -> p (a b)")
    p.emit("sync", lambda e: e.dma_start(
        out=stg[:, 0:512].rearrange("p (cc m) -> p cc m", m=64),
        in_=wn["fWin"].rearrange("(cc p) m -> p cc m", p=P)),
        writes=[("w_stage", "win")], dma="c_pr")
    p.emit("vector", lambda e: e.tensor_copy(
        out=t("w_Win").rearrange("p cc m -> p (cc m)"), in_=stg[:, 0:512]),
        reads=[("w_stage", "win")], writes=["w_Win"])
    p.emit("sync", lambda e: e.dma_start(out=stg[0:64, C:2 * C], in_=wn["fWout"]),
           writes=[("w_stage", "wout")], dma="c_pr")
    p.emit("vector", lambda e: e.tensor_copy(out=t("w_Wout")[0:64, :], in_=stg[0:64, C:2 * C]),
           reads=[("w_stage", "wout")], writes=["w_Wout"])
    p.emit("gpsimd", lambda e: e.dma_start(out=t("w_Wout")[64:128, :], in_=t("w_Wout")[0:64, :]),
           reads=["w_Wout"], writes=[("w_Wout", "hi")], dma="c_wodup")
    p.emit("sync", lambda e: e.dma_start(out=stg[0:64, 2 * C:2 * C + 1], in_=wn["fb2"][:, None]),
           writes=[("w_stage", "fb2")], dma="c_pr")
    p.emit("vector", lambda e: e.tensor_copy(out=t("scab")[0:64, 0, 0:1], in_=stg[0:64, 2 * C:2 * C + 1]),
           reads=[("w_stage", "fb2")], writes=[("scab", "fb2")])
    def mm_bout(e):
        e.matmul(pw[0][0:1, 0:512], t("scab")[0:64, 0, 0:1], t("w_Wout")[0:64, 0:512],
                 start=True, stop=True)
        return e.matmul(pw[1][0:1, 0:512], t("scab")[0:64, 0, 0:1], t("w_Wout")[0:64, 512:1024],
                        start=True, stop=True)
    p.emit("tensor", mm_bout, reads=[("scab", "fb2"), "w_Wout"], writes=[(PB(0),), (PB(1),)])
    p.emit("sync", lambda e: e.dma_start(out=stg[0:1, 3 * C:4 * C], in_=wn["fbout"][None, :]),
           writes=[("w_stage", "bout0")], dma="c_pr")
    for half in range(2):
        p.emit("vector", lambda e, half=half: e.tensor_tensor(
            out=t("lscr")[0:1, 0:512], in0=pw[half][0:1, 0:512],
            in1=stg[0:1, 3 * C + half * 512:3 * C + (half + 1) * 512], op=ALU.add),
            reads=[(PB(half),), ("w_stage", "bout0")], writes=[("lscr", "bo")])
        p.emit("vector", lambda e, half=half: e.tensor_copy(
            out=t("rows2")[0:1, half * 512:(half + 1) * 512],
            in_=t("lscr")[0:1, 0:512]),
            reads=[("lscr", "bo")], writes=[("rows2", "bout")])
    p.emit("sync", lambda e: e.dma_start(out=stg[0:1, 2 * C:2 * C + 64], in_=wn["fbin"][None, :]),
           writes=[("w_stage", "bin")], dma="c_pr")
    p.emit("vector", lambda e: e.tensor_copy(out=t("binrow")[0:1, :],
                                             in_=stg[0:1, 2 * C:2 * C + 64]),
           reads=[("w_stage", "bin")], writes=["binrow"])

    def load_w1(wname, dst, ncols, gfold, bcol, b0name, rowdst, rowkey):
        qkeflat = t("qke").rearrange("p a b -> p (a b)")
        for cc in range(NCH):
            p.emit("sync", lambda e, cc=cc: e.dma_start(
                out=stg[:, 0:ncols], in_=wn[wname][cc * P:(cc + 1) * P, :]),
                writes=[("w_stage", wname, cc)], dma="c_wl")
            if gfold is not None:
                p.emit("vector", lambda e, cc=cc: e.tensor_scalar(
                    out=dst[:, cc, :], in0=stg[:, 0:ncols],
                    scalar1=t("g_cols")[:, gfold, cc:cc + 1], scalar2=None, op0=ALU.mult),
                    reads=[("w_stage", wname, cc), ("g_cols", gfold)], writes=[(wname, cc)])
            else:
                p.emit("vector", lambda e, cc=cc: e.tensor_copy(out=dst[:, cc, :], in_=stg[:, 0:ncols]),
                       reads=[("w_stage", wname, cc)], writes=[(wname, cc)])
            if bcol is not None:
                p.emit("vector", lambda e, cc=cc: e.tensor_copy(
                    out=qkeflat[:, 0:ncols], in_=stg[:, 0:ncols]),
                    reads=[("w_stage", wname, cc)], writes=[("qke", "stagebf")])
                def mmb(e, cc=cc):
                    last = None
                    for oi, (o, nn) in enumerate(splits(0, ncols)):
                        bank = pw[2] if oi == 0 else pw[3]
                        last = e.matmul(bank[0:1, 0:nn], t("b_colsb")[:, bcol, cc:cc + 1],
                                        qkeflat[:, o:o + nn],
                                        start=(cc == 0), stop=(cc == NCH - 1))
                    return last
                p.emit("tensor", mmb, reads=[("qke", "stagebf"), "b_colsb"],
                       writes=[(PB(2),), (PB(3),)])
        if bcol is not None:
            p.emit("sync", lambda e: e.dma_start(out=stg[0:1, 3 * C:3 * C + ncols],
                                                 in_=wn[b0name][None, :]),
                   writes=[("w_stage", "b0w")], dma="c_pr")
            for oi, (o, nn) in enumerate(splits(0, ncols)):
                bank = pw[2] if oi == 0 else pw[3]
                p.emit("vector", lambda e, o=o, nn=nn, bank=bank: e.tensor_tensor(
                    out=t("lscr")[0:1, 0:nn], in0=bank[0:1, 0:nn],
                    in1=stg[0:1, 3 * C + o:3 * C + o + nn], op=ALU.add),
                    reads=[(PB(2),), (PB(3),), ("w_stage", "b0w")], writes=[("lscr", "bw")])
                p.emit("vector", lambda e, o=o, nn=nn: e.tensor_copy(
                    out=rowdst[0:1, o:o + nn], in_=t("lscr")[0:1, 0:nn]),
                    reads=[("lscr", "bw")], writes=[rowkey])

    load_w1("in_w", t("w_in"), C, 0, 0, "in_b", t("r_inb"), ("r_inb",))
    load_w1("actp_w", t("w_actp"), C, 0, 0, "actp_b", t("r_actb"), ("r_actb",))
    qkef = t("qke").rearrange("p a b -> p (a b)")
    for cc in range(NCH):
        p.emit("sync", lambda e, cc=cc: e.dma_start(
            out=stg[:, 0:2 * C], in_=wn["qk_w"][cc * P:(cc + 1) * P, :]),
            writes=[("w_stage", "qk", cc)], dma="c_wl")
        p.emit("vector", lambda e, cc=cc: e.tensor_copy(out=qkef[:, 0:2 * C], in_=stg[:, 0:2 * C]),
               reads=[("w_stage", "qk", cc)], writes=[("qke", "stagebf")])
        p.emit("sync", lambda e, cc=cc: e.dma_start(
            out=qkbf[:, :, cc, :].rearrange("j p m -> p j m"),
            in_=qkef[:, 0:2 * C].rearrange("p (j m) -> p j m", m=P)),
            reads=[("qke", "stagebf")], writes=[("qkbf", cc)], dma="c_wlo")
    p.emit("sync", lambda e: e.dma_start(out=stg[0:1, 2 * C:4 * C], in_=wn["qk_b"][None, :]),
           writes=[("w_stage", "qkb")], dma="c_pr")
    p.emit("vector", lambda e: e.tensor_copy(out=t("r_qkb")[0:1, :], in_=stg[0:1, 2 * C:4 * C]),
           reads=[("w_stage", "qkb")], writes=[("r_qkb",)])

    def ln_fm(a_bf, abf_keys, sqk, nco, a_f, acol, af_keys, xnt, xn_name, b0):
        assert nco <= 512
        b1 = b0 + 1
        def mmsum(e):
            last = None
            for cc in range(NCH):
                last = e.matmul(pw[b1][0:1, 0:nco], t("ones_col")[:, :], a_bf[:, cc, 0:nco],
                                start=(cc == 0), stop=(cc == NCH - 1))
            return last
        p.emit("tensor", mmsum, reads=abf_keys + ["ones_col"], writes=[(PB(b1),)])
        def mmsq(e):
            last = None
            for cc in range(NCH):
                last = e.matmul(pw[b1][32:33, 0:nco], t("ones_col")[:, :],
                                t("sqbf")[:, cc, 0:nco],
                                start=(cc == 0), stop=(cc == NCH - 1))
            return last
        p.emit("tensor", mmsq, reads=[sqk, "ones_col"], writes=[(PB(b1), "q")])
        p.emit("scalar", lambda e: e.activation(
            out=t("srow")[0:1, 0, 0:nco], in_=pw[b1][0:1, 0:nco], func=AF.Identity),
            reads=[(PB(b1),), (PB(b1), "q")], writes=[("srow", 0)])
        p.emit("scalar", lambda e: e.activation(
            out=t("srow")[0:1, 1, 0:nco], in_=pw[b1][32:33, 0:nco], func=AF.Identity),
            reads=[(PB(b1), "q")], writes=[("srow", 1)])
        def mmrep(e):
            e.matmul(pw[b0][:, 0:nco], t("invC_col")[0:1, :],
                     t("srow")[0:1, 0, 0:nco], start=True, stop=True)
            return e.matmul(pw[b1][:, 0:nco], t("invC_col")[0:1, :],
                            t("srow")[0:1, 1, 0:nco], start=True, stop=True)
        p.emit("tensor", mmrep, reads=[("srow", 0), ("srow", 1), "invC_col"],
               writes=[(PB(b0),), (PB(b1), "E")])
        p.emit("scalar", lambda e: e.activation(
            out=t("mur")[:, 0:nco], in_=pw[b0][:, 0:nco], func=AF.Identity),
            reads=[(PB(b0),)], writes=[("mur", "v")])
        p.emit("gpsimd", lambda e: e.tensor_tensor(
            out=t("nmu2")[:, 0:nco], in0=t("mur")[:, 0:nco],
            in1=t("mur")[:, 0:nco], op=ALU.mult),
            reads=[("mur", "v")], writes=["nmu2"])
        p.emit("scalar", lambda e: e.activation(
            out=t("rstd")[:, 0:nco], in_=pw[b1][:, 0:nco], func=AF.Identity,
            bias=t("eps5")[:, 0:1]),
            reads=[(PB(b1), "E"), "eps5"], writes=[("rstd", "E")])
        p.emit("gpsimd", lambda e: e.tensor_tensor(
            out=t("rstd")[:, 0:nco], in0=t("rstd")[:, 0:nco],
            in1=t("nmu2")[:, 0:nco], op=ALU.subtract),
            reads=[("rstd", "E"), "nmu2"], writes=[("rstd", "v")])
        p.emit("vector", lambda e: e.reciprocal(out=t("rstd")[:, 0:nco], in_=t("rstd")[:, 0:nco]),
               reads=[("rstd", "v")], writes=[("rstd", "v")])
        p.emit("scalar", lambda e: e.activation(out=t("rstd")[:, 0:nco], in_=t("rstd")[:, 0:nco],
                                                func=AF.Sqrt),
               reads=[("rstd", "v")], writes=[("rstd", "v")])
        for cc in range(NCH):
            p.emit("vector", lambda e, cc=cc: e.tensor_tensor(
                out=t("lscr")[:, 0:nco], in0=a_f[:, cc, acol:acol + nco],
                in1=t("mur")[:, 0:nco], op=ALU.subtract),
                reads=af_keys + [("mur", "v")], writes=[("lscr", "x")])
            p.emit("vector", lambda e, cc=cc: e.tensor_tensor(
                out=xnt[:, cc, 0:nco], in0=t("lscr")[:, 0:nco],
                in1=t("rstd")[:, 0:nco], op=ALU.mult),
                reads=[("lscr", "x"), ("rstd", "v")], writes=[(xn_name, cc)])

    def pkv_open(e):
        e.matmul(pkv[:, 0:512], t("zrow")[0:1, :], t("ones_row")[0:1, 0:512],
                 start=True, stop=False)
        return e.matmul(pkv[:, 512:1024], t("zrow")[0:1, :], t("ones_row")[0:1, 0:512],
                        start=True, stop=False)
    p.emit("tensor", pkv_open, reads=["zrow", "ones_row"], writes=[("pkv",)])

    rot2 = {0: 0, 1: 0}

    def nbank(par):
        rot2[par] = 1 - rot2[par]
        return 2 * par + rot2[par]

    def pass1_head(ti):
        p.curpar[0] = ti % 2
        par = ti % 2
        PB0, PB1 = 2 * par, 2 * par + 1
        t0 = ti * TT
        x0, u_tm, u_fm = t("x0"), t("u_tm"), t("u_fm")
        xed = t("xed")
        st = t("stats")
        p.emit("sync", lambda e: e.dma_start(
            out=x0[:, :, :], in_=x_my[t0 + HL:t0 + HL + TT, :].rearrange("(g p) c -> p g c", p=P)),
            writes=["x0"], dma=f"c_x0{ti % 2}")
        p.emit("sync", lambda e: e.dma_start(
            out=xed[0:4, :], in_=x_my[t0:t0 + HL, :]),
            writes=[("xed", "el")], dma=f"c_xe{ti % 2}")
        p.emit("sync", lambda e: e.dma_start(
            out=xed[4:8, :], in_=x_my[t0 + HL + TT:t0 + 2 * HL + TT, :]),
            writes=[("xed", "er")], dma=f"c_xe{ti % 2}")
        edk = [("xed", "el"), ("xed", "er")]

        for g in range(NG):
            p.emit("vector", lambda e, g=g: e.reduce_sum(out=st[:, g:g + 1], in_=x0[:, g, :], axis=AX.X),
                   reads=["x0"], writes=[("stats", "s", g)])
            p.emit("scalar", lambda e, g=g: e.activation(out=t("usq")[:, :], in_=x0[:, g, :],
                                                         func=AF.Square, accum_out=st[:, 4 + g:5 + g]),
                   reads=["x0"], writes=[("stats", "q", g), "usq"])
        p.emit("vector", lambda e: e.reduce_sum(out=t("red1")[0:8, 0:1], in_=xed[0:8, :], axis=AX.X),
               reads=edk, writes=[("red1", "e")])
        p.emit("scalar", lambda e: e.activation(out=t("usq")[0:8, :], in_=xed[0:8, :],
                                                func=AF.Square, accum_out=st[0:8, 8:9]),
               reads=edk, writes=[("stats", "qe"), "usq"])

        def fin(sl_s, sl_q, sl_mu, sl_nb, keys, tag):
            p.emit("vector", lambda e: e.tensor_scalar(out=sl_mu, in0=sl_s, scalar1=1.0 / C,
                                                       scalar2=None, op0=ALU.mult),
                   reads=keys, writes=[("stats", tag, "mu")])
            p.emit("vector", lambda e: e.tensor_tensor(out=sl_nb, in0=sl_mu, in1=sl_mu, op=ALU.mult),
                   reads=[("stats", tag, "mu")], writes=[("stats", tag, "m2")])
            p.emit("vector", lambda e: e.scalar_tensor_tensor(out=sl_q, in0=sl_q, scalar=1.0 / C,
                                                              in1=sl_nb, op0=ALU.mult, op1=ALU.subtract),
                   reads=keys + [("stats", tag, "m2")], writes=[("stats", tag, "v")])
            p.emit("vector", lambda e: e.tensor_scalar(out=sl_q, in0=sl_q, scalar1=1e-5,
                                                       scalar2=None, op0=ALU.add),
                   reads=[("stats", tag, "v")], writes=[("stats", tag, "v")])
            p.emit("vector", lambda e: e.reciprocal(out=sl_q, in_=sl_q),
                   reads=[("stats", tag, "v")], writes=[("stats", tag, "v")])
            p.emit("scalar", lambda e: e.activation(out=sl_q, in_=sl_q, func=AF.Sqrt),
                   reads=[("stats", tag, "v")], writes=[("stats", tag, "r")])
            p.emit("vector", lambda e: e.scalar_tensor_tensor(out=sl_nb, in0=sl_mu, scalar=-1.0,
                                                              in1=sl_q, op0=ALU.mult, op1=ALU.mult),
                   reads=[("stats", tag, "r"), ("stats", tag, "mu")], writes=[("stats", tag, "nb")])

        fin(st[:, 0:NG], st[:, 4:4 + NG], st[:, 10:10 + NG], t("lscr")[:, 0:NG],
            [("stats", "s", g) for g in range(NG)] + [("stats", "q", g) for g in range(NG)], "g")
        fin(t("red1")[0:8, 0:1], st[0:8, 8:9], st[0:8, 9:10], t("lscr")[0:8, 4:5],
            [("red1", "e"), ("stats", "qe")], "e")

        for g in range(NG):
            p.emit("scalar", lambda e, g=g: e.activation(
                out=u_tm[:, g, :], in_=x0[:, g, :], func=AF.Identity,
                scale=st[:, 4 + g:5 + g], bias=t("lscr")[:, g:g + 1]),
                reads=["x0", ("stats", "g", "r"), ("stats", "g", "nb")], writes=[("u_tm", g)])
        p.emit("scalar", lambda e: e.activation(
            out=xed[32:40, :], in_=xed[0:8, :], func=AF.Identity,
            scale=st[0:8, 8:9], bias=t("lscr")[0:8, 4:5]),
            reads=edk + [("stats", "e", "r"), ("stats", "e", "nb")], writes=[("xed", "ue")])

        for g in range(NG):
            def trg(e, g=g):
                last = None
                for cc in range(NCH):
                    bank = pw[PB0 + cc // 4]
                    last = e.transpose(bank[:, (cc % 4) * P:(cc % 4 + 1) * P],
                                       u_tm[:, g, cc * P:(cc + 1) * P], t("ident_f")[:, :])
                return last
            p.emit("tensor", trg, reads=[("u_tm", g), "ident_f"],
                   writes=[(PB(PB0),), (PB(PB1),)])
            for half in range(2):
                p.emit("vector", lambda e, g=g, half=half: e.tensor_copy(
                    out=u_fm[:, half * 4:(half + 1) * 4, HL + g * P:HL + (g + 1) * P],
                    in_=pw[PB0 + half][:, :].rearrange("p (cc n) -> p cc n", n=P)),
                    reads=[(PB(PB0 + half),)], writes=[("u_fm", g, half)])
        def tre(e):
            last = None
            for cc in range(NCH):
                last = e.transpose(pw[PB0][:, cc * 8:(cc + 1) * 8], xed[32:40, cc * P:(cc + 1) * P],
                                   t("ident_f")[32:40, 32:40])
            return last
        p.emit("tensor", tre, reads=[("xed", "ue"), "ident_f"], writes=[(PB(PB0),)])
        def eedge(e):
            last = None
            for cc in range(NCH):
                last = e.tensor_copy(out=u_fm[:, cc, 0:4], in_=pw[PB0][:, cc * 8:cc * 8 + 4])
                last = e.tensor_copy(out=u_fm[:, cc, W - 4:W], in_=pw[PB0][:, cc * 8 + 4:cc * 8 + 8])
            return last
        p.emit("vector", eedge, reads=[(PB(PB0),)], writes=[("u_fm", "edge")])
        ufmk = [("u_fm", g, h) for g in range(NG) for h in range(2)] + [("u_fm", "edge")]

        p.emit("gpsimd", lambda e: e.tensor_copy(
            out=t("u_bf")[:, :, HL:HL + TT],
            in_=u_fm[:, :, HL:HL + TT]),
            reads=ufmk, writes=["u_bf"])

        def mm_win(e):
            last = None
            for hh in range(2):
                for cc in range(NCH):
                    last = e.matmul(pw[PB1][hh * 64:(hh + 1) * 64, 0:TH], t("w_Win")[:, cc, :],
                                    t("u_bf")[:, cc, HL + hh * TH:HL + (hh + 1) * TH],
                                    start=(cc == 0), stop=False)
                last = e.matmul(pw[PB1][hh * 64:(hh + 1) * 64, 0:TH],
                                t("binrow")[0:1, :],
                                t("ones_row")[0:1, 0:TH], start=False, stop=True)
            return last
        p.emit("tensor", mm_win, reads=["u_bf", "w_Win", "binrow", "ones_row"],
               writes=[(PB(PB1),)])
        p.emit("vector", lambda e: e.tensor_copy(out=t("s2g2")[:, 0, :], in_=pw[PB1][:, 0:TH]),
               reads=[(PB(PB1),)], writes=[("s2g2", 0)])
        def inner1(e):
            last = None
            for i in range(16):
                last = e.activation(out=t("hscr")[:, i, :], in_=t("s2g2")[:, 0, :],
                                    func=AF.Identity, scale=fcn[:, i:i + 1],
                                    bias=fcn[:, 16 + i:17 + i])
            return last
        p.emit("scalar", inner1,
               reads=[("s2g2", 0), ("ffn_c", "fW1", 0), ("ffn_c", "fW1", 1),
                      ("ffn_c", "fb1", 0), ("ffn_c", "fb1", 1)],
               writes=[("hscr", 0)])
        def inner2(e):
            last = None
            for i in range(16):
                last = e.tensor_scalar(out=t("hscr")[:, i, :], in0=t("hscr")[:, i, :],
                                       scalar1=0.0, scalar2=fcn[:, 32 + i:33 + i],
                                       op0=ALU.max, op1=ALU.mult)
            return last
        p.emit("vector", inner2, reads=[("hscr", 0), ("ffn_c", "fW2", 0), ("ffn_c", "fW2", 1)],
               writes=[("hscr", 1)])
        hs = t("hscr").rearrange("p i n -> p (i n)")
        for stepi, width in enumerate((8 * TH, 4 * TH, 2 * TH, TH)):
            p.emit("vector", lambda e, width=width: e.tensor_tensor(
                out=hs[:, 0:width], in0=hs[:, 0:width], in1=hs[:, width:2 * width], op=ALU.add),
                reads=[("hscr", stepi + 1)], writes=[("hscr", stepi + 2)])
        p.emit("vector", lambda e: e.tensor_copy(out=t("s2g2")[:, 1, :], in_=hs[:, 0:TH]),
               reads=[("hscr", 5)], writes=[("s2g2", 1)])
        for g in range(NG):
            def mmwo(e, g=g):
                hh, off = divmod(g, max(NG // 2, 1))
                last = None
                for half in range(2):
                    last = e.matmul(pw[PB0 + half][:, 0:512],
                                    t("s2g2")[hh * 64:(hh + 1) * 64, 1, off * P:(off + 1) * P],
                                    t("w_Wout")[hh * 64:(hh + 1) * 64, half * 512:(half + 1) * 512],
                                    start=True, stop=False)
                    last = e.matmul(pw[PB0 + half][:, 0:512],
                                    t("ones_row")[0:1, 0:P],
                                    t("rows2")[0:1, half * 512:(half + 1) * 512],
                                    start=False, stop=True)
                return last
            p.emit("tensor", mmwo, reads=[("s2g2", 1), "w_Wout", ("w_Wout", "hi"), ("rows2", "bout"), "ones_row"],
                   writes=[(PB(PB0),), (PB(PB1),)])
            for half in range(2):
                p.emit("scalar", lambda e, half=half: e.activation(
                    out=t("evs")[:, half, :], in_=pw[PB0 + half][:, 0:512],
                    func=AF.Identity),
                    reads=[(PB(PB0 + half),)], writes=[("evs", half)])
                p.emit("gpsimd", lambda e, g=g, half=half: e.tensor_tensor(
                    out=t("y0t")[:, g, half * 512:(half + 1) * 512],
                    in0=x0[:, g, half * 512:(half + 1) * 512],
                    in1=t("evs")[:, half, :], op=ALU.add),
                    reads=["x0", ("evs", half)], writes=[("y0t", g, half)])
        p.emit("sync", lambda e: e.dma_start(
            out=y0sp[t0:t0 + TT, :].rearrange("(g p) c -> p g c", p=P), in_=t("y0t")[:, :, :]),
            reads=[("y0t", g, h) for g in range(NG) for h in range(2)], dma=f"c_y0{ti % 2}")

        a1f = t("a1f")
        scr = t("cscr").rearrange("p cc n -> p (cc n)")
        tp = taps
        for cc in range(NCH):
            p.emit("vector", lambda e, cc=cc: e.tensor_scalar(
                out=scr[:, cc * W2:cc * W2 + W2], in0=u_fm[:, cc, 0:W2],
                scalar1=tp[:, 0, cc, 0:1], scalar2=tp[:, 0, cc, 3:4], op0=ALU.mult, op1=ALU.add),
                reads=ufmk + [("taps", 0, "w"), ("taps", 0, "b")], writes=[("cscr", cc)])
            p.emit("vector", lambda e, cc=cc: e.scalar_tensor_tensor(
                out=scr[:, cc * W2:cc * W2 + W2], in0=u_fm[:, cc, 2:W],
                scalar=tp[:, 0, cc, 2:3], in1=scr[:, cc * W2:cc * W2 + W2],
                op0=ALU.mult, op1=ALU.add),
                reads=ufmk + [("taps", 0, "w"), ("cscr", cc)], writes=[("cscr", cc)])
            p.emit("vector", lambda e, cc=cc: e.scalar_tensor_tensor(
                out=a1f[:, cc, 1:W - 1], in0=u_fm[:, cc, 1:W - 1],
                scalar=tp[:, 0, cc, 1:2], in1=scr[:, cc * W2:cc * W2 + W2],
                op0=ALU.mult, op1=ALU.add),
                reads=ufmk + [("taps", 0, "w"), ("cscr", cc)], writes=[("a1f", cc)])
        a1k = [("a1f", cc) for cc in range(NCH)]
        p.emit("sync", lambda e: e.dma_start(
            out=a1sp[ti],
            in_=a1f[:, :, HL:HL + TT]),
            reads=a1k, dma=f"c_a1{ti % 2}")

        p.emit("gpsimd", lambda e: e.tensor_copy(
            out=t("a1bf")[:, :, 0:T4],
            in_=a1f[:, :, 2:W2]),
            reads=a1k, writes=["a1bf"])
        p.emit("scalar", lambda e: e.activation(
            out=t("sqbf")[:, :, :],
            in_=a1f[:, :, 2:W2], func=AF.Square),
            reads=a1k, writes=["sqbf"])
        ln_fm(t("a1bf"), ["a1bf"], "sqbf", T4, a1f, 2, a1k, t("xn1"), "xn1", PB0)
        xn1k = [("xn1", cc) for cc in range(NCH)]

        for m in range(NCH):
            bk = nbank(par)
            def mma(e, m=m, bk=bk):
                last = None
                for cc in range(NCH):
                    last = e.matmul(pw[bk][:, 0:TT], t("w_actp")[:, cc, m * P:(m + 1) * P],
                                    t("xn1")[:, cc, 2:T2], start=(cc == 0), stop=False)
                return e.matmul(pw[bk][:, 0:TT], t("r_actb")[0:1, m * P:(m + 1) * P],
                                t("ones_row")[0:1, 0:TT], start=False, stop=True)
            p.emit("tensor", mma, reads=xn1k + ["w_actp", ("r_actb",), "ones_row"],
                   writes=[(PB(bk),)])
            p.emit("scalar", lambda e, m=m, bk=bk: e.activation(
                out=t("acr")[:, m, :], in_=pw[bk][:, 0:TT], func=AF.Silu),
                reads=[(PB(bk),)], writes=[("acr", m)])
        p.emit("sync", lambda e: e.dma_start(
            out=arsp[ti], in_=t("acr")[:, :, :]),
            reads=[("acr", m) for m in range(NCH)], dma=f"c_ar{ti % 2}")

        for m in range(NCH):
            chunks = [(o, nn, nbank(par)) for (o, nn) in splits(0, T4)]
            def mmi(e, m=m, chunks=chunks):
                last = None
                for (o, nn, bk) in chunks:
                    for cc in range(NCH):
                        last = e.matmul(pw[bk][:, 0:nn], t("w_in")[:, cc, m * P:(m + 1) * P],
                                        t("xn1")[:, cc, o:o + nn], start=(cc == 0), stop=False)
                    last = e.matmul(pw[bk][:, 0:nn], t("r_inb")[0:1, m * P:(m + 1) * P],
                                    t("ones_row")[0:1, 0:nn], start=False, stop=True)
                return last
            p.emit("tensor", mmi, reads=xn1k + ["w_in", ("r_inb",), "ones_row"],
                   writes=[(PB(bk),) for (_o, _n, bk) in chunks])
            for (o, nn, bk) in chunks:
                p.emit("vector", lambda e, m=m, o=o, nn=nn, bk=bk: e.tensor_copy(
                    out=t("xi0")[:, m, o:o + nn], in_=pw[bk][:, 0:nn]),
                    reads=[(PB(bk),)], writes=[("xi0", m)])
        xi0k = [("xi0", m) for m in range(NCH)]
        if ti == 0:
            p.emit("vector", lambda e: e.tensor_scalar(
                out=t("xi0")[:, :, 0:2], in0=t("xi0")[:, :, 0:2],
                scalar1=t("lrmask")[:, 0:1], scalar2=None, op0=ALU.mult),
                reads=xi0k + ["lrmask"], writes=xi0k)
        if ti == NTILE - 1:
            p.emit("vector", lambda e: e.tensor_scalar(
                out=t("xi0")[:, :, T2:T4], in0=t("xi0")[:, :, T2:T4],
                scalar1=t("lrmask")[:, 1:2], scalar2=None, op0=ALU.mult),
                reads=xi0k + ["lrmask"], writes=xi0k)

        for cc in range(NCH):
            p.emit("vector", lambda e, cc=cc: e.tensor_scalar(
                out=scr[:, cc * W2:cc * W2 + T2], in0=t("xi0")[:, cc, 0:T2],
                scalar1=tp[:, 1, cc, 0:1], scalar2=tp[:, 1, cc, 3:4], op0=ALU.mult, op1=ALU.add),
                reads=xi0k + [("taps", 1, "w"), ("taps", 1, "b")], writes=[("cscr", cc)])
            p.emit("vector", lambda e, cc=cc: e.scalar_tensor_tensor(
                out=scr[:, cc * W2:cc * W2 + T2], in0=t("xi0")[:, cc, 2:T4],
                scalar=tp[:, 1, cc, 2:3], in1=scr[:, cc * W2:cc * W2 + T2],
                op0=ALU.mult, op1=ALU.add),
                reads=xi0k + [("taps", 1, "w"), ("cscr", cc)], writes=[("cscr", cc)])
            p.emit("vector", lambda e, cc=cc: e.scalar_tensor_tensor(
                out=t("xi")[:, cc, :], in0=t("xi0")[:, cc, 1:T2 + 1],
                scalar=tp[:, 1, cc, 1:2], in1=scr[:, cc * W2:cc * W2 + T2],
                op0=ALU.mult, op1=ALU.add),
                reads=xi0k + [("taps", 1, "w"), ("cscr", cc)], writes=[("xi", cc, "c")])
            p.emit("scalar", lambda e, cc=cc: e.activation(
                out=t("xi")[:, cc, :], in_=t("xi")[:, cc, :], func=AF.Silu),
                reads=[("xi", cc, "c")], writes=[("xi", cc)])
        xik = [("xi", cc) for cc in range(NCH)]
        if ti == 0:
            p.emit("vector", lambda e: e.tensor_scalar(
                out=t("xi")[:, :, 0:1], in0=t("xi")[:, :, 0:1],
                scalar1=t("lrmask")[:, 0:1], scalar2=None, op0=ALU.mult),
                reads=xik + ["lrmask"], writes=xik)
        if ti == NTILE - 1:
            p.emit("vector", lambda e: e.tensor_scalar(
                out=t("xi")[:, :, T2 - 1:T2], in0=t("xi")[:, :, T2 - 1:T2],
                scalar1=t("lrmask")[:, 1:2], scalar2=None, op0=ALU.mult),
                reads=xik + ["lrmask"], writes=xik)

        for cc in range(NCH):
            p.emit("vector", lambda e, cc=cc: e.tensor_scalar(
                out=scr[:, cc * W2:cc * W2 + TT], in0=t("xi")[:, cc, 0:TT],
                scalar1=tp[:, 2, cc, 0:1], scalar2=tp[:, 2, cc, 3:4], op0=ALU.mult, op1=ALU.add),
                reads=xik + [("taps", 2, "w"), ("taps", 2, "b")], writes=[("cscr", cc)])
            p.emit("vector", lambda e, cc=cc: e.scalar_tensor_tensor(
                out=scr[:, cc * W2:cc * W2 + TT], in0=t("xi")[:, cc, 2:T2],
                scalar=tp[:, 2, cc, 2:3], in1=scr[:, cc * W2:cc * W2 + TT],
                op0=ALU.mult, op1=ALU.add),
                reads=xik + [("taps", 2, "w"), ("cscr", cc)], writes=[("cscr", cc)])
            p.emit("vector", lambda e, cc=cc: e.scalar_tensor_tensor(
                out=t("lepe_sb")[:, cc, :], in0=t("xi")[:, cc, 1:TT + 1],
                scalar=tp[:, 2, cc, 1:2], in1=scr[:, cc * W2:cc * W2 + TT],
                op0=ALU.mult, op1=ALU.add),
                reads=xik + [("taps", 2, "w"), ("cscr", cc)], writes=[("lepe_sb", cc)])
        p.emit("sync", lambda e: e.dma_start(
            out=lpsp[ti], in_=t("lepe_sb")[:, :, :]),
            reads=[("lepe_sb", cc) for cc in range(NCH)], dma=f"c_lp{ti % 2}")

        return

    def qk_joint(ta, tb):
        for j in range(16):
            sl3 = j % 4
            p.curpar[0] = 0
            if j % 2 == 0:
                p.emit("sync", lambda e, j=j, sl3=sl3: e.dma_start(
                    out=tens["w3s"].rearrange("p s c m -> p s (c m)")[:, sl3:sl3 + 2, :],
                    in_=qkbf[j:j + 2].rearrange("j p c m -> p j (c m)")),
                    reads=[("qkbf", cc2) for cc2 in range(NCH)],
                    writes=[("w3s", sl3), ("w3s", sl3 + 1)], dma=f"c_qs{sl3}")
            for par in (0, 1):
                p.curpar[0] = par
                bk = nbank(par)
                def mmq(e, j=j, bk=bk, sl3=sl3):
                    last = None
                    for cc in range(NCH):
                        last = e.matmul(pw[bk][:, 0:TT], tens["w3s"][:, sl3, cc, :],
                                        t("xi")[:, cc, 1:TT + 1], start=(cc == 0), stop=False)
                    return e.matmul(pw[bk][:, 0:TT], t("r_qkb")[0:1, j * P:(j + 1) * P],
                                    t("ones_row")[0:1, 0:TT], start=False, stop=True)
                p.emit("tensor", mmq,
                       reads=[("xi", cc) for cc in range(NCH)] +
                             [("w3s", sl3), ("r_qkb",), "ones_row"],
                       writes=[(PB(bk),)])
                p.emit("scalar", lambda e, bk=bk: e.activation(
                    out=t("scab")[:, 0, :], in_=pw[bk][:, 0:TT], func=AF.Relu,
                    scale=-1.0),
                    reads=[(PB(bk),)], writes=[("scab", 0)])
                p.emit("scalar", lambda e: e.activation(
                    out=t("scab")[:, 1, :], in_=t("scab")[:, 0, :], func=AF.Exp,
                    scale=-1.0),
                    reads=[("scab", 0)], writes=[("scab", 1)])
                p.emit("vector", lambda e, j=j, bk=bk: e.scalar_tensor_tensor(
                    out=t("qke")[:, j, :], in0=pw[bk][:, 0:TT], scalar=0.0,
                    in1=t("scab")[:, 1, :], op0=ALU.max, op1=ALU.add),
                    reads=[(PB(bk),), ("scab", 1)], writes=[("qke", j)])
        p.curpar[0] = 0

    def pass1_tail(ti):
        p.curpar[0] = ti % 2
        par = ti % 2
        t0 = ti * TT
        xik = [("xi", cc) for cc in range(NCH)]

        for cc in range(NCH):
            p.emit("vector", lambda e, cc=cc: e.reduce_sum(
                out=t("red1")[:, 0:1], in_=t("qke")[:, 8 + cc, :], axis=AX.X),
                reads=[("qke", 8 + cc)], writes=[("red1", "k")])
            p.emit("vector", lambda e, cc=cc: e.tensor_tensor(
                out=t("ksum")[:, cc:cc + 1], in0=t("ksum")[:, cc:cc + 1],
                in1=t("red1")[:, 0:1], op=ALU.add),
                reads=[("red1", "k"), "ksum"], writes=["ksum"])

        for j in range(16):
            bk = nbank(par)
            p.emit("tensor", lambda e, j=j, bk=bk: e.matmul(
                pw[bk][:, 0:TT], t("Rm")[:, j % 8, :], t("qke")[:, j, :], start=True, stop=True),
                reads=[("qke", j), ("Rm", j % 8)], writes=[(PB(bk),)])
            dst = t("qro") if j < 8 else t("kro")
            key = ("qro", j) if j < 8 else ("kro", j - 8)
            p.emit("scalar", lambda e, j=j, bk=bk, dst=dst: e.activation(
                out=dst[:, j % 8, :], in_=pw[bk][:, 0:TT], func=AF.Identity,
                bias=t("crope_col")[:, j % 8:j % 8 + 1]),
                reads=[(PB(bk),), "crope_col"], writes=[key])
        p.emit("sync", lambda e: e.dma_start(
            out=qrsp[ti], in_=t("qro")[:, :, :]),
            reads=[("qro", j) for j in range(8)], dma=f"c_qr{ti % 2}")

        pt = pt4 if par == 0 else pt5
        ptk = "pt4" if par == 0 else "pt5"
        for g in range(NG):
            def trk(e, g=g):
                last = None
                for cc in range(NCH):
                    last = e.transpose(pt[:, cc * P:(cc + 1) * P],
                                       t("kro")[:, cc, g * P:(g + 1) * P], t("ident_b")[:, :])
                return last
            p.emit("tensor", trk, reads=[("kro", cc) for cc in range(NCH)] + ["ident_b"],
                   writes=[(ptk,)])
            p.emit("scalar", lambda e, g=g: e.activation(
                out=t("krtvh")[:, 0, g, :], in_=pt[:, :], func=AF.Identity),
                reads=[(ptk,)], writes=[("krtvh", "k", g)])
            def trv(e, g=g):
                last = None
                for cc in range(NCH):
                    last = e.transpose(pt[:, cc * P:(cc + 1) * P],
                                       t("xi")[:, cc, 1 + g * P:1 + (g + 1) * P], t("ident_b")[:, :])
                return last
            p.emit("tensor", trv, reads=xik + ["ident_b", ("krtvh", "k", g)], writes=[(ptk, "v")])
            p.emit("scalar", lambda e, g=g: e.activation(
                out=t("krtvh")[:, 1, g, :], in_=pt[:, :], func=AF.Identity),
                reads=[(ptk, "v")], writes=[("krtvh", "v", g)])
            def mmkv(e, g=g, ti=ti):
                last = None
                for h in range(H):
                    last = e.matmul(pkv[0:64, h * 64:(h + 1) * 64],
                                    t("krtvh")[:, 0, g, h * 64:(h + 1) * 64],
                                    t("krtvh")[:, 1, g, h * 64:(h + 1) * 64],
                                    start=False, stop=False)
                return last
            p.emit("tensor", mmkv, reads=[("krtvh", "k", g), ("krtvh", "v", g)], writes=[("pkv",)])

    assert NTILE % 2 == 0

    def pair_run(tile_fn, ta, tb):
        p.record()
        tile_fn(ta)
        A = p.stop_record()
        p.record()
        tile_fn(tb)
        B = p.stop_record()
        for i in range(max(len(A), len(B))):
            if i < len(A):
                p.commit(A[i])
            if i < len(B):
                p.commit(B[i])

    for tj in range(NTILE // 2):
        pair_run(pass1_head, 2 * tj, 2 * tj + 1)
        qk_joint(2 * tj, 2 * tj + 1)
        pair_run(pass1_tail, 2 * tj, 2 * tj + 1)
    p.curpar[0] = 0

    def pkv_close(e):
        e.matmul(pkv[:, 0:512], t("zrow")[0:1, :], t("ones_row")[0:1, 0:512],
                 start=False, stop=True)
        return e.matmul(pkv[:, 512:1024], t("zrow")[0:1, :], t("ones_row")[0:1, 0:512],
                        start=False, stop=True)
    p.emit("tensor", pkv_close, reads=["zrow", "ones_row", ("pkv",)], writes=[("pkv",)])
    p.emit("vector", lambda e: e.tensor_copy(out=t("kv_raw")[0:64, :], in_=pkv[0:64, :]),
           reads=[("pkv",)], writes=[("kv_raw", "kv")])
    p.emit("tensor", lambda e: e.transpose(pw[0][0:NCH, 0:P], t("ksum")[:, :], t("ident_f")[:, :]),
           reads=["ksum", "ident_f"], writes=[(PB(0),)])
    p.emit("vector", lambda e: e.tensor_copy(out=t("lscr")[0:NCH, 0:P], in_=pw[0][0:NCH, 0:P]),
           reads=[(PB(0),)], writes=[("lscr", "kst")])
    p.emit("sync", lambda e: e.dma_start(out=kmb.rearrange("(a b) -> a b", a=NCH), in_=t("lscr")[0:NCH, 0:P]),
           reads=[("lscr", "kst")], writes=["kmb"], dma="c_km")
    p.emit("sync", lambda e: e.dma_start(out=t("kv_raw")[64:65, :], in_=kmb[0:C][None, :]),
           reads=["kmb"], writes=[("kv_raw", "ks")], dma="c_km")
    p.emit("gpsimd", lambda e: e.indirect_dma_start(
        out=S_kv[:, :], out_offset=IndirectOffsetOnAxis(ap=t("idx")[0:65, 0:1], axis=0),
        in_=t("kv_raw")[:, :], in_offset=None),
        reads=[("kv_raw", "kv"), ("kv_raw", "ks"), ("idx", 0)], writes=[("S_kv",)], dma="c_hs1")
    p.emit("gpsimd", lambda e: e.indirect_dma_start(
        out=S_fl[:, :], out_offset=IndirectOffsetOnAxis(ap=t("idx")[0:2, 2:3], axis=0),
        in_=t("magic")[:, :], in_offset=None),
        reads=["magic", ("idx", 2), ("S_kv",)], writes=[("S_fl", "kv")], dma="c_hs1f")

    def make_poll(sem, rows, tag, prev_keys):
        def fn(e):
            with e.register(f"tg_{tag}") as tg:
                e.reg_mov(tg, 0)
                for i, r in enumerate(rows):
                    e.dma_start(out=t("fbuf")[0:1, i * 16:(i + 1) * 16],
                                in_=S_fl[r:r + 1, :]).then_inc(sem, 16)
                    e.reg_alu(tg, tg, 16, ALU.add)
                    e.wait_ge(sem, tg)
                    with e.While(lambda i=i: e.value_load(t("fbuf")[0:1, i * 16:i * 16 + 1]) != MAGIC):
                        e.dma_start(out=t("fbuf")[0:1, i * 16:(i + 1) * 16],
                                    in_=S_fl[r:r + 1, :]).then_inc(sem, 16)
                        e.reg_alu(tg, tg, 16, ALU.add)
                        e.wait_ge(sem, tg)
            return e.nop()
        p.emit("gpsimd", fn, reads=prev_keys, writes=[("fbuf", tag)])

    make_poll(poll1, (0, 1), "kv", [("S_fl", "kv"), "fbuf"])
    p.emit("sync", lambda e: e.dma_start(
        out=t("kv_cmb")[0:65, :], in_=S_kv[0:65, :]),
        reads=[("fbuf", "kv"), ("S_kv",)], writes=[("kv_cmb", "rd")], dma="c_hs1r")
    p.emit("vector", lambda e: e.tensor_copy(out=t("kv_raw")[0:65, :], in_=t("kv_cmb")[0:65, :]),
           reads=[("kv_cmb", "rd"), ("kv_raw", "kv"), ("kv_raw", "ks")], writes=[("kv_raw", "s0")])
    p.emit("sync", lambda e: e.dma_start(
        out=t("kv_cmb")[0:65, :], in_=S_kv[65:130, :]),
        reads=[("kv_raw", "s0"), ("S_kv",)], writes=[("kv_cmb", "rd2")], dma="c_hs1r2")
    p.emit("vector", lambda e: e.tensor_tensor(
        out=t("kv_raw")[0:65, :], in0=t("kv_raw")[0:65, :], in1=t("kv_cmb")[0:65, :],
        op=ALU.add),
        reads=[("kv_cmb", "rd2"), ("kv_raw", "s0")], writes=[("kv_raw", "tot")])
    p.emit("vector", lambda e: e.tensor_scalar(
        out=t("kv_raw")[64:65, :], in0=t("kv_raw")[64:65, :], scalar1=1.0 / SEQ,
        scalar2=1.0, op0=ALU.mult, op1=ALU.add),
        reads=[("kv_raw", "tot")], writes=[("kv_raw", "km")])
    p.emit("sync", lambda e: e.dma_start(out=kmb[None, 0:C], in_=t("kv_raw")[64:65, :]),
           reads=[("kv_raw", "km")], writes=["kmb"], dma="c_km")
    p.emit("sync", lambda e: e.dma_start(out=t("km_col")[:, :], in_=kmb.rearrange("(cc p) -> p cc", p=P),
           allow_slow_non_contiguous=True),
           reads=["kmb"], writes=["km_col"], dma="c_km")
    p.emit("vector", lambda e: e.tensor_copy(out=t("km_bf")[:, :], in_=t("km_col")[:, :]),
           reads=["km_col"], writes=["km_bf"])
    def mmkm(e):
        last = None
        for cc in range(NCH):
            last = e.matmul(pw[0][:, cc:cc + 1], t("Rm")[:, cc, :], t("km_bf")[:, cc:cc + 1],
                            start=True, stop=True)
        return last
    p.emit("tensor", mmkm, reads=["km_bf"] + [("Rm", cc) for cc in range(NCH)], writes=[(PB(0),)])
    p.emit("vector", lambda e: e.tensor_copy(out=t("km_bf")[:, :], in_=pw[0][:, 0:NCH]),
           reads=[(PB(0),)], writes=[("km_bf", "r")])
    p.emit("gpsimd", lambda e: e.memset(t("kmr_bd")[:, :, :], 0.0), writes=[("kmr_bd", "z")])

    kmrflat = t("kmr_bd").rearrange("p cc m -> p (cc m)")
    p.emit("gpsimd", lambda e: e.dma_start(
        out=kmrflat[0:64, 0:127:18], in_=t("km_bf")[0:64, :],
        allow_slow_non_contiguous=True),
        reads=[("km_bf", "r"), ("kmr_bd", "z")], writes=[("kmr_bd", "a")], dma="c_kbd")
    p.emit("gpsimd", lambda e: e.dma_start(
        out=kmrflat[64:128, 1:128:18], in_=t("km_bf")[64:128, :],
        allow_slow_non_contiguous=True),
        reads=[("km_bf", "r"), ("kmr_bd", "z"), ("kmr_bd", "a")],
        writes=[("kmr_bd", "b")], dma="c_kbd2")
    p.emit("vector", lambda e: e.tensor_scalar(
        out=t("kv_raw")[0:64, :], in0=t("kv_raw")[0:64, :], scalar1=1.0 / SEQ,
        scalar2=None, op0=ALU.mult),
        reads=[("kv_raw", "tot")], writes=[("kv_raw", "sc")])
    p.emit("vector", lambda e: e.tensor_copy(
        out=t("qr2").rearrange("p cc n -> p (cc n)")[0:64, 0:C], in_=t("kv_raw")[0:64, :]),
        reads=[("kv_raw", "sc")], writes=[("qr2", "kvbf")])
    p.emit("gpsimd", lambda e: e.memset(t("kv_bd")[:, :, :], 0.0), writes=[("kv_bd", "z")])
    qr2flat = t("qr2").rearrange("p cc n -> p (cc n)")
    p.emit("gpsimd", lambda e: e.dma_start(
        out=t("kv_bd")[0:64, :, 0:64],
        in_=qr2flat[0:64, 0:C].rearrange("p (cc hf ee) -> p cc hf ee", cc=NCH, hf=2)[:, :, 0, :]),
        reads=[("qr2", "kvbf"), ("kv_bd", "z")], writes=[("kv_bd", "a")], dma="c_kvb")
    p.emit("gpsimd", lambda e: e.dma_start(
        out=t("kv_bd")[64:128, :, 64:128],
        in_=qr2flat[0:64, 0:C].rearrange("p (cc hf ee) -> p cc hf ee", cc=NCH, hf=2)[:, :, 1, :]),
        reads=[("qr2", "kvbf"), ("kv_bd", "z"), ("kv_bd", "a")], writes=[("kv_bd", "b")],
        dma="c_kvb2")
    selz = t("selz")
    p.emit("gpsimd", lambda e: e.memset(selz[:, :], 0.0), writes=["selz"])
    p.emit("gpsimd", lambda e: e.affine_select(
        out=selz.rearrange("h (cc hf m) -> h cc hf m", cc=NCH, hf=2),
        in_=selz.rearrange("h (cc hf m) -> h cc hf m", cc=NCH, hf=2),
        compare_op=ALU.not_equal, fill=1.0, base=0,
        pattern=[[-2, NCH], [-1, 2], [0, 64]], channel_multiplier=1),
        reads=["selz"], writes=["selz"])

    kvbdk = [("kv_bd", "a"), ("kv_bd", "b")]
    kmrk = [("kmr_bd", "a"), ("kmr_bd", "b")]

    def load_w2(wname, dst, nsrc, ncols, wkey, gfold=None):
        for cc in range(nsrc):
            p.emit("sync", lambda e, cc=cc: e.dma_start(
                out=stg[:, 0:ncols], in_=wn[wname][cc * P:(cc + 1) * P, :]),
                writes=[("w_stage", wname, cc)], dma="c_w2")
            if gfold is None:
                p.emit("vector", lambda e, cc=cc: e.tensor_copy(
                    out=dst[:, cc, :], in_=stg[:, 0:ncols]),
                    reads=[("w_stage", wname, cc)], writes=[(wkey, cc)])
            else:
                p.emit("vector", lambda e, cc=cc: e.tensor_scalar(
                    out=dst[:, cc, :], in0=stg[:, 0:ncols],
                    scalar1=t("g_cols")[:, gfold, cc:cc + 1], scalar2=None, op0=ALU.mult),
                    reads=[("w_stage", wname, cc), ("g_cols", gfold)], writes=[(wkey, cc)])

    load_w2("out_w", t("w_out"), NCH, C, "w_out")
    woutk = [("w_out", cc) for cc in range(NCH)]
    p.emit("sync", lambda e: e.dma_start(out=stg[0:1, 3 * C:4 * C], in_=wn["out_b"][None, :]),
           writes=[("w_stage", "r_out_b")], dma="c_pr")
    p.emit("vector", lambda e: e.tensor_copy(out=t("rows2")[0:1, 0:C], in_=stg[0:1, 3 * C:4 * C]),
           reads=[("w_stage", "r_out_b")], writes=[("rows2", "out_b")])

    def pass2a_tile(ti):
        p.curpar[0] = ti % 2
        par = ti % 2
        PB0, PB1 = 2 * par, 2 * par + 1
        t0 = ti * TT
        p.emit("sync", lambda e: e.dma_start(
            out=t("qr2")[:, :, :], in_=qrsp[ti]),
            writes=["qr2"], dma=f"c_qr{ti % 2}")
        p.emit("sync", lambda e: e.dma_start(
            out=t("ar2")[:, :, :], in_=arsp[ti]),
            writes=["ar2"], dma=f"c_ar{ti % 2}")
        p.emit("sync", lambda e: e.dma_start(
            out=t("lp2")[:, :, :], in_=lpsp[ti]),
            writes=["lp2"], dma=f"c_lp{ti % 2}")
        p.emit("sync", lambda e: e.dma_start(
            out=t("a1l")[:, :, :], in_=a1sp[ti]),
            writes=["a1l"], dma=f"c_a1{ti % 2}")
        def mms(e):
            last = None
            for cc in range(NCH):
                last = e.matmul(pw[PB0][0:16, 0:TT], t("kmr_bd")[:, cc, :], t("qr2")[:, cc, :],
                                start=(cc == 0), stop=(cc == NCH - 1))
            return last
        p.emit("tensor", mms, reads=["qr2"] + kmrk, writes=[(PB(PB0),)])
        p.emit("vector", lambda e: e.tensor_scalar(out=t("zsh")[:, :], in0=pw[PB0][0:16, 0:TT],
                                                   scalar1=1.0, scalar2=1e-6, op0=ALU.mult, op1=ALU.add),
               reads=[(PB(PB0),)], writes=["zsh"])
        p.emit("vector", lambda e: e.reciprocal(out=t("zsh")[:, :], in_=t("zsh")[:, :]),
               reads=["zsh"], writes=["zsh"])
        p.emit("vector", lambda e: e.tensor_copy(out=t("zbf")[:, :], in_=t("zsh")[:, :]),
               reads=["zsh"], writes=["zbf"])
        for cc in range(NCH):
            p.emit("tensor", lambda e, cc=cc: e.matmul(
                pw[PB0][:, 0:TT], t("selz")[:, cc * P:(cc + 1) * P], t("zbf")[:, :],
                start=True, stop=True),
                reads=["zbf", "selz"], writes=[(PB(PB0),)])
            p.emit("vector", lambda e, cc=cc: e.tensor_tensor(
                out=t("zae")[:, :], in0=pw[PB0][:, 0:TT], in1=t("ar2")[:, cc, :], op=ALU.mult),
                reads=[(PB(PB0),), "ar2"], writes=["zae"])
            p.emit("vector", lambda e, cc=cc: e.tensor_tensor(
                out=t("lab")[:, :], in0=t("lp2")[:, cc, :], in1=t("ar2")[:, cc, :], op=ALU.mult),
                reads=["lp2", "ar2"], writes=[("lab",)])
            bk = PB1
            p.emit("tensor", lambda e, cc=cc, bk=bk: e.matmul(
                pw[bk][:, 0:TT], t("kv_bd")[:, cc, :], t("qr2")[:, cc, :], start=True, stop=True),
                reads=["qr2"] + kvbdk, writes=[(PB(bk),)])
            p.emit("vector", lambda e, bk=bk: e.tensor_tensor(
                out=t("zscr")[:, :], in0=pw[bk][:, 0:TT], in1=t("zae")[:, :], op=ALU.mult),
                reads=[(PB(bk),), "zae"], writes=["zscr"])
            p.emit("vector", lambda e, cc=cc: e.tensor_tensor(
                out=t("aab")[:, cc, :], in0=t("zscr")[:, :], in1=t("lab")[:, :], op=ALU.add),
                reads=["zscr", ("lab",)], writes=[("aab", cc)])
        aak = [("aab", cc) for cc in range(NCH)]
        for m in range(NCH):
            bk = nbank(par)
            def mmo(e, m=m, bk=bk):
                last = None
                for cc in range(NCH):
                    last = e.matmul(pw[bk][:, 0:TT], t("w_out")[:, cc, m * P:(m + 1) * P],
                                    t("aab")[:, cc, :], start=(cc == 0), stop=False)
                return e.matmul(pw[bk][:, 0:TT], t("rows2")[0:1, m * P:(m + 1) * P],
                                t("ones_row")[0:1, 0:TT], start=False, stop=True)
            p.emit("tensor", mmo, reads=aak + woutk + [("rows2", "out_b"), "ones_row"],
                   writes=[(PB(bk),)])
            p.emit("scalar", lambda e, m=m, bk=bk: e.activation(
                out=t("evs")[:, m % 2, 0:TT], in_=pw[bk][:, 0:TT], func=AF.Identity),
                reads=[(PB(bk),)], writes=[("evs", m % 2)])
            p.emit("gpsimd", lambda e, m=m: e.tensor_tensor(
                out=t("a2w")[:, m, :], in0=t("a1l")[:, m, :], in1=t("evs")[:, m % 2, 0:TT],
                op=ALU.add),
                reads=["a1l", ("evs", m % 2)], writes=[("a2w", m)])
        a2k = [("a2w", m) for m in range(NCH)]
        p.emit("sync", lambda e: e.dma_start(
            out=a2sp[ti], in_=t("a2w")[:, :, :]),
            reads=a2k, dma=f"c_s2{ti % 2}")
        p.emit("vector", lambda e: e.tensor_copy(
            out=t("a2eg")[:, ti, :, 0:1], in_=t("a2w")[:, :, 0:1]),
            reads=a2k, writes=[("a2eg", ti, 0)])
        p.emit("vector", lambda e: e.tensor_copy(
            out=t("a2eg")[:, ti, :, 1:2], in_=t("a2w")[:, :, TT - 1:TT]),
            reads=a2k, writes=[("a2eg", ti, 1)])
        if ti == 0:
            p.emit("vector", lambda e: e.tensor_copy(
                out=t("a2keep")[:, 0, :], in_=t("a2w")[:, :, 0:1].rearrange("p cc o -> p (cc o)")),
                reads=a2k, writes=[("a2keep", 0)])
        if ti == NTILE - 1:
            p.emit("vector", lambda e: e.tensor_copy(
                out=t("a2keep")[:, 1, :], in_=t("a2w")[:, :, TT - 1:TT].rearrange("p cc o -> p (cc o)")),
                reads=a2k, writes=[("a2keep", 1)])

    for tj in range(NTILE // 2):
        pair_run(pass2a_tile, 2 * tj, 2 * tj + 1)
    p.curpar[0] = 0

    p.emit("sync", lambda e: e.dma_start(
        out=hrow.rearrange("(a p) b -> p a b", p=P), in_=t("zero8")[:, :, :]),
        reads=["zero8"], writes=[("hrow", "z")], dma="c_hz")
    p.emit("gpsimd", lambda e: e.indirect_dma_start(
        out=S_a2[:, :], out_offset=IndirectOffsetOnAxis(ap=t("idx")[:, 1:2], axis=0),
        in_=t("a2keep")[:, 0, :], in_offset=None),
        reads=[("a2keep", 0), ("idx", 1)], writes=[("S_a2", "a")], dma="c_h2a")
    p.emit("gpsimd", lambda e: e.indirect_dma_start(
        out=S_a2[:, :], out_offset=IndirectOffsetOnAxis(ap=t("idx")[:, 6:7], axis=0),
        in_=t("a2keep")[:, 1, :], in_offset=None),
        reads=[("a2keep", 1), ("idx", 6), ("S_a2", "a")], writes=[("S_a2", "b")], dma="c_h2b")
    p.emit("gpsimd", lambda e: e.indirect_dma_start(
        out=S_fl[:, :], out_offset=IndirectOffsetOnAxis(ap=t("idx")[0:2, 3:4], axis=0),
        in_=t("magic")[:, :], in_offset=None),
        reads=["magic", ("idx", 3), ("S_a2", "a"), ("S_a2", "b")], writes=[("S_fl", "a2")],
        dma="c_h2f")
    make_poll(poll2, (2, 3), "a2", [("S_fl", "a2"), "fbuf"])
    p.emit("sync", lambda e: e.dma_start(
        out=t("a2rd")[:, :, :], in_=S_a2[0:2 * P, :].rearrange("(s p) n -> p s n", s=2)),
        reads=[("fbuf", "a2"), ("S_a2", "a"), ("S_a2", "b")], writes=["a2rd"], dma="c_h2r")
    p.emit("gpsimd", lambda e: e.indirect_dma_start(
        out=hrow[:, :], out_offset=IndirectOffsetOnAxis(ap=t("idx")[:, 4:5], axis=0),
        in_=t("a2rd")[:, 0, :], in_offset=None),
        reads=["a2rd", ("idx", 4), ("hrow", "z")], writes=[("hrow", "a")], dma="c_h3a")
    p.emit("gpsimd", lambda e: e.indirect_dma_start(
        out=hrow[:, :], out_offset=IndirectOffsetOnAxis(ap=t("idx")[:, 5:6], axis=0),
        in_=t("a2rd")[:, 1, :], in_offset=None),
        reads=["a2rd", ("idx", 5), ("hrow", "z"), ("hrow", "a")], writes=[("hrow", "b")],
        dma="c_h3b")
    hrk = [("hrow", "a"), ("hrow", "b")]
    p.emit("sync", lambda e: e.dma_start(
        out=t("hpar")[:, :, :], in_=hrow[0:2 * P, :].rearrange("(s p) b -> p s b", s=2)),
        reads=hrk, writes=["hpar"], dma="c_hz")

    cvf = tens["qke~1"].rearrange("p a b -> p (a b)")
    qkf = tens["qke~0"].rearrange("p a b -> p (a b)")
    for cc in range(NCH):
        p.emit("sync", lambda e, cc=cc: e.dma_start(
            out=stg[:, 0:HID], in_=wn["fc1_w"][cc * P:(cc + 1) * P, :]),
            writes=[("w_stage", "f1", cc)], dma="c_w2")
        p.emit("vector", lambda e, cc=cc: e.tensor_scalar(
            out=cvf[:, 0:HID], in0=stg[:, 0:HID],
            scalar1=t("g_cols")[:, 1, cc:cc + 1], scalar2=None, op0=ALU.mult),
            reads=[("w_stage", "f1", cc), ("g_cols", 1)], writes=[("qke~1", "fold")])
        p.emit("sync", lambda e, cc=cc: e.dma_start(
            out=fc1bf[:, :, cc, :].rearrange("hcc p m -> p hcc m"),
            in_=cvf[:, 0:HID].rearrange("p (hcc m) -> p hcc m", m=P)),
            reads=[("qke~1", "fold")], writes=[("fc1bf", cc)], dma="c_w2o")
        p.emit("vector", lambda e, cc=cc: e.tensor_copy(out=qkf[:, 0:HID], in_=stg[:, 0:HID]),
               reads=[("w_stage", "f1", cc)], writes=[("qke~0", "raw")])
        if cc == 0:
            def mmb2open(e):
                last = None
                for i in range(4):
                    last = e.matmul(pw[i][:, 0:512], t("zrow")[0:1, :],
                                    t("ones_row")[0:1, 0:512], start=True, stop=False)
                return last
            p.emit("tensor", mmb2open, reads=["zrow", "ones_row"],
                   writes=[(PB(i),) for i in range(4)])
        def mmb2(e, cc=cc):
            last = None
            for oi, (o, nn) in enumerate(splits(0, HID)):
                bank = pw[oi % 4]
                row = 0 if oi < 4 else 32
                last = e.matmul(bank[row:row + 1, 0:nn], t("b_colsb")[:, 1, cc:cc + 1],
                                qkf[:, o:o + nn], start=False, stop=False)
            return last
        p.emit("tensor", mmb2, reads=[("qke~0", "raw"), "b_colsb"],
               writes=[(PB(i),) for i in range(4)])
    def mmb2close(e):
        last = None
        for i in range(4):
            last = e.matmul(pw[i][:, 0:512], t("zrow")[0:1, :],
                            t("ones_row")[0:1, 0:512], start=False, stop=True)
        return last
    p.emit("tensor", mmb2close, reads=["zrow", "ones_row"] + [(PB(i),) for i in range(4)],
           writes=[(PB(i),) for i in range(4)])
    p.emit("sync", lambda e: e.dma_start(
        out=stg[0:1, 0:HID], in_=wn["fc1_b"][None, :]),
        writes=[("w_stage", "f1b0")], dma="c_pr")
    for oi, (o, nn) in enumerate(splits(0, HID)):
        bank = pw[oi % 4]
        row = 0 if oi < 4 else 32
        p.emit("vector", lambda e, o=o, nn=nn, bank=bank, row=row: e.tensor_tensor(
            out=t("lscr")[0:1, 0:nn], in0=bank[row:row + 1, 0:nn],
            in1=stg[0:1, o:o + nn], op=ALU.add),
            reads=[(PB(oi % 4),), ("w_stage", "f1b0")], writes=[("lscr", "f")])
        p.emit("vector", lambda e, o=o, nn=nn: e.tensor_copy(
            out=t("r_fc1b")[0:1, o:o + nn], in_=t("lscr")[0:1, 0:nn]),
            reads=[("lscr", "f")], writes=[("r_fc1b",)])
    for hcc in range(NHID):
        p.emit("sync", lambda e, hcc=hcc: e.dma_start(
            out=stg[:, 0:C], in_=wn["fc2_w"][hcc * P:(hcc + 1) * P, :]),
            writes=[("w_stage", "f2", hcc)], dma="c_w2")
        p.emit("vector", lambda e, hcc=hcc: e.tensor_copy(out=cvf[:, 0:C], in_=stg[:, 0:C]),
               reads=[("w_stage", "f2", hcc)], writes=[("qke~1", "f2")])
        p.emit("sync", lambda e, hcc=hcc: e.dma_start(
            out=fc2bf[hcc, :, :], in_=cvf[:, 0:C]),
            reads=[("qke~1", "f2")], writes=[("fc2bf", hcc)], dma="c_w2o")
    p.emit("sync", lambda e: e.dma_start(out=stg[0:1, 3 * C:4 * C], in_=wn["fc2_b"][None, :]),
           writes=[("w_stage", "r_fc2_b")], dma="c_pr")
    p.emit("vector", lambda e: e.tensor_copy(out=t("rows2")[0:1, C:2 * C], in_=stg[0:1, 3 * C:4 * C]),
           reads=[("w_stage", "r_fc2_b")], writes=[("rows2", "fc2_b")])

    def fc2bank(m):
        return pw[m % 4][:, 0:TT], (PB(m % 4),)

    def pass2b_head(ti):
        p.curpar[0] = ti % 2
        par = ti % 2
        PB0, PB1 = 2 * par, 2 * par + 1
        t0 = ti * TT
        a2m = t("a2m")
        p.emit("sync", lambda e: e.dma_start(
            out=a2m[:, :, :], in_=a2sp[ti]),
            writes=["a2m"], dma=f"c_s2{ti % 2}")
        a2tk = ["a2m"]
        if ti == 0:
            Lk, Lap = ["hpar"], lambda cc: t("hpar")[:, 0, cc:cc + 1]
        else:
            Lk = [("a2eg", ti - 1, 1)]
            Lap = lambda cc: t("a2eg")[:, ti - 1, cc, 1:2]
        if ti == NTILE - 1:
            Rk, Rap = ["hpar"], lambda cc: t("hpar")[:, 1, cc:cc + 1]
        else:
            Rk = [("a2eg", ti + 1, 0)]
            Rap = lambda cc: t("a2eg")[:, ti + 1, cc, 0:1]
        p.emit("sync", lambda e: e.dma_start(
            out=t("y0l")[:, :, :], in_=y0sp[t0:t0 + TT, :].rearrange("(g p) c -> p g c", p=P)),
            writes=["y0l"], dma=f"c_y0{ti % 2}")
        cv = t("cvs")
        tp = taps
        for cc in range(NCH):
            p.emit("vector", lambda e, cc=cc: e.tensor_scalar(
                out=cv[:, cc, 1:TT], in0=a2m[:, cc, 0:TT - 1], scalar1=tp[:, 3, cc, 0:1],
                scalar2=tp[:, 3, cc, 3:4], op0=ALU.mult, op1=ALU.add),
                reads=a2tk + [("taps", 3, "w"), ("taps", 3, "b")], writes=[("cvs", cc)])
            p.emit("vector", lambda e, cc=cc: e.tensor_scalar(
                out=cv[:, cc, 0:1], in0=Lap(cc), scalar1=tp[:, 3, cc, 0:1],
                scalar2=tp[:, 3, cc, 3:4], op0=ALU.mult, op1=ALU.add),
                reads=Lk + [("taps", 3, "w"), ("taps", 3, "b")], writes=[("cvs", cc)])
            p.emit("vector", lambda e, cc=cc: e.scalar_tensor_tensor(
                out=cv[:, cc, 0:TT - 1], in0=a2m[:, cc, 1:TT], scalar=tp[:, 3, cc, 2:3],
                in1=cv[:, cc, 0:TT - 1], op0=ALU.mult, op1=ALU.add),
                reads=a2tk + [("taps", 3, "w"), ("cvs", cc)], writes=[("cvs", cc)])
            p.emit("vector", lambda e, cc=cc: e.scalar_tensor_tensor(
                out=cv[:, cc, TT - 1:TT], in0=Rap(cc), scalar=tp[:, 3, cc, 2:3],
                in1=cv[:, cc, TT - 1:TT], op0=ALU.mult, op1=ALU.add),
                reads=Rk + [("taps", 3, "w"), ("cvs", cc)], writes=[("cvs", cc)])
            p.emit("vector", lambda e, cc=cc: e.scalar_tensor_tensor(
                out=t("a3")[:, cc, :], in0=a2m[:, cc, :], scalar=tp[:, 3, cc, 1:2],
                in1=cv[:, cc, :], op0=ALU.mult, op1=ALU.add),
                reads=a2tk + [("taps", 3, "w"), ("cvs", cc)], writes=[("a3", cc)])
        a3k = [("a3", cc) for cc in range(NCH)]
        p.emit("gpsimd", lambda e: e.tensor_copy(
            out=t("a3bf")[:, :, :],
            in_=t("a3")[:, :, :]),
            reads=a3k, writes=["a3bf"])
        p.emit("scalar", lambda e: e.activation(
            out=t("sqbf")[:, :, 0:TT],
            in_=t("a3")[:, :, :], func=AF.Square),
            reads=a3k, writes=["sqbf"])
        ln_fm(t("a3bf"), ["a3bf"], "sqbf", TT, t("a3"), 0, a3k, t("xn2"), "xn2", PB0)
        xn2k = [("xn2", cc) for cc in range(NCH)]
        return

    def mlp_joint(ta, tb):
        assert 2 * TT <= 512
        for r in range(2):
            for hh in range(16):
                hcc = 16 * r + hh
                sl = hh % 4
                p.curpar[0] = 0
                if hh % 2 == 0:
                    p.emit("sync", lambda e, hcc=hcc, sl=sl: e.dma_start(
                        out=t("w1s").rearrange("p s c m -> p s (c m)")[:, sl:sl + 2, :],
                        in_=fc1bf[hcc:hcc + 2].rearrange("h p c m -> p h (c m)")),
                        reads=[("fc1bf", cc2) for cc2 in range(NCH)],
                        writes=[("w1s", sl), ("w1s", sl + 1)], dma=f"c_f1{sl}")
                for par in (0, 1):
                    p.curpar[0] = par
                    PB0 = 2 * par
                    slp = PB0 + (hh % 2)
                    def mmf1(e, hcc=hcc, sl=sl, slp=slp):
                        last = None
                        for cc in range(NCH):
                            last = e.matmul(pw[slp][:, 0:TT], t("w1s")[:, sl, cc, :],
                                            t("xn2")[:, cc, :], start=(cc == 0), stop=False)
                        return e.matmul(pw[slp][:, 0:TT],
                                        t("r_fc1b")[0:1, hcc * P:(hcc + 1) * P],
                                        t("ones_row")[0:1, 0:TT], start=False, stop=True)
                    p.emit("tensor", mmf1,
                           reads=[("xn2", cc) for cc in range(NCH)] +
                                 [("w1s", sl), ("r_fc1b",), "ones_row"],
                           writes=[(PB(slp),)])
                    p.emit("scalar", lambda e, hh=hh, slp=slp: e.activation(
                        out=t("g1")[:, hh, :], in_=pw[slp][:, 0:TT], func=AF.Gelu),
                        reads=[(PB(slp),)], writes=[("g1", hh)])
            for mh in range(2):
                for par in (0, 1):
                    p.curpar[0] = par
                    PB0, PB1 = 2 * par, 2 * par + 1
                    def f2open(e, PB0=PB0, PB1=PB1):
                        e.matmul(pw[PB0][:, 0:512], t("zrow")[0:1, :],
                                 t("ones_row")[0:1, 0:512], start=True, stop=False)
                        return e.matmul(pw[PB1][:, 0:512], t("zrow")[0:1, :],
                                        t("ones_row")[0:1, 0:512], start=True, stop=False)
                    p.emit("tensor", f2open, reads=["zrow", "ones_row"],
                           writes=[(PB(PB0),), (PB(PB1),)])
                for hh in range(16):
                    hcc = 16 * r + hh
                    sl = hh % 4
                    p.curpar[0] = 0
                    if hh % 2 == 0:
                        p.emit("sync", lambda e, hcc=hcc, sl=sl, mh=mh: e.dma_start(
                            out=t("w2s")[:, sl:sl + 2, 0:512],
                            in_=fc2bf[hcc:hcc + 2, :, mh * 512:(mh + 1) * 512]
                            .rearrange("h p m -> p h m")),
                            reads=[("fc2bf", hcc), ("fc2bf", hcc + 1)],
                            writes=[("w2s", sl), ("w2s", sl + 1)], dma=f"c_f2{sl}")
                    for par in (0, 1):
                        p.curpar[0] = par
                        PB0, PB1 = 2 * par, 2 * par + 1
                        def mmf2(e, hh=hh, sl=sl, PB0=PB0):
                            last = None
                            for mm in range(4):
                                bnk = pw[PB0 + mm // 2]
                                last = e.matmul(bnk[:, (mm % 2) * TT:(mm % 2 + 1) * TT],
                                                t("w2s")[:, sl, mm * P:(mm + 1) * P],
                                                t("g1")[:, hh, :], start=False, stop=False)
                            return last
                        p.emit("tensor", mmf2, reads=[("g1", hh), ("w2s", sl)],
                               writes=[(PB(PB0),), (PB(PB1),)])
                for par in (0, 1):
                    p.curpar[0] = par
                    PB0, PB1 = 2 * par, 2 * par + 1
                    def f2close(e, r=r, mh=mh, PB0=PB0, PB1=PB1):
                        if r == 0:
                            for mm in range(4):
                                m = mh * 4 + mm
                                bnk = pw[PB0 + mm // 2]
                                e.matmul(bnk[:, (mm % 2) * TT:(mm % 2 + 1) * TT],
                                         t("rows2")[0:1, C + m * P:C + (m + 1) * P],
                                         t("ones_row")[0:1, 0:TT], start=False, stop=False)
                        e.matmul(pw[PB0][:, 0:512], t("zrow")[0:1, :],
                                 t("ones_row")[0:1, 0:512], start=False, stop=True)
                        return e.matmul(pw[PB1][:, 0:512], t("zrow")[0:1, :],
                                        t("ones_row")[0:1, 0:512], start=False, stop=True)
                    p.emit("tensor", f2close,
                           reads=[("rows2", "fc2_b"), "ones_row", "zrow",
                                  (PB(PB0),), (PB(PB1),)],
                           writes=[(PB(PB0),), (PB(PB1),)])
                    for mm in range(4):
                        m = mh * 4 + mm
                        p.emit("scalar", lambda e, m=m, mm=mm, PB0=PB0: e.activation(
                            out=t("evs")[:, mm % 2, 0:TT],
                            in_=pw[PB0 + mm // 2][:, (mm % 2) * TT:(mm % 2 + 1) * TT],
                            func=AF.Identity),
                            reads=[(PB(PB0 + mm // 2),)], writes=[("evs", mm % 2)])
                        p.emit("gpsimd", lambda e, m=m, mm=mm: e.tensor_tensor(
                            out=t("a3")[:, m, :], in0=t("a3")[:, m, :],
                            in1=t("evs")[:, mm % 2, 0:TT], op=ALU.add),
                            reads=[("evs", mm % 2), ("a3", m)], writes=[("a3", m)])
        p.curpar[0] = 0

    def pass2b_exit(ti):
        p.curpar[0] = ti % 2
        par = ti % 2
        PB0, PB1 = 2 * par, 2 * par + 1
        t0 = ti * TT
        a3k = [("a3", cc) for cc in range(NCH)]
        for g in range(NG):
            def trx(e, g=g):
                last = None
                for cc in range(NCH):
                    bank = pw[PB0 + cc // 4]
                    last = e.transpose(bank[:, (cc % 4) * P:(cc % 4 + 1) * P],
                                       t("a3")[:, cc, g * P:(g + 1) * P], t("ident_f")[:, :])
                return last
            p.emit("tensor", trx, reads=a3k + ["ident_f"],
                   writes=[(PB(PB0),), (PB(PB1),)])
            for half in range(2):
                p.emit("scalar", lambda e, half=half: e.activation(
                    out=t("evs")[:, half, :], in_=pw[PB0 + half][:, 0:512],
                    func=AF.Identity),
                    reads=[(PB(PB0 + half),)], writes=[("evs", half)])
                p.emit("gpsimd", lambda e, g=g, half=half: e.tensor_tensor(
                    out=t("y0l")[:, g, half * 512:(half + 1) * 512],
                    in0=t("y0l")[:, g, half * 512:(half + 1) * 512],
                    in1=t("evs")[:, half, :], op=ALU.add),
                    reads=["y0l", ("evs", half)], writes=[("y0l", g, half)])
        p.emit("sync", lambda e: e.dma_start(
            out=out_my[t0:t0 + TT, :].rearrange("(g p) c -> p g c", p=P), in_=t("y0l")[:, :, :]),
            reads=[("y0l", g, h) for g in range(NG) for h in range(2)], dma=f"c_o{ti % 2}")

    for tj in range(NTILE // 2):
        pair_run(pass2b_head, 2 * tj, 2 * tj + 1)
        mlp_joint(2 * tj, 2 * tj + 1)
        pair_run(pass2b_exit, 2 * tj, 2 * tj + 1)

    p.barrier_on("gpsimd", [(ch, p.count[ch]) for ch in ("c_o0", "c_o1") if ch in p.count])

    with nc.Block() as block:
        p.seal(block)
    es.close()
    return nc


def kernel(**inputs):
    B, L, Cc = 4, 8192, 1024
    NT = L // 2
    HL = 4
    x = np.asarray(inputs["L0_attn_x"], dtype=np.float32)
    nc = build_program(NT=NT, TT=256)

    wmap = {
        "cpe1_w": inputs["L0_attn_cpe1_w"].reshape(Cc, 3), "cpe1_b": inputs["L0_attn_cpe1_b"],
        "norm1_g": inputs["L0_attn_norm1_g"], "norm1_b": inputs["L0_attn_norm1_b"],
        "in_w": inputs["L0_attn_in_w"], "in_b": inputs["L0_attn_in_b"],
        "actp_w": inputs["L0_attn_actp_w"], "actp_b": inputs["L0_attn_actp_b"],
        "dwc_w": inputs["L0_attn_dwc_w"].reshape(Cc, 3), "dwc_b": inputs["L0_attn_dwc_b"],
        "qk_w": inputs["L0_attn_qk_w"], "qk_b": inputs["L0_attn_qk_b"],
        "lepe_w": inputs["L0_attn_lepe_w"].reshape(Cc, 3), "lepe_b": inputs["L0_attn_lepe_b"],
        "out_w": inputs["L0_attn_out_w"], "out_b": inputs["L0_attn_out_b"],
        "cpe2_w": inputs["L0_attn_cpe2_w"].reshape(Cc, 3), "cpe2_b": inputs["L0_attn_cpe2_b"],
        "norm2_g": inputs["L0_attn_norm2_g"], "norm2_b": inputs["L0_attn_norm2_b"],
        "fc1_w": inputs["L0_attn_fc1_w"], "fc1_b": inputs["L0_attn_fc1_b"],
        "fc2_w": inputs["L0_attn_fc2_w"], "fc2_b": inputs["L0_attn_fc2_b"],
        "fWin": inputs["L0_ffn_Win"], "fbin": inputs["L0_ffn_bin_"],
        "fW1": inputs["L0_ffn_W1"], "fb1": inputs["L0_ffn_b1"],
        "fW2": inputs["L0_ffn_W2"], "fb2": inputs["L0_ffn_b2"],
        "fWout": inputs["L0_ffn_Wout"], "fbout": inputs["L0_ffn_bout"],
    }
    wmap = {k: np.ascontiguousarray(np.asarray(v, dtype=np.float32)) for k, v in wmap.items()}

    ar = np.arange(128, dtype=np.int32)
    in_maps = []
    for core in range(8):
        b, half = core // 2, core % 2
        xm = np.zeros((NT + 2 * HL, Cc), dtype=np.float32)
        s = half * NT
        lo, hi = s - HL, s + NT + HL
        clo, chi = max(lo, 0), min(hi, L)
        xm[clo - lo:clo - lo + (chi - clo)] = x[b, clo:chi]
        lr = np.zeros((128, 2), dtype=np.float32)
        lr[:, 0] = 1.0 if half == 1 else 0.0
        lr[:, 1] = 1.0 if half == 0 else 0.0
        m = {"x_my": xm, "lrmask": lr,
             "kv_idx": (np.arange(65, dtype=np.int32) + 65 * half).reshape(65, 1),
             "fkv_idx": np.array([[half], [half]], dtype=np.int32),
             "fa2_idx": np.array([[2 + half], [2 + half]], dtype=np.int32)}
        if half == 0:
            m["a2a_idx"] = (ar + 2 * 128).reshape(128, 1)
            m["a2b_idx"] = (ar + 0).reshape(128, 1)
            m["haloA_idx"] = (ar + 2 * 128).reshape(128, 1)
            m["haloB_idx"] = (ar + 128).reshape(128, 1)
        else:
            m["a2a_idx"] = (ar + 128).reshape(128, 1)
            m["a2b_idx"] = (ar + 2 * 128).reshape(128, 1)
            m["haloA_idx"] = (ar + 0).reshape(128, 1)
            m["haloB_idx"] = (ar + 2 * 128).reshape(128, 1)
        m.update(wmap)
        in_maps.append(m)

    res = run_bass_kernel_spmd(nc, in_maps, core_ids=list(range(8)))
    out = np.empty((B, L, Cc), dtype=np.float32)
    for core in range(8):
        b, half = core // 2, core % 2
        out[b, half * NT:(half + 1) * NT] = np.asarray(res.results[core]["out_my"])
    return out
